# version 07
# speedup vs baseline: 1.3143x; 1.0233x over previous
import numpy as np
from contextlib import ExitStack

import concourse.bass as bass
import concourse.mybir as mybir
from concourse.bass_utils import run_bass_kernel_spmd

F32 = mybir.dt.float32
BF16 = mybir.dt.bfloat16
I32 = mybir.dt.int32
AF = mybir.ActivationFunctionType
ALU = mybir.AluOpType

T, D, HN, KH = 4096, 1024, 16, 64
KV = KH + 1
NCH = 32
INNER, STATE = 2048, 16
FFH = 4096
FFI = 3072
NEG_LOG_LAM = float(np.log(np.float64(np.float32(1e-8))))
QS = float(np.float32(1.0 / 127.0))
LN_EPS = 1e-5

ENGINES = ("sync", "scalar", "vector", "tensor", "gpsimd")


def host_constants():
    return {
        "c_triu": np.triu(np.ones((128, 128), np.float32)),
        "c_trilm1": np.tril(np.ones((128, 128), np.float32), -1),
        "c_ones_row": np.ones((1, 128), np.float32),
        "c_ident": np.eye(128, dtype=np.float32),
    }


class SemC:
    __slots__ = ("h", "n")

    def __init__(self, h):
        self.h = h
        self.n = 0


class Prog:
    def __init__(self, nc, es):
        self.nc = nc
        self.es = es
        self.ops = {e: [] for e in ENGINES}
        self.sems = {}

    def sem(self, name):
        s = SemC(self.es.enter_context(self.nc.semaphore(name)))
        self.sems[name] = s
        return s

    def sb(self, name, shape, dt=F32):
        return self.es.enter_context(self.nc.sbuf_tensor(name, list(shape), dt))

    def emit(self, eng, fn, waits=(), inc="auto", inc_by=1):
        if inc == "auto":
            inc = self.sems.get("sm_" + eng) if eng != "sync" else None
        if inc is not None:
            inc.n += inc_by
        best = {}
        for s, v in waits:
            if v > 0 and (s.h not in best or best[s.h][1] < v):
                best[s.h] = (s, v)
        wl = tuple((s.h, v) for s, v in best.values())

        def g(e, fn=fn, wl=wl, inc=inc, inc_by=inc_by):
            for h, v in wl:
                e.wait_ge(h, v)
            r = fn(e)
            if inc is not None:
                if r is None:
                    r = e.nop()
                r.then_inc(inc.h, inc_by)

        self.ops[eng].append(g)

    def wait(self, eng, *waits):
        wl = tuple((s.h, v) for s, v in waits if v > 0)
        if not wl:
            return

        def g(e, wl=wl):
            for h, v in wl:
                e.wait_ge(h, v)

        self.ops[eng].append(g)

    def scope(self, name):
        for e in ENGINES:
            self.ops[e].append(("scope", name))


def build_core_program(upto="all", dbg=()):
    nc = bass.Bass("TRN2", target_bir_lowering=False, detect_race_conditions=True)
    es = ExitStack()
    P = Prog(nc, es)

    din = {}

    def ein(name, shape, dt=F32):
        din[name] = nc.dram_tensor(name, list(shape), dt, kind="ExternalInput").ap()
        return din[name]

    x_in = ein("x", (T, D))
    for nm in ("q_w", "k_w", "v_w", "o_w"):
        ein(nm, (D, D))
    for nm in ("q_b", "k_b", "v_b", "o_b", "ln1_w", "ln1_b", "ln2_w", "ln2_b", "f2_b"):
        ein(nm, (D,))
    ein("dec_w", (D, HN)); ein("dec_b", (HN,))
    ein("f1_w", (D, FFH)); ein("f1_b", (FFH,)); ein("f2_w", (FFH, D))
    ein("W_in", (D, 2 * INNER)); ein("conv_w", (INNER, 3)); ein("conv_b", (INNER,))
    ein("W_state", (INNER, 2 * STATE)); ein("W_time", (INNER, INNER))
    ein("b_time", (INNER,)); ein("A_log", (INNER, STATE)); ein("skip", (INNER,))
    ein("W_out", (INNER, D))
    ein("w_gate", (FFI, D), I32); ein("w_up", (FFI, D), I32); ein("w_down", (D, FFI), I32)
    for nm, arr in host_constants().items():
        ein(nm, arr.shape)

    dbg = set(dbg)
    douts = {}

    def scr(name, shape, dt=F32):
        kind = "ExternalOutput" if name in dbg else "Internal"
        t = nc.dram_tensor(name, list(shape), dt, kind=kind).ap()
        if name in dbg:
            douts[name] = t
        return t

    out_d = nc.dram_tensor("out", [T, D], F32, kind="ExternalOutput").ap()
    douts["out"] = out_d

    s_xn0 = scr("xn0", (T, D))
    s_x1 = scr("x1", (T, D))
    s_x2 = scr("x2", (T, D))
    s_x3 = scr("x3", (T, D))
    s_xmid = scr("x_mid", (T, D))
    s_phiq_fm = scr("phiq_fm", (D, T), BF16)
    s_phik_fm = scr("phik_fm", (D, T), BF16)
    s_phiq_tok = scr("phiq_tok", (T, D), BF16)
    s_phik_tok = scr("phik_tok", (T, D), BF16)
    s_v_tok = scr("v_tok", (T, D), BF16)
    s_attnfm = scr("attn_fm_d", (D, T), BF16)
    s_z3fm = scr("z3fm", (D, T), BF16)
    s_h1fm = scr("h1fm", (FFH, T), BF16)
    s_h2fm = scr("h2fm", (FFI, T), BF16)
    s_hfm = scr("h_fm", (INNER, T))
    s_hfm_bf = scr("h_fm_bf", (INNER, T), BF16)
    s_sg = scr("sg_fm", (INNER, T))
    s_winbf = scr("win_bf", (D, 2 * INNER), BF16)
    s_wtt = scr("wt_tiled", (128, 16 * 16 * 128), BF16)
    s_woutbf = scr("wout_bf", (INNER, D), BF16)
    s_wgT = scr("wgT", (D, FFI), BF16)
    s_wuT = scr("wuT", (D, FFI), BF16)
    s_wdT = scr("wdT", (FFI, D), BF16)
    s_f1bf = scr("f1_bf", (D, FFH), BF16)
    s_dbg_misc = scr("dbg_misc", (128, 2048)) if "dbg_misc" in dbg else None

    def sbR(es_, n, s, d=F32):
        return es_.enter_context(nc.sbuf_tensor(n, list(s), d, side="right"))

    es_lion = ExitStack()
    LI = lambda n, s, d=F32: sbR(es_lion, n, s, d)
    ident_f = P.sb("ident_f", (128, 128))
    ident_b = P.sb("ident_b", (128, 128), BF16)
    triu_f = LI("triu_f", (128, 128))
    trilm1_f = LI("trilm1_f", (128, 128))
    ones_row = P.sb("ones_row", (1, 128))
    g1c = P.sb("g1c", (128, 8)); b1c = P.sb("b1c", (128, 8))
    g2c = P.sb("g2c", (128, 8)); b2c = P.sb("b2c", (128, 8))
    bqc = P.sb("bqc", (128, 8)); nbqc = P.sb("nbqc", (128, 8))
    bkc = P.sb("bkc", (128, 8)); nbkc = P.sb("nbkc", (128, 8))
    bvc = P.sb("bvc", (128, 8))
    bdecc = P.sb("bdecc", (16, 1)); nbdecc = P.sb("nbdecc", (16, 1))
    f1bc = P.sb("f1bc", (128, 32))
    btimec = P.sb("btimec", (128, 16))
    cw0c = P.sb("cw0c", (128, 16)); cw1c = P.sb("cw1c", (128, 16))
    cw2c = P.sb("cw2c", (128, 16)); cbc = P.sb("cbc", (128, 16))
    skipc = P.sb("skipc", (128, 16))
    A_sb = P.sb("A_sb", (128, 256))
    ob_bc = LI("ob_bc", (128, 1024))
    f2b_bc = LI("f2b_bc", (128, 1024))
    lloc_tok = LI("lloc_tok", (128, NCH * HN))
    af_tok = LI("af_tok", (128, NCH * HN))
    ab_tok = LI("ab_tok", (128, NCH * HN))
    gam_bc = LI("gam_bc", (128, NCH * HN))
    gamrow = LI("gamrow", (1, NCH * HN))
    eps_col = P.sb("eps_col", (128, 1))
    ln_m = P.sb("ln_m", (128, 1)); ln_s2 = P.sb("ln_s2", (128, 1))
    ln_v = P.sb("ln_v", (128, 1)); ln_r = P.sb("ln_r", (128, 1))
    ln_scr = P.sb("ln_scr", (128, 1024))
    ln_scr2 = P.sb("ln_scr2", (128, 1024))
    ln_guard = [0]
    lnb_m = P.sb("lnb_m", (128, 16)); lnb_s2 = P.sb("lnb_s2", (128, 16))
    lnb_v = P.sb("lnb_v", (128, 16)); lnb_r = P.sb("lnb_r", (128, 16))
    f32stage = P.sb("f32stage", (128, 4096))
    bfstage = P.sb("bfstage", (128, 4096), BF16)
    phtmp1 = P.sb("phtmp1", (128, 512))
    phtmp2 = P.sb("phtmp2", (128, 512))

    psf = [P.es.enter_context(nc.psum_tensor(f"psf{i}", [128, 512], F32)) for i in range(6)]
    psb = [P.es.enter_context(nc.psum_tensor(f"psb{i}", [128, 1024], BF16)) for i in range(2)]
    bankfree = {}

    def bfree(bid):
        w = bankfree.get(bid)
        return [w] if w else []

    sm = {e: P.sem("sm_" + e) for e in ENGINES}
    ld_pool = {"sync": [P.sem(f"ldS{i}") for i in range(16)],
               "scalar": [P.sem(f"ldA{i}") for i in range(10)]}
    ld_idx = {"sync": 0, "scalar": 0}
    store_sems = {}
    fence_sem = P.sem("fence")

    def all_dma_vals():
        vals = [(s, s.n) for s in ld_pool["sync"] if s.n > 0]
        vals += [(s, s.n) for s in ld_pool["scalar"] if s.n > 0]
        vals += [(s, s.n) for pair in store_sems.values() for s in pair if s.n > 0]
        return vals

    def fence(dma=True):
        vals = []
        for e in ENGINES:
            P.emit(e, lambda en: en.nop(), waits=[(sm[e], sm[e].n)], inc=sm[e])
            vals.append((sm[e], sm[e].n))
        if dma:
            P.emit("sync", lambda en: en.nop(), waits=vals + all_dma_vals(),
                   inc=fence_sem)
            fv = (fence_sem, fence_sem.n)
            for e in ENGINES:
                P.wait(e, fv)
        else:
            for e in ENGINES:
                P.wait(e, *vals)

    def efence(*engs):
        vals = []
        for e in engs:
            P.emit(e, lambda en: en.nop(), waits=[(sm[e], sm[e].n)], inc=sm[e])
            vals.append((sm[e], sm[e].n))
        for e in engs:
            P.wait(e, *vals)

    def load(sbuf_ap, dram_ap, q="sync", waits=(), slow=False):
        s = ld_pool[q][ld_idx[q] % len(ld_pool[q])]
        ld_idx[q] += 1
        w = list(waits)
        if s.n > 0:
            w.append((s, s.n))
        nm = dram_ap.tensor.name
        if nm in store_sems:
            for ss in store_sems[nm]:
                if ss.n > 0:
                    w.append((ss, ss.n))
        P.emit(q, lambda e, o=sbuf_ap, i=dram_ap, sl=slow: e.dma_start(
            out=o, in_=i, allow_slow_non_contiguous=sl),
               waits=w, inc=s, inc_by=16)
        return (s, s.n)

    store_idx = {}

    def store(dram_ap, sbuf_ap, waits=(), q="sync", slow=False):
        nm = dram_ap.tensor.name
        if nm not in store_sems:
            store_sems[nm] = [P.sem("st_" + nm + "a"), P.sem("st_" + nm + "b")]
            store_idx[nm] = 0
        k = store_idx[nm]
        store_idx[nm] = 1 - k
        s = store_sems[nm][k]
        w = list(waits)
        if s.n > 0:
            w.append((s, s.n))
        P.emit(q, lambda e, o=dram_ap, i=sbuf_ap, sl=slow: e.dma_start(
            out=o, in_=i, allow_slow_non_contiguous=sl),
               waits=w, inc=s, inc_by=16)
        return (s, s.n)

    def tval():
        return (sm["tensor"], sm["tensor"].n)

    def vval():
        return (sm["vector"], sm["vector"].n)

    def sval():
        return (sm["scalar"], sm["scalar"].n)

    def gval():
        return (sm["gpsimd"], sm["gpsimd"].n)

    def mm(ps_ap, bid, lhsT, rhs, start, stop, waits=(), inc=True):
        w = list(waits)
        if start:
            w += bfree(bid)
        P.emit("tensor", lambda e, o=ps_ap, l=lhsT, r=rhs, st=start, sp=stop:
               e.matmul(o, l, r, start=st, stop=sp), waits=w,
               inc=sm["tensor"] if inc else None)

    def evac(eng, bid, fn, waits=()):
        s = sm[eng]
        P.emit(eng, fn, waits=waits, inc=s)
        bankfree[bid] = (s, s.n)
        return (s, s.n)

    def layer_norm(dst_ap, src_ap, W, waits=()):
        P.emit("vector", lambda e: e.tensor_scalar(
            ln_scr[:, 0:W], src_ap, 1.0, 0.0, op0=ALU.mult, op1=ALU.add,
            accum_out=ln_m[:, 0:1]), waits=list(waits) + [ (sm["vector"], ln_guard[0]) ])
        P.emit("vector", lambda e: e.scalar_tensor_tensor(
            ln_scr2[:, 0:W], src_ap, 1.0, src_ap, op0=ALU.mult, op1=ALU.mult,
            accum_out=ln_s2[:, 0:1]))
        v2 = vval()
        P.emit("vector", lambda e: e.tensor_tensor(ln_v[:, 0:1], ln_m[:, 0:1],
                                                   ln_m[:, 0:1], op=ALU.mult),
               waits=[v2])
        v3 = vval()
        P.emit("vector", lambda e, W=W: e.scalar_tensor_tensor(
            ln_v[:, 0:1], ln_v[:, 0:1], -1.0 / W, ln_s2[:, 0:1],
            op0=ALU.mult, op1=ALU.add), waits=[v3])
        v4 = vval()
        P.emit("scalar", lambda e, W=W: e.activation(
            ln_r[:, 0:1], ln_v[:, 0:1], AF.Ln, bias=eps_col[:, 0:1], scale=1.0 / W),
            waits=[v4])
        s1 = sval()
        P.emit("scalar", lambda e: e.activation(ln_r[:, 0:1], ln_r[:, 0:1], AF.Exp,
                                                scale=-0.5), waits=[s1])
        sv = sval()
        P.emit("vector", lambda e, W=W: e.tensor_scalar(
            ln_m[:, 0:1], ln_m[:, 0:1], 1.0 / W, None, op0=ALU.mult), waits=[v4])
        v5 = vval()
        P.emit("vector", lambda e: e.tensor_scalar(
            dst_ap, src_ap, ln_m[:, 0:1], ln_r[:, 0:1], op0=ALU.subtract, op1=ALU.mult),
            waits=[sv, v5])
        ln_guard[0] = sm["vector"].n
        return vval()

    def layer_norm_span(src_f, NT, W, waits_f):
        for i in range(NT):
            P.emit("vector", lambda e, i=i: e.tensor_scalar(
                ln_scr[:, 0:W], src_f(i), 1.0, 0.0, op0=ALU.mult, op1=ALU.add,
                accum_out=lnb_m[:, i:i + 1]),
                waits=list(waits_f(i)) + ([(sm["vector"], ln_guard[0])]
                                          if i == 0 else [vval()]))
            P.emit("vector", lambda e, i=i: e.scalar_tensor_tensor(
                ln_scr2[:, 0:W], src_f(i), 1.0, src_f(i), op0=ALU.mult, op1=ALU.mult,
                accum_out=lnb_s2[:, i:i + 1]), waits=[vval()] if i else ())
        v2 = vval()
        P.emit("vector", lambda e: e.tensor_tensor(
            lnb_v[:, 0:NT], lnb_m[:, 0:NT], lnb_m[:, 0:NT], op=ALU.mult), waits=[v2])
        v3 = vval()
        P.emit("vector", lambda e, W=W: e.scalar_tensor_tensor(
            lnb_v[:, 0:NT], lnb_v[:, 0:NT], -1.0 / W, lnb_s2[:, 0:NT],
            op0=ALU.mult, op1=ALU.add), waits=[v3])
        v4 = vval()
        P.emit("scalar", lambda e, W=W: e.activation(
            lnb_r[:, 0:NT], lnb_v[:, 0:NT], AF.Ln, bias=eps_col[:, 0:1], scale=1.0 / W),
            waits=[v4])
        s1 = sval()
        P.emit("scalar", lambda e: e.activation(lnb_r[:, 0:NT], lnb_r[:, 0:NT], AF.Exp,
                                                scale=-0.5), waits=[s1])
        svr = sval()
        P.emit("vector", lambda e, W=W: e.tensor_scalar(
            lnb_m[:, 0:NT], lnb_m[:, 0:NT], 1.0 / W, None, op0=ALU.mult), waits=[v4])
        v5 = vval()
        first = [True]

        def norm(i, dst_ap, waits=()):
            w = list(waits) + ([svr, v5] if first[0] else [])
            first[0] = False
            P.emit("vector", lambda e, i=i, d=dst_ap: e.tensor_scalar(
                d, src_f(i), lnb_m[:, i:i + 1], lnb_r[:, i:i + 1],
                op0=ALU.subtract, op1=ALU.mult), waits=w)
            ln_guard[0] = sm["vector"].n
            return vval()

        return norm

    w1 = load(ident_f[:], din["c_ident"][:])
    load(triu_f[:], din["c_triu"][:])
    load(trilm1_f[:], din["c_trilm1"][:])
    load(ones_row[:], din["c_ones_row"][:])
    for sb_t, nm in [(g1c, "ln1_w"), (b1c, "ln1_b"), (g2c, "ln2_w"), (b2c, "ln2_b"),
                     (btimec, "b_time"), (cbc, "conv_b"), (skipc, "skip")]:
        load(sb_t[:], din[nm].rearrange("(c p) -> p c", p=128), slow=True)
    load(bdecc[:, 0:1], din["dec_b"].rearrange("(c p) -> p c", p=16), slow=True)
    for j, sb_t in enumerate((cw0c, cw1c, cw2c)):
        load(sb_t[:].rearrange("p (c one) -> p c one", one=1),
             din["conv_w"][:, j:j + 1].rearrange("(c p) one -> p c one", p=128),
             slow=True)
    wA = load(A_sb[:].rearrange("p (c s) -> p c s", s=16),
              din["A_log"].rearrange("(c p) s -> p c s", p=128))
    P.emit("vector", lambda e: e.memset(eps_col[:], LN_EPS), inc=sm["vector"])
    P.emit("vector", lambda e: e.tensor_copy(ident_b[:], ident_f[:]), waits=[w1],
           inc=sm["vector"])
    P.emit("scalar", lambda e: e.activation(A_sb[:], A_sb[:], AF.Exp), waits=[wA],
           inc=sm["scalar"])
    P.emit("vector", lambda e: e.tensor_scalar_mul(A_sb[:], A_sb[:], -1.0),
           waits=[sval()], inc=sm["vector"])

    P.scope("p0_weights")
    es_p0 = ExitStack()
    tstage8 = sbR(es_p0, "tstage8", (128, 8 * 3072), BF16)
    i32stage = sbR(es_p0, "i32stage", (128, 3072), I32)

    def transpose_int_w(dst, src, rows, cols):
        P.wait("vector", *[(s, s.n) for pair in store_sems.values() for s in pair
                           if s.n > 0])
        for rb in range(rows // 128):
            wld = load(i32stage[:, 0:cols], src[128 * rb:128 * (rb + 1), :], q="scalar")
            P.emit("vector", lambda e, c=cols: e.tensor_copy(
                bfstage[:, 0:c], i32stage[:, 0:c]), waits=[wld, tval()],
                inc=sm["vector"])
            vv = vval()
            P.wait("scalar", vv)
            for cb in range(cols // 128):
                bid = 6 + cb % 2
                P.emit("tensor", lambda e, cb=cb: e.transpose(
                    psb[cb % 2][:, 0:128], bfstage[:, 128 * cb:128 * (cb + 1)],
                    ident_b[:]), waits=[vv] + bfree(bid), inc=sm["tensor"])
                tv = tval()
                evac("vector", bid, lambda e, cb=cb, rb=rb, rows=rows: e.tensor_copy(
                    tstage8[:, cb * rows + 128 * rb:cb * rows + 128 * (rb + 1)],
                    psb[cb % 2][:, 0:128]), waits=[tv])
        vv = vval()
        for cb in range(cols // 128):
            store(dst[128 * cb:128 * (cb + 1), 0:rows],
                  tstage8[:, cb * rows:(cb + 1) * rows], waits=[vv])

    transpose_int_w(s_wgT, din["w_gate"], FFI, D)
    transpose_int_w(s_wuT, din["w_up"], FFI, D)
    transpose_int_w(s_wdT, din["w_down"], D, FFI)
    fence()
    es_p0.close()

    def cast_rows(dst, src, rows, cols, tiled=False):
        for rb in range(rows // 128):
            wld = load(f32stage[:, 0:cols], src[128 * rb:128 * (rb + 1), :], q="scalar")
            P.emit("vector", lambda e, c=cols: e.tensor_copy(bfstage[:, 0:c],
                                                             f32stage[:, 0:c]),
                   waits=[wld], inc=sm["vector"])
            vv = vval()
            if tiled:
                src3 = bfstage[:, 0:cols].rearrange("p (mc j) -> p mc j", j=128)
                dst3 = s_wtt[:, :].rearrange("p (mc kc j) -> p mc (kc j)", kc=16, j=128)[
                    :, :, 128 * rb:128 * (rb + 1)]
                st = store(dst3, src3, waits=[vv])
            else:
                st = store(dst[128 * rb:128 * (rb + 1), :], bfstage[:, 0:cols], waits=[vv])
            P.wait("scalar", st)
            P.wait("vector", st)
        fence()

    cast_rows(s_winbf, din["W_in"], D, 2 * INNER)
    cast_rows(s_woutbf, din["W_out"], INNER, D)
    cast_rows(None, din["W_time"], INNER, INNER, tiled=True)

    es_a1 = ExitStack()
    A1 = lambda n, s, d=F32: sbR(es_a1, n, s, d)
    wq_bf = A1("wq_bf", (128, 8 * 1024), BF16)
    wk_bf = A1("wk_bf", (128, 8 * 1024), BF16)
    wv_bf = A1("wv_bf", (128, 8 * 1024), BF16)
    wdec_bf = A1("wdec_bf", (128, 8 * 16), BF16)
    es_fold = ExitStack()
    FO = lambda n, s, d=F32: sbR(es_fold, n, s, d)
    rowbuf = FO("rowbuf", (1, 1024))
    rowbuf2 = FO("rowbuf2", (1, 1024))
    brow = FO("brow", (1, 4096))
    bprow = FO("bprow", (1, 4096))
    btmp = FO("btmp", (128, 128))
    w_ob = load(rowbuf[:], din["o_b"][None, :])
    w_f2b = load(rowbuf2[:], din["f2_b"][None, :])

    def bcast_row(dst_sb, row_sb, n, wait):
        for i in range((n + 511) // 512):
            nn = min(512, n - 512 * i)
            bid = i % 2
            mm(psf[bid][:, 0:nn], bid, ones_row[:], row_sb[0:1, 512 * i:512 * i + nn],
               True, True, waits=[wait])
            tv = tval()
            evac("vector", bid, lambda e, b=bid, i=i, nn=nn: e.tensor_copy(
                dst_sb[:, 512 * i:512 * i + nn], psf[b][:, 0:nn]), waits=[tv])
        return vval()

    v_obbc = bcast_row(ob_bc, rowbuf, 1024, w_ob)
    v_f2bc = bcast_row(f2b_bc, rowbuf2, 1024, w_f2b)

    def fold_prep(w_name, n_out, gcol, betacol, b_name, wbf_sb, bias_col, neg_bias_col,
                  wbf_dram=None, bias_npart=128):
        nkc = 8
        npass = max(1, n_out // 2048) if n_out >= 2048 else 1
        ncols_pass = min(n_out, 2048)
        for ip in range(npass):
            for kc in range(nkc):
                wld = load(f32stage[:, 0:n_out], din[w_name][128 * kc:128 * (kc + 1), :],
                           q="scalar")
                if ip == 0:
                    P.emit("vector", lambda e, kc=kc, n=n_out: e.tensor_scalar(
                        bfstage[:, 0:n], f32stage[:, 0:n], gcol[:, kc:kc + 1], None,
                        op0=ALU.mult), waits=[wld], inc=sm["vector"])
                    vv = vval()
                    if wbf_dram is not None:
                        st = store(wbf_dram[128 * kc:128 * (kc + 1), :], bfstage[:, 0:n_out],
                                   waits=[vv])
                        P.wait("vector", st)
                    else:
                        P.emit("vector", lambda e, kc=kc, n=n_out: e.tensor_copy(
                            wbf_sb[:, n * kc:n * (kc + 1)], bfstage[:, 0:n]),
                            waits=[vv], inc=sm["vector"])
                nb = (min(ncols_pass, n_out - 2048 * ip) + 511) // 512
                for ib in range(nb):
                    c0 = 2048 * ip + 512 * ib
                    cn = min(512, n_out - c0)
                    mm(psf[2 + ib][0:1, 0:cn], 2 + ib, betacol[:, kc:kc + 1],
                       f32stage[:, c0:c0 + cn], kc == 0, kc == nkc - 1,
                       waits=[wld, vval()])
                P.wait("scalar", tval(), vval())
            tv = tval()
            for ib in range(nb):
                c0 = 2048 * ip + 512 * ib
                cn = min(512, n_out - c0)
                evac("vector", 2 + ib, lambda e, c0=c0, cn=cn, ib=ib: e.tensor_copy(
                    brow[0:1, c0:c0 + cn], psf[2 + ib][0:1, 0:cn]), waits=[tv])
        wb = load(bprow[0:1, 0:n_out], din[b_name][None, :])
        P.emit("vector", lambda e, n=n_out: e.tensor_add(
            brow[0:1, 0:n], brow[0:1, 0:n], bprow[0:1, 0:n]), waits=[wb, vval()],
            inc=sm["vector"])
        vv = vval()
        nmc = max(1, n_out // 128) if n_out >= 128 else 1
        cw = min(128, n_out)
        for mc in range(nmc):
            P.emit("vector", lambda e, mc=mc, cw=cw: e.tensor_copy(
                btmp[0:1, 0:cw], brow[0:1, 128 * mc:128 * mc + cw]),
                waits=[vv] + bfree(5), inc=sm["vector"])
            vv2 = vval()
            P.emit("tensor", lambda e: e.transpose(psf[5][:, 0:128], btmp[:, 0:128],
                                                   ident_f[:]),
                   waits=[vv2] + bfree(5), inc=sm["tensor"])
            tv = tval()
            evac("vector", 5, lambda e, mc=mc, cw=cw: e.tensor_copy(
                bias_col[0:cw, mc:mc + 1], psf[5][0:cw, 0:1]), waits=[tv])
            if neg_bias_col is not None:
                P.emit("vector", lambda e, mc=mc, cw=cw: e.tensor_scalar_mul(
                    neg_bias_col[0:cw, mc:mc + 1], bias_col[0:cw, mc:mc + 1], -1.0),
                    waits=[vval()], inc=sm["vector"])
        fence()

    fold_prep("q_w", D, g1c, b1c, "q_b", wq_bf, bqc, nbqc)
    fold_prep("k_w", D, g1c, b1c, "k_b", wk_bf, bkc, nbkc)
    fold_prep("v_w", D, g1c, b1c, "v_b", wv_bf, bvc, None)
    fold_prep("dec_w", HN, g1c, b1c, "dec_b", wdec_bf, bdecc, nbdecc, bias_npart=16)
    fold_prep("f1_w", FFH, g2c, b2c, "f1_b", None, f1bc, None, wbf_dram=s_f1bf)
    fence()
    es_fold.close()

    SPA = 512
    NTT = SPA // 128
    x_span = A1("x_span", (128, NTT * 1024))
    z2_bf = A1("z2_bf", (128, NTT * 1024), BF16)
    z2fm = A1("z2fm", (128, 8 * SPA), BF16)
    stg_fm = A1("stg_fm", (128, SPA), BF16)
    vbf_fm = A1("vbf_fm", (128, 8 * SPA), BF16)
    stg_tok = [A1(f"stg_tok{i}", (128, 1024), BF16) for i in range(2)]
    stg_tok_st = [None, None]
    stg_fm_st = None
    dptmp = A1("dptmp", (16, 1024))
    dptmp2 = A1("dptmp2", (16, 1024))
    logl_buf = A1("logl_buf", (16, 1024))
    logl_tokc = A1("logl_tokc", (128, 16))

    P.scope("p_a1")
    def phase_a1():
        nonlocal stg_fm_st
        for sp in range(T // SPA):
            t0 = sp * SPA
            fence(dma=False)
            for tt in range(NTT):
                wld = load(x_span[:, 1024 * tt:1024 * (tt + 1)],
                           x_in[t0 + 128 * tt:t0 + 128 * (tt + 1), :])
                layer_norm(x_span[:, 1024 * tt:1024 * (tt + 1)],
                           x_span[:, 1024 * tt:1024 * (tt + 1)], 1024, waits=[wld])
                vX = vval()
                store(s_xn0[t0 + 128 * tt:t0 + 128 * (tt + 1), :],
                      x_span[:, 1024 * tt:1024 * (tt + 1)], waits=[vX])
                layer_norm(z2_bf[:, 1024 * tt:1024 * (tt + 1)],
                           x_span[:, 1024 * tt:1024 * (tt + 1)], 1024, waits=[vX])
                vZ = vval()
                for kc in range(8):
                    bid = 6 + kc % 2
                    P.emit("tensor", lambda e, tt=tt, kc=kc: e.transpose(
                        psb[kc % 2][:, 0:128],
                        z2_bf[:, 1024 * tt + 128 * kc:1024 * tt + 128 * (kc + 1)],
                        ident_b[:]), waits=[vZ] + bfree(bid), inc=sm["tensor"])
                    tv = tval()
                    evac("vector", bid, lambda e, tt=tt, kc=kc: e.tensor_copy(
                        z2fm[:, SPA * kc + 128 * tt:SPA * kc + 128 * (tt + 1)],
                        psb[kc % 2][:, 0:128]), waits=[tv])
            vZF = vval()

            for (wsb, bcol, nbcol, fmd, tokd, is_v) in (
                    (wq_bf, bqc, nbqc, s_phiq_fm, s_phiq_tok, False),
                    (wk_bf, bkc, nbkc, s_phik_fm, s_phik_tok, False),
                    (wv_bf, bvc, None, None, s_v_tok, True)):
                for mc in range(8):
                    for nt in range(SPA // 512):
                        bid = (mc * 2 + nt) % 4
                        ps = psf[bid]
                        for kc in range(8):
                            mm(ps[:, 0:512], bid,
                               wsb[:, 1024 * kc + 128 * mc:1024 * kc + 128 * (mc + 1)],
                               z2fm[:, SPA * kc + 512 * nt:SPA * kc + 512 * (nt + 1)],
                               kc == 0, kc == 7, waits=[vZF] if kc == 0 else ())
                        tv = tval()
                        col = 512 * nt
                        if not is_v:
                            P.emit("scalar", lambda e, ps=ps, nb=nbcol, mc=mc: e.activation(
                                phtmp1[:, 0:512], ps[:, 0:512], AF.Relu,
                                bias=nb[:, mc:mc + 1], scale=-1.0),
                                waits=[tv, vval()], inc=sm["scalar"])
                            P.emit("scalar", lambda e: e.activation(
                                phtmp1[:, 0:512], phtmp1[:, 0:512], AF.Exp, scale=-1.0),
                                waits=[sval()])
                            P.emit("scalar", lambda e, ps=ps, b=bcol, mc=mc: e.activation(
                                phtmp2[:, 0:512], ps[:, 0:512], AF.Relu,
                                bias=b[:, mc:mc + 1]), inc=sm["scalar"])
                            sv = sval()
                            bankfree[bid] = sv
                            P.emit("vector", lambda e, col=col: e.tensor_add(
                                stg_fm[:, col:col + 512], phtmp1[:, 0:512],
                                phtmp2[:, 0:512]),
                                waits=[sv] + (stg_fm_st or []), inc=sm["vector"])
                        else:
                            evac("scalar", bid, lambda e, ps=ps, b=bcol, mc=mc, col=col:
                                 e.activation(stg_fm[:, col:col + 512], ps[:, 0:512],
                                              AF.Identity, bias=b[:, mc:mc + 1]),
                                 waits=[tv] + (stg_fm_st or []))
                    vv = vval()
                    svv = sval()
                    st_w = None
                    if fmd is not None:
                        st_w = store(fmd[128 * mc:128 * (mc + 1), t0:t0 + SPA],
                                     stg_fm[:, 0:SPA], waits=[vv, svv])
                    P.emit("vector", lambda e, mc=mc: e.tensor_copy(
                        vbf_fm[:, SPA * mc:SPA * (mc + 1)], stg_fm[:, 0:SPA]),
                        waits=[vv, svv], inc=sm["vector"])
                    stg_fm_st = [vval()] + ([st_w] if st_w else [])
                vv = vval()
                for tt in range(NTT):
                    sl = tt % 2
                    for mc in range(8):
                        bid = 6 + mc % 2
                        P.emit("tensor", lambda e, tt=tt, mc=mc: e.transpose(
                            psb[mc % 2][:, 128:256],
                            vbf_fm[:, SPA * mc + 128 * tt:SPA * mc + 128 * (tt + 1)],
                            ident_b[:]), waits=[vv] + bfree(bid), inc=sm["tensor"])
                        tv = tval()
                        evac("vector", bid, lambda e, sl=sl, mc=mc: e.tensor_copy(
                            stg_tok[sl][:, 128 * mc:128 * (mc + 1)],
                            psb[mc % 2][:, 128:256]),
                            waits=[tv] + ([stg_tok_st[sl]] if stg_tok_st[sl] else []))
                    vv2 = vval()
                    stg_tok_st[sl] = store(tokd[t0 + 128 * tt:t0 + 128 * (tt + 1), :],
                                           stg_tok[sl][:, 0:1024], waits=[vv2])

            for nt in range(SPA // 512):
                bid = 4
                ps = psf[4]
                for kc in range(8):
                    mm(ps[0:16, 0:512], bid, wdec_bf[:, 16 * kc:16 * (kc + 1)],
                       z2fm[:, SPA * kc + 512 * nt:SPA * kc + 512 * (nt + 1)],
                       kc == 0, kc == 7)
                tv = tval()
                P.emit("scalar", lambda e, ps=ps: e.activation(
                    dptmp[:, 0:512], ps[0:16, 0:512], AF.Abs, bias=bdecc[:, 0:1]),
                    waits=[tv, vval()], inc=sm["scalar"])
                P.emit("scalar", lambda e: e.activation(dptmp[:, 0:512], dptmp[:, 0:512],
                                                        AF.Exp, scale=-1.0),
                       waits=[sval()])
                P.emit("scalar", lambda e: e.activation(dptmp[:, 0:512], dptmp[:, 0:512],
                                                        AF.Ln, bias=1.0),
                       waits=[sval()])
                evac("scalar", bid, lambda e, ps=ps: e.activation(
                    dptmp2[:, 0:512], ps[0:16, 0:512], AF.Relu, bias=nbdecc[:, 0:1],
                    scale=-1.0))
                sv = sval()
                P.emit("vector", lambda e, nt=nt: e.tensor_add(
                    logl_buf[:, 512 * nt:512 * (nt + 1)], dptmp[:, 0:512], dptmp2[:, 0:512]),
                    waits=[sv])
                P.emit("vector", lambda e, nt=nt: e.tensor_scalar(
                    logl_buf[:, 512 * nt:512 * (nt + 1)],
                    logl_buf[:, 512 * nt:512 * (nt + 1)], -1.0, NEG_LOG_LAM,
                    op0=ALU.mult, op1=ALU.max), waits=[vval()], inc=sm["vector"])
            vLL = vval()
            for ci in range(SPA // 128):
                c = sp * (SPA // 128) + ci
                P.emit("tensor", lambda e, ci=ci: e.transpose(
                    psf[4][:, 0:16], logl_buf[0:16, 128 * ci:128 * (ci + 1)],
                    ident_f[0:16, 0:16]),
                    waits=[vLL] + bfree(4), inc=sm["tensor"])
                tv = tval()
                evac("vector", 4, lambda e: e.tensor_copy(logl_tokc[:], psf[4][:, 0:16]),
                     waits=[tv])
                vv = vval()
                mm(psf[5][:, 0:16], 5, triu_f[:], logl_tokc[:], True, True, waits=[vv])
                mm(psf[5][:, 16:32], 5, trilm1_f[:], logl_tokc[:], True, True)
                tv = tval()
                P.emit("scalar", lambda e, c=c: e.activation(
                    af_tok[:, 16 * c:16 * (c + 1)], psf[5][:, 0:16], AF.Exp),
                    waits=[tv], inc=sm["scalar"])
                P.emit("scalar", lambda e, c=c: e.activation(
                    ab_tok[:, 16 * c:16 * (c + 1)], psf[5][:, 16:32], AF.Exp),
                    inc=sm["scalar"])
                sv = sval()
                evac("vector", 5, lambda e, c=c: e.tensor_copy(
                    lloc_tok[:, 16 * c:16 * (c + 1)], psf[5][:, 0:16]), waits=[sv])
        vv = vval()
        svv = sval()
        wgr = load(gamrow[0:1, :], af_tok[127:128, :], waits=[svv, vv])
        mm(psf[0][:, 0:512], 0, ones_row[:], gamrow[0:1, :], True, True, waits=[wgr])
        tv = tval()
        evac("vector", 0, lambda e: e.tensor_copy(gam_bc[:], psf[0][:, 0:512]), waits=[tv])

    phase_a1()
    fence()
    es_a1.close()
    if s_dbg_misc is not None:
        for i, t in enumerate((lloc_tok, af_tok, ab_tok, gam_bc)):
            store(s_dbg_misc[:, 512 * i:512 * (i + 1)], t[:])
        fence()
    if upto == "pa1":
        return _emit(nc, P, es, din, douts)

    P.scope("p_a2")
    es_a2 = ExitStack()
    A2 = lambda n, s, d=F32: sbR(es_a2, n, s, d)
    att_acc = A2("att_acc", (64, 2 * T), BF16)
    phq_tok_h = A2("phq_tok_h", (128, 2 * NCH * KH), BF16)
    phk_tok_h = A2("phk_tok_h", (128, 2 * NCH * KH), BF16)
    phq_fm_h = A2("phq_fm_h", (64, 2 * T), BF16)
    phk_fm_h = A2("phk_fm_h", (64, 2 * T), BF16)
    v_tok_h = A2("v_tok_h", (128, 2 * NCH * KV), BF16)
    R_sb = A2("R_sb", (64, 2 * NCH * KV))
    R_bf = A2("R_bf", (64, 2 * NCH * KV), BF16)
    S_sb = A2("S_sb", (64, 2 * KV))
    S_bf = A2("S_bf", (64, 2 * KV), BF16)
    kscl = A2("kscl", (128, 2 * KH), BF16)
    qf_tok = A2("qf_tok", (128, 2 * KH), BF16)
    qb_tok = A2("qb_tok", (128, 2 * KH), BF16)
    qf_fm = A2("qf_fm", (64, 2 * 128), BF16)
    qb_fm = A2("qb_fm", (64, 2 * 128), BF16)
    D_sb = A2("D_sb", (128, 2 * 128))
    W_sb = A2("W_sb", (128, 2 * 128), BF16)
    numden = A2("numden", (128, 4))
    att_tok = A2("att_tok", (128, 2 * KH), BF16)
    lrow = A2("lrow", (1, 2 * 128))

    def phase_a2():
        P.emit("vector", lambda e: e.memset(
            v_tok_h[:].rearrange("p (c k) -> p c k", k=KV)[:, :, KH:KV], 1.0),
            inc=sm["vector"])
        SL = {}

        def ld_head(sl, h):
            hc = h * KH
            o = sl * NCH * KH
            o5 = sl * NCH * KV
            w = []
            w.append(load(phq_tok_h[:, o:o + NCH * KH].rearrange(
                "p (c k) -> p c k", k=KH),
                s_phiq_tok[:, hc:hc + KH].rearrange("(c p) k -> p c k", p=128)))
            w.append(load(phk_tok_h[:, o:o + NCH * KH].rearrange(
                "p (c k) -> p c k", k=KH),
                s_phik_tok[:, hc:hc + KH].rearrange("(c p) k -> p c k", p=128)))
            w.append(load(phq_fm_h[:, sl * T:(sl + 1) * T], s_phiq_fm[hc:hc + KH, :]))
            w.append(load(phk_fm_h[:, sl * T:(sl + 1) * T], s_phik_fm[hc:hc + KH, :]))
            w.append(load(v_tok_h[:, o5:o5 + NCH * KV].rearrange(
                "p (c k) -> p c k", k=KV)[:, :, 0:KH],
                s_v_tok[:, hc:hc + KH].rearrange("(c p) k -> p c k", p=128)))
            SL[(sl, "lds")] = w
            P.emit("vector", lambda e, sl=sl: e.memset(
                R_sb[:, 2 * NCH * KV * 0 + (sl * NCH + NCH - 1) * KV:
                     (sl * NCH + NCH) * KV], 0.0), inc=sm["vector"])
            P.emit("vector", lambda e, sl=sl: e.memset(
                S_sb[:, KV * sl:KV * (sl + 1)], 0.0), inc=sm["vector"])
            P.emit("vector", lambda e, sl=sl: e.tensor_copy(
                R_bf[:, (sl * NCH + NCH - 1) * KV:(sl * NCH + NCH) * KV],
                R_sb[:, (sl * NCH + NCH - 1) * KV:(sl * NCH + NCH) * KV]),
                waits=[vval()], inc=sm["vector"])
            P.emit("vector", lambda e, sl=sl: e.tensor_copy(
                S_bf[:, KV * sl:KV * (sl + 1)], S_sb[:, KV * sl:KV * (sl + 1)]),
                waits=[vval()], inc=sm["vector"])
            SL[(sl, "init")] = vval()

        def bwd_step(sl, h, c):
            sc = c * HN + h
            ksl = kscl[:, KH * sl:KH * (sl + 1)]
            ro = sl * NCH * KV
            P.emit("vector", lambda e, c=c, sc=sc, sl=sl, ksl=ksl: e.tensor_scalar(
                ksl, phq_tok_h[:, 0:KH] if False else
                phk_tok_h[:, sl * NCH * KH + KH * c:sl * NCH * KH + KH * (c + 1)],
                af_tok[:, sc:sc + 1], None, op0=ALU.mult),
                waits=[SL[(sl, "kfree")]] if (sl, "kfree") in SL else
                      SL[(sl, "lds")] + [SL[(sl, "init")]], inc=sm["vector"])
            vv = vval()
            yield
            bid = 4 + sl
            mm(psf[bid][0:64, 0:KV], bid, ksl,
               v_tok_h[:, sl * NCH * KV + KV * c:sl * NCH * KV + KV * (c + 1)],
               True, True, waits=[vv])
            tv = tval()
            SL[(sl, "kfree")] = tv
            yield
            evac("vector", bid, lambda e, c=c, sc=sc, ro=ro, bid=bid:
                 e.scalar_tensor_tensor(
                     R_sb[:, ro + KV * (c - 1):ro + KV * c],
                     R_sb[:, ro + KV * c:ro + KV * (c + 1)],
                     gam_bc[0:64, sc:sc + 1], psf[bid][0:64, 0:KV],
                     op0=ALU.mult, op1=ALU.add), waits=[tv, vval()])
            yield
            P.emit("vector", lambda e, c=c, ro=ro: e.tensor_copy(
                R_bf[:, ro + KV * (c - 1):ro + KV * c],
                R_sb[:, ro + KV * (c - 1):ro + KV * c]),
                waits=[vval()], inc=sm["vector"])

        def fwd_step(sl, h, c):
            sc = c * HN + h
            tc = c * 128
            o = sl * NCH * KH
            o5 = sl * NCH * KV
            qo = sl * T
            dsl_ = D_sb[:, 128 * sl:128 * (sl + 1)]
            wsl_ = W_sb[:, 128 * sl:128 * (sl + 1)]
            P.emit("tensor", lambda e, sc=sc, sl=sl: e.transpose(
                psf[5][0:1, 128 * sl:128 * sl + 128], lloc_tok[:, sc:sc + 1],
                ident_f[:]), waits=bfree(5), inc=sm["tensor"])
            tvL = tval()
            evac("vector", 5, lambda e, sl=sl: e.tensor_copy(
                lrow[0:1, 128 * sl:128 * (sl + 1)],
                psf[5][0:1, 128 * sl:128 * sl + 128]), waits=[tvL, vval()])
            vL = vval()
            yield
            bidD = sl
            mm(psf[bidD][:, 0:128], bidD, ones_row[:],
               lrow[0:1, 128 * sl:128 * (sl + 1)], True, True, waits=[vL])
            tv1 = tval()
            bidS = 2 + sl
            mm(psf[bidS][:, 0:128], bidS, phk_fm_h[:, qo + tc:qo + tc + 128],
               phq_fm_h[:, qo + tc:qo + tc + 128], True, True)
            tv2 = tval()
            yield
            P.emit("scalar", lambda e, sc=sc, dsl_=dsl_, bidD=bidD: e.activation(
                dsl_, psf[bidD][:, 0:128], AF.Abs, bias=lloc_tok[:, sc:sc + 1],
                scale=-1.0), waits=[tv1] + ([SL[(sl, "dfree")]]
                                            if (sl, "dfree") in SL else []),
                inc=sm["scalar"])
            bankfree[bidD] = sval()
            P.emit("scalar", lambda e, dsl_=dsl_: e.activation(
                dsl_, dsl_, AF.Exp, scale=-1.0), waits=[sval()], inc=sm["scalar"])
            svD = sval()
            yield
            qfsl = qf_tok[:, KH * sl:KH * (sl + 1)]
            qbsl = qb_tok[:, KH * sl:KH * (sl + 1)]
            P.emit("vector", lambda e, c=c, sc=sc, o=o, qfsl=qfsl: e.tensor_scalar(
                qfsl, phq_tok_h[:, o + KH * c:o + KH * (c + 1)],
                af_tok[:, sc:sc + 1], None, op0=ALU.mult),
                waits=[SL[(sl, "qtfree")]] if (sl, "qtfree") in SL else [],
                inc=sm["vector"])
            vq1 = vval()
            yield
            P.emit("tensor", lambda e, sl=sl, qfsl=qfsl: e.transpose(
                psb[0][0:64, 128 * sl:128 * sl + 128], qfsl, ident_b[:]),
                waits=[vq1] + bfree(6), inc=sm["tensor"])
            tq1 = tval()
            evac("vector", 6, lambda e, sl=sl: e.tensor_copy(
                qf_fm[:, 128 * sl:128 * (sl + 1)],
                psb[0][0:64, 128 * sl:128 * sl + 128]), waits=[tq1])
            P.emit("vector", lambda e, c=c, sc=sc, o=o, qbsl=qbsl: e.tensor_scalar(
                qbsl, phq_tok_h[:, o + KH * c:o + KH * (c + 1)],
                ab_tok[:, sc:sc + 1], None, op0=ALU.mult), inc=sm["vector"])
            vq2 = vval()
            yield
            evac("vector", bidS, lambda e, dsl_=dsl_, wsl_=wsl_, bidS=bidS:
                 e.tensor_tensor(wsl_, dsl_, psf[bidS][:, 0:128], op=ALU.mult),
                 waits=[svD, tv2])
            SL[(sl, "dfree")] = vval()
            P.emit("tensor", lambda e, sl=sl, qbsl=qbsl: e.transpose(
                psb[1][0:64, 128 * sl:128 * sl + 128], qbsl, ident_b[:]),
                waits=[vq2] + bfree(7), inc=sm["tensor"])
            tq2 = tval()
            SL[(sl, "qtfree")] = tq2
            evac("vector", 7, lambda e, sl=sl: e.tensor_copy(
                qb_fm[:, 128 * sl:128 * (sl + 1)],
                psb[1][0:64, 128 * sl:128 * sl + 128]), waits=[tq2])
            vq = vval()
            yield
            bidO = 2 + sl
            mm(psf[bidO][:, 0:KV], bidO, wsl_,
               v_tok_h[:, o5 + KV * c:o5 + KV * (c + 1)], True, False, waits=[vq])
            mm(psf[bidO][:, 0:KV], bidO, qf_fm[:, 128 * sl:128 * (sl + 1)],
               S_bf[:, KV * sl:KV * (sl + 1)], False, False)
            mm(psf[bidO][:, 0:KV], bidO, qb_fm[:, 128 * sl:128 * (sl + 1)],
               R_bf[:, sl * NCH * KV + KV * c:sl * NCH * KV + KV * (c + 1)],
               False, True)
            tvO = tval()
            ksl = kscl[:, KH * sl:KH * (sl + 1)]
            P.emit("vector", lambda e, c=c, sc=sc, o=o, ksl=ksl: e.tensor_scalar(
                ksl, phk_tok_h[:, o + KH * c:o + KH * (c + 1)],
                ab_tok[:, sc:sc + 1], None, op0=ALU.mult),
                waits=[SL[(sl, "kfree")]] if (sl, "kfree") in SL else [],
                inc=sm["vector"])
            vk = vval()
            yield
            bidU = 4 + sl
            mm(psf[bidU][0:64, 0:KV], bidU, ksl,
               v_tok_h[:, o5 + KV * c:o5 + KV * (c + 1)], True, True, waits=[vk])
            tvS = tval()
            SL[(sl, "kfree")] = tvS
            yield
            evac("vector", bidU, lambda e, sc=sc, sl=sl, bidU=bidU:
                 e.scalar_tensor_tensor(
                     S_sb[:, KV * sl:KV * (sl + 1)], S_sb[:, KV * sl:KV * (sl + 1)],
                     gam_bc[0:64, sc:sc + 1], psf[bidU][0:64, 0:KV],
                     op0=ALU.mult, op1=ALU.add), waits=[tvS, vval()])
            P.emit("vector", lambda e, sl=sl: e.tensor_copy(
                S_bf[:, KV * sl:KV * (sl + 1)], S_sb[:, KV * sl:KV * (sl + 1)]),
                waits=[vval()], inc=sm["vector"])
            P.emit("vector", lambda e, sl=sl, bidO=bidO: e.tensor_scalar_max(
                numden[:, 2 * sl:2 * sl + 1], psf[bidO][:, KH:KV], 1e-6),
                waits=[tvO, vval()], inc=sm["vector"])
            P.emit("vector", lambda e, sl=sl: e.reciprocal(
                numden[:, 2 * sl + 1:2 * sl + 2], numden[:, 2 * sl:2 * sl + 1]),
                waits=[vval()])
            yield
            atsl = att_tok[:, KH * sl:KH * (sl + 1)]
            evac("vector", bidO, lambda e, sl=sl, bidO=bidO, atsl=atsl: e.tensor_scalar(
                atsl, psf[bidO][:, 0:KH], numden[:, 2 * sl + 1:2 * sl + 2], None,
                op0=ALU.mult), waits=[vval()])
            va = vval()
            yield
            P.emit("tensor", lambda e, sl=sl, atsl=atsl: e.transpose(
                psb[0][0:64, 384 + 128 * sl:384 + 128 * sl + 128] if False else
                psb[1][0:64, 256 + 128 * sl:256 + 128 * sl + 128],
                atsl, ident_b[:]), waits=[va] + bfree(7), inc=sm["tensor"])
            tva = tval()
            evac("vector", 7, lambda e, tc=tc, sl=sl: e.tensor_copy(
                att_acc[:, sl * T + tc:sl * T + tc + 128],
                psb[1][0:64, 256 + 128 * sl:256 + 128 * sl + 128]),
                waits=[tva] + ([SL[(sl, "atfree")]] if (sl, "atfree") in SL else []))

        def zipper(g0, g1):
            a0, a1 = True, True
            while a0 or a1:
                if a0:
                    a0 = next(g0, "end") != "end"
                if a1:
                    a1 = next(g1, "end") != "end"

        for hp in range(HN // 2):
            fence(dma=False)
            ld_head(0, 2 * hp)
            ld_head(1, 2 * hp + 1)
            for c in range(NCH - 1, 0, -1):
                zipper(bwd_step(0, 2 * hp, c), bwd_step(1, 2 * hp + 1, c))
            P.wait("tensor", vval())
            P.wait("scalar", vval())
            for c in range(NCH):
                zipper(fwd_step(0, 2 * hp, c), fwd_step(1, 2 * hp + 1, c))
            vA = vval()
            for sl in (0, 1):
                h = 2 * hp + sl
                SL[(sl, "atfree")] = store(
                    s_attnfm[h * KH:(h + 1) * KH, :], att_acc[:, sl * T:(sl + 1) * T],
                    waits=[vA])
            for key in ("kfree", "qtfree", "dfree"):
                SL.pop((0, key), None)
                SL.pop((1, key), None)

    phase_a2()
    fence()
    if upto == "pa2":
        return _emit(nc, P, es, din, douts)

    P.scope("p_a3a")
    ow_bf = A2("ow_bf", (128, 8 * 1024), BF16)
    attn_t = A2("attn_t", (128, 8 * 128), BF16)
    xm_t = A2("xm_t", (128, 1024))
    z3_bf = A2("z3_bf", (128, 1024), BF16)
    xn0_t = A2("xn0_t", (128, 1024))
    stg_a3 = A2("stg_a3", (128, 1024), BF16)
    for kc in range(8):
        wld = load(f32stage[:, 0:1024], din["o_w"][128 * kc:128 * (kc + 1), :], q="scalar")
        P.emit("vector", lambda e, kc=kc: e.tensor_copy(
            ow_bf[:, 1024 * kc:1024 * (kc + 1)], f32stage[:, 0:1024]),
            waits=[wld], inc=sm["vector"])
        P.wait("scalar", vval())
    fence(dma=False)

    def phase_a3a():
        xm_st = None
        z3_st = None
        at_guard = [None]
        for tt in range(T // 128):
            t0 = tt * 128
            efence("tensor", "vector", "scalar", "sync")
            wld = load(xn0_t[:], s_xn0[t0:t0 + 128, :])
            wat = load(attn_t[:].rearrange("p (kc t) -> p kc t", t=128),
                       s_attnfm[:, t0:t0 + 128].rearrange("(kc p) t -> p kc t", p=128),
                       waits=[at_guard[0]] if at_guard[0] else ())
            for kc in range(8):
                for nb2 in range(2):
                    mm(psf[nb2][:, 0:512], nb2, attn_t[:, 128 * kc:128 * (kc + 1)],
                       ow_bf[:, 1024 * kc + 512 * nb2:1024 * kc + 512 * (nb2 + 1)],
                       kc == 0, kc == 7,
                       waits=[wat] if (kc == 0 and nb2 == 0) else ())
            at_guard[0] = tval()
            tv = tval()
            for nb2 in range(2):
                sl = slice(512 * nb2, 512 * (nb2 + 1))
                evac("vector", nb2, lambda e, ps=psf[nb2], sl=sl: e.tensor_add(
                    xm_t[:, sl], ps[:, 0:512], ob_bc[:, sl]),
                    waits=[tv, wld] + ([xm_st] if (nb2 == 0 and xm_st) else []))
                P.emit("vector", lambda e, sl=sl: e.tensor_add(
                    xm_t[:, sl], xm_t[:, sl], xn0_t[:, sl]), waits=[vval()],
                    inc=sm["vector"])
            vXM = vval()
            xm_st = store(s_xmid[t0:t0 + 128, :], xm_t[:], waits=[vXM])
            layer_norm(z3_bf[:], xm_t[:], 1024, waits=[vXM])
            vZ3 = vval()
            for kc in range(8):
                bid = 6 + kc % 2
                P.emit("tensor", lambda e, kc=kc: e.transpose(
                    psb[kc % 2][:, 0:128], z3_bf[:, 128 * kc:128 * (kc + 1)], ident_b[:]),
                    waits=[vZ3] + bfree(bid), inc=sm["tensor"])
                tv = tval()
                evac("vector", bid, lambda e, kc=kc: e.tensor_copy(
                    stg_a3[:, 128 * kc:128 * (kc + 1)], psb[kc % 2][:, 0:128]),
                    waits=[tv] + ([z3_st] if z3_st and kc == 0 else []))
            vv = vval()
            z3_st = store(s_z3fm.rearrange("(kc p) t -> p kc t", p=128)[:, :, t0:t0 + 128],
                          stg_a3[:, 0:1024].rearrange("p (kc j) -> p kc j", j=128),
                          waits=[vv])

    phase_a3a()
    es_a2.close()
    fence()
    if upto == "pa3a":
        return _emit(nc, P, es, din, douts)

    P.scope("p_a3b")
    es_f = ExitStack()
    A3 = lambda n, s, d=F32: sbR(es_f, n, s, d)
    fw_sb = A3("fw_sb", (128, 8 * FFH), BF16)
    zsp_fm = A3("zsp_fm", (128, 8 * 2048), BF16)
    h1stg = [A3(f"h1stg{i}", (128, 2048), BF16) for i in range(2)]
    h1_st = [None, None]
    wF1l = [load(fw_sb[:, FFH * kc:FFH * (kc + 1)], s_f1bf[128 * kc:128 * (kc + 1), :],
                 q="scalar") for kc in range(8)]
    P.wait("tensor", *wF1l)

    def phase_a3b():
        SPB = 2048
        for spn in range(T // SPB):
            t0 = spn * SPB
            efence("tensor", "vector", "scalar", "sync")
            wz = [load(zsp_fm[:, SPB * kc:SPB * (kc + 1)],
                       s_z3fm[128 * kc:128 * (kc + 1), t0:t0 + SPB]) for kc in range(8)]
            for mc in range(FFH // 128):
                for kc in range(8):
                    for nt in range(4):
                        mm(psf[nt][:, 0:512], nt,
                           fw_sb[:, FFH * kc + 128 * mc:FFH * kc + 128 * (mc + 1)],
                           zsp_fm[:, SPB * kc + 512 * nt:SPB * kc + 512 * (nt + 1)],
                           kc == 0, kc == 7,
                           waits=wz if (kc == 0 and mc == 0 and nt == 0) else ())
                tv = tval()
                sl = mc % 2
                for nt in range(4):
                    evac("scalar", nt, lambda e, nt=nt, sl=sl, mc=mc: e.activation(
                        h1stg[sl][:, 512 * nt:512 * (nt + 1)], psf[nt][:, 0:512],
                        AF.Gelu, bias=f1bc[:, mc:mc + 1]),
                        waits=[tv] + ([h1_st[sl]] if (nt == 0 and h1_st[sl]) else []))
                sv = sval()
                h1_st[sl] = store(s_h1fm[128 * mc:128 * (mc + 1), t0:t0 + SPB],
                                  h1stg[sl][:, 0:SPB], waits=[sv])

    phase_a3b()
    fence()

    P.scope("p_a3c")
    f2_sb = fw_sb
    h1t = A3("h1t", (128, 32 * 128), BF16)
    xin_t = A3("xin_t", (128, 1024))
    xm2_t = A3("xm2_t", (128, 1024))
    for kc in range(32):
        wld = load(f32stage[:, 0:1024], din["f2_w"][128 * kc:128 * (kc + 1), :], q="scalar")
        P.emit("vector", lambda e, kc=kc: e.tensor_copy(
            f2_sb[:, 1024 * kc:1024 * (kc + 1)], f32stage[:, 0:1024]), waits=[wld],
            inc=sm["vector"])
        P.wait("scalar", vval())
    fence(dma=False)

    def phase_a3c():
        x1_st = None
        for tt in range(T // 128):
            t0 = tt * 128
            efence("tensor", "vector", "scalar", "sync")
            wh = load(h1t[:], s_h1fm.rearrange("(kc p) t -> p kc t", p=128)[:, :, t0:t0 + 128])
            wx = load(xin_t[:], x_in[t0:t0 + 128, :])
            wxm = load(xm2_t[:], s_xmid[t0:t0 + 128, :],
                       waits=[x1_st] if x1_st else ())
            for kc in range(32):
                for nb2 in range(2):
                    mm(psf[nb2][:, 0:512], nb2, h1t[:, 128 * kc:128 * (kc + 1)],
                       f2_sb[:, 1024 * kc + 512 * nb2:1024 * kc + 512 * (nb2 + 1)],
                       kc == 0, kc == 31,
                       waits=[wh] if (kc == 0 and nb2 == 0) else ())
            tv = tval()
            for nb2 in range(2):
                sl = slice(512 * nb2, 512 * (nb2 + 1))
                evac("vector", nb2, lambda e, ps=psf[nb2], sl=sl: e.tensor_add(
                    xm2_t[:, sl], xm2_t[:, sl], ps[:, 0:512]),
                    waits=[tv, wxm] + ([x1_st] if (nb2 == 0 and x1_st) else []))
                P.emit("vector", lambda e, sl=sl: e.tensor_add(
                    xm2_t[:, sl], xm2_t[:, sl], f2b_bc[:, sl]), waits=[vval()])
                P.emit("vector", lambda e, sl=sl: e.tensor_add(
                    xm2_t[:, sl], xm2_t[:, sl], xin_t[:, sl]), waits=[wx, vval()],
                    inc=sm["vector"])
            vX1 = vval()
            x1_st = store(s_x1[t0:t0 + 128, :], xm2_t[:], waits=[vX1])

    phase_a3c()
    es_f.close()
    es_lion.close()
    fence()
    if upto == "pa3":
        return _emit(nc, P, es, din, douts)

    def do_ffn(src_d, dst_d, tag):
        es1 = ExitStack()
        es_gu = ExitStack()
        FF = lambda n, s, d=F32: sbR(es1, n + tag, s, d)
        GU = lambda n, s, d=F32: sbR(es_gu, n + tag, s, d)
        xsp = FF("xsp", (128, 16 * 1024))
        zsp = FF("zsp", (128, 2 * 1024), BF16)
        xg = [None] * 16
        ztg = [None, None]
        gstmp = FF("gs", (128, 2048))
        h2stg = [FF(f"h2stg{i}", (128, 2048), BF16) for i in range(2)]
        wgu_sb = GU("wgu", (128, 2 * 2 * 1024), BF16)
        zfm = GU("zfm", (128, 2 * 8 * 2048), BF16)
        h2_st = [None, None]
        h2_prev_v = None
        wgu_ld = [None, None]
        wgu_tg = [None, None]

        def ld_wgu(mc):
            sl2 = mc % 2
            o = 2048 * sl2
            w1 = load(wgu_sb[:, o:o + 1024].rearrange("p (kc m) -> p kc m", m=128),
                      s_wgT.rearrange("(kc p) m -> p kc m", p=128)[
                          :, :, 128 * (mc % 24):128 * (mc % 24) + 128],
                      q="scalar", waits=[wgu_tg[sl2]] if wgu_tg[sl2] else ())
            w2 = load(wgu_sb[:, o + 1024:o + 2048].rearrange("p (kc m) -> p kc m", m=128),
                      s_wuT.rearrange("(kc p) m -> p kc m", p=128)[
                          :, :, 128 * (mc % 24):128 * (mc % 24) + 128],
                      q="scalar")
            wgu_ld[sl2] = [w1, w2]

        ld_wgu(0)
        SPB = 2048
        NT = SPB // 128
        xl = [None] * NT
        vZF_s = [None, None]

        def ffn_ln_span(spn):
            t0 = spn * SPB
            zo = (spn % 2) * 8 * SPB
            for tt in range(NT):
                xl[tt] = load(xsp[:, 1024 * tt:1024 * (tt + 1)],
                              src_d[t0 + 128 * tt:t0 + 128 * (tt + 1), :],
                              waits=[xg[tt]] if xg[tt] else ())
            norm = layer_norm_span(
                lambda i: xsp[:, 1024 * i:1024 * (i + 1)],
                NT, 1024, lambda i: [xl[i]])
            for tt in range(NT):
                zs = tt % 2
                lnv = norm(tt, zsp[:, 1024 * zs:1024 * (zs + 1)],
                           waits=[ztg[zs]] if ztg[zs] else ())
                xg[tt] = lnv
                for kc in range(8):
                    bid = 6 + kc % 2
                    P.emit("tensor", lambda e, kc=kc, zs=zs: e.transpose(
                        psb[kc % 2][:, 0:128],
                        zsp[:, 1024 * zs + 128 * kc:1024 * zs + 128 * (kc + 1)],
                        ident_b[:]), waits=[lnv] + bfree(bid), inc=sm["tensor"])
                    tv = tval()
                    evac("vector", bid, lambda e, kc=kc, tt=tt, zo=zo: e.tensor_copy(
                        zfm[:, zo + SPB * kc + 128 * tt:zo + SPB * kc + 128 * (tt + 1)],
                        psb[kc % 2][:, 0:128]), waits=[tv])
                ztg[zs] = tval()
            vZF_s[spn % 2] = vval()

        ffn_ln_span(0)
        for spn in range(T // SPB):
            t0 = spn * SPB
            zo = (spn % 2) * 8 * SPB
            vZF = vZF_s[spn % 2]
            if spn + 1 < T // SPB:
                ffn_ln_span(spn + 1)
            for mc in range(FFI // 128):
                sl2 = mc % 2
                o2 = 2048 * sl2
                wld2 = wgu_ld[sl2]
                if spn * 24 + mc + 1 < 48:
                    ld_wgu(mc + 1)
                for kc in range(8):
                    for nt in range(4):
                        mm(psf[nt][:, 0:512], nt,
                           wgu_sb[:, o2 + 128 * kc:o2 + 128 * (kc + 1)],
                           zfm[:, zo + SPB * kc + 512 * nt:zo + SPB * kc + 512 * (nt + 1)],
                           kc == 0, kc == 7,
                           waits=([vZF] + wld2) if (kc == 0 and nt == 0) else ())
                for nt in range(4):
                    evac("scalar", nt, lambda e, nt=nt: e.activation(
                        gstmp[:, 512 * nt:512 * (nt + 1)], psf[nt][:, 0:512],
                        AF.Silu, scale=QS),
                        waits=[tval()] + ([h2_prev_v] if h2_prev_v else []))
                svG = sval()
                for kc in range(8):
                    for nt in range(4):
                        mm(psf[nt][:, 0:512], nt,
                           wgu_sb[:, o2 + 1024 + 128 * kc:o2 + 1024 + 128 * (kc + 1)],
                           zfm[:, zo + SPB * kc + 512 * nt:zo + SPB * kc + 512 * (nt + 1)],
                           kc == 0, kc == 7)
                tv = tval()
                wgu_tg[sl2] = tv
                sl = mc % 2
                for nt in range(4):
                    evac("vector", nt, lambda e, nt=nt, sl=sl: e.tensor_tensor(
                        h2stg[sl][:, 512 * nt:512 * (nt + 1)],
                        gstmp[:, 512 * nt:512 * (nt + 1)], psf[nt][:, 0:512],
                        op=ALU.mult),
                        waits=[tv, svG] + ([h2_st[sl]] if (nt == 0 and h2_st[sl])
                                           else []))
                vv = vval()
                h2_prev_v = vv
                h2_st[sl] = store(s_h2fm[128 * mc:128 * (mc + 1), t0:t0 + SPB],
                                  h2stg[sl][:, 0:SPB], waits=[vv])
        efence("tensor", "vector", "scalar", "sync")
        es_gu.close()
        wd_sb = FF("wd", (128, 24 * 1024), BF16)
        h2t = FF("h2t", (128, 24 * 128), BF16)
        xdn = FF("xdn", (128, 1024))
        wd_l = [load(wd_sb[:, 1024 * kc:1024 * (kc + 1)],
                     s_wdT[128 * kc:128 * (kc + 1), :], q="scalar")
                for kc in range(24)]
        P.wait("tensor", *wd_l)
        s2c = QS * QS
        xo_st = None
        for tt in range(T // 128):
            t0 = tt * 128
            efence("tensor", "vector", "scalar", "sync")
            wh = load(h2t[:], s_h2fm.rearrange("(kc p) t -> p kc t", p=128)[:, :, t0:t0 + 128])
            wx = load(xdn[:], src_d[t0:t0 + 128, :],
                      waits=[xo_st] if xo_st else ())
            for kc in range(24):
                for nb2 in range(2):
                    mm(psf[4 + nb2][:, 0:512], 4 + nb2, h2t[:, 128 * kc:128 * (kc + 1)],
                       wd_sb[:, 1024 * kc + 512 * nb2:1024 * kc + 512 * (nb2 + 1)],
                       kc == 0, kc == 23, waits=[wh] if (kc == 0 and nb2 == 0) else ())
            tv = tval()
            for nb2 in range(2):
                sl = slice(512 * nb2, 512 * (nb2 + 1))
                evac("vector", 4 + nb2, lambda e, ps=psf[4 + nb2], sl=sl:
                     e.scalar_tensor_tensor(
                         xdn[:, sl], ps[:, 0:512], s2c, xdn[:, sl],
                         op0=ALU.mult, op1=ALU.add),
                    waits=[tv, wx] + ([xo_st] if (nb2 == 0 and xo_st) else []))
            vXO = vval()
            xo_st = store(dst_d[t0:t0 + 128, :], xdn[:], waits=[vXO])
        es1.close()
        fence()

    P.scope("p_f1")
    do_ffn(s_x1, s_x2, "f1")
    if upto == "pf1":
        return _emit(nc, P, es, din, douts)

    P.scope("p_b1")
    es_b1 = ExitStack()
    BB = lambda n, s, d=F32: sbR(es_b1, n, s, d)
    SPB1 = 512
    win_full = BB("win_full", (128, 8 * 4096), BF16)
    zb_fm = BB("zb_fm", (128, 2 * 8 * SPB1), BF16)
    xb_sp = BB("xb_sp", (128, 4 * 1024))
    zb_sp = BB("zb_sp", (128, 2 * 1024), BF16)
    inner_sb = BB("inner_sb", (128, 16 * SPB1))
    carry2 = BB("carry2", (128, 16 * 2))
    hconv = BB("hconv", (128, 2 * SPB1))
    hconv_bf = BB("hconv_bf", (128, 2 * SPB1), BF16)
    sg_stg = BB("sg_stg", (128, SPB1))
    win_l = [load(win_full[:, 4096 * kc:4096 * (kc + 1)],
                  s_winbf[128 * kc:128 * (kc + 1), :], q="scalar")
             for kc in range(8)]
    P.wait("tensor", *win_l)
    P.emit("vector", lambda e: e.memset(carry2[:], 0.0), inc=sm["vector"])

    def phase_b1():
        SPB = SPB1
        sg_done = None
        xg_b = [None] * 4
        ztg_b = [None, None]
        zfm_g = [None, None]
        vZF_b = [None, None]
        hc_g = [[], []]
        conv_g = [None]

        def b1_ln_span(spn):
            t0 = spn * SPB
            zo = (spn % 2) * 8 * SPB
            xl = [None] * 4
            for tt in range(4):
                xl[tt] = load(xb_sp[:, 1024 * tt:1024 * (tt + 1)],
                              s_x2[t0 + 128 * tt:t0 + 128 * (tt + 1), :],
                              waits=[xg_b[tt]] if xg_b[tt] else ())
            norm = layer_norm_span(
                lambda i: xb_sp[:, 1024 * i:1024 * (i + 1)],
                4, 1024, lambda i: [xl[i]])
            for tt in range(4):
                zs = tt % 2
                lnv = norm(tt, zb_sp[:, 1024 * zs:1024 * (zs + 1)],
                           waits=[ztg_b[zs]] if ztg_b[zs] else ())
                xg_b[tt] = lnv
                for kc in range(8):
                    bid = 6 + kc % 2
                    P.emit("tensor", lambda e, kc=kc, zs=zs: e.transpose(
                        psb[kc % 2][:, 0:128],
                        zb_sp[:, 1024 * zs + 128 * kc:1024 * zs + 128 * (kc + 1)],
                        ident_b[:]), waits=[lnv] + bfree(bid), inc=sm["tensor"])
                    tv = tval()
                    evac("vector", bid, lambda e, kc=kc, tt=tt, zo=zo: e.tensor_copy(
                        zb_fm[:, zo + SPB * kc + 128 * tt:zo + SPB * kc + 128 * (tt + 1)],
                        psb[kc % 2][:, 0:128]),
                        waits=[tv] + ([zfm_g[spn % 2]] if (tt == 0 and kc == 0
                                                           and zfm_g[spn % 2]) else []))
                ztg_b[zs] = tval()
            vZF_b[spn % 2] = vval()

        b1_ln_span(0)
        for spn in range(T // SPB):
            t0 = spn * SPB
            zo = (spn % 2) * 8 * SPB
            vZF = vZF_b[spn % 2]
            for mc in range(32):
                bid = mc % 4
                ps = psf[bid]
                for kc in range(8):
                    mm(ps[:, 0:512], bid,
                       win_full[:, 4096 * kc + 128 * mc:4096 * kc + 128 * (mc + 1)],
                       zb_fm[:, zo + SPB * kc:zo + SPB * (kc + 1)],
                       kc == 0, kc == 7, waits=[vZF] if kc == 0 else ())
                tv = tval()
                if mc < 16:
                    evac("vector", bid, lambda e, mc=mc, ps=ps: e.tensor_copy(
                        inner_sb[:, SPB * mc:SPB * (mc + 1)], ps[:, 0:512]),
                        waits=[tv] + ([conv_g[0]] if (mc == 0 and conv_g[0]) else []))
                else:
                    cc = mc - 16
                    evac("scalar", bid, lambda e, ps=ps: e.activation(
                        sg_stg[:, 0:SPB], ps[:, 0:512], AF.Silu),
                        waits=[tv] + ([sg_done] if sg_done else []))
                    sv = sval()
                    sg_done = store(s_sg[128 * cc:128 * (cc + 1), t0:t0 + SPB],
                                    sg_stg[:, 0:SPB], waits=[sv])
            zfm_g[spn % 2] = tval()
            vIN = vval()
            if spn + 1 < T // SPB:
                b1_ln_span(spn + 1)
            for cc in range(16):
                isl = inner_sb[:, SPB * cc:SPB * (cc + 1)]
                sl = cc % 2
                ho = SPB * sl
                wv = ([vIN] if cc < 2 else []) + hc_g[sl]
                P.emit("vector", lambda e, cc=cc, ho=ho: e.tensor_scalar(
                    hconv[:, ho:ho + 1], carry2[:, 2 * cc:2 * cc + 1], cw0c[:, cc:cc + 1],
                    None, op0=ALU.mult), waits=wv)
                P.emit("vector", lambda e, cc=cc, ho=ho: e.scalar_tensor_tensor(
                    hconv[:, ho:ho + 1], carry2[:, 2 * cc + 1:2 * cc + 2], cw1c[:, cc:cc + 1],
                    hconv[:, ho:ho + 1], op0=ALU.mult, op1=ALU.add), waits=[vval()])
                P.emit("vector", lambda e, cc=cc, isl=isl, ho=ho: e.scalar_tensor_tensor(
                    hconv[:, ho:ho + 1], isl[:, 0:1], cw2c[:, cc:cc + 1], hconv[:, ho:ho + 1],
                    op0=ALU.mult, op1=ALU.add), waits=[vval()])
                P.emit("vector", lambda e, cc=cc, ho=ho: e.tensor_scalar(
                    hconv[:, ho + 1:ho + 2], carry2[:, 2 * cc + 1:2 * cc + 2], cw0c[:, cc:cc + 1],
                    None, op0=ALU.mult), waits=[vval()])
                P.emit("vector", lambda e, cc=cc, isl=isl, ho=ho: e.scalar_tensor_tensor(
                    hconv[:, ho + 1:ho + 2], isl[:, 0:1], cw1c[:, cc:cc + 1], hconv[:, ho + 1:ho + 2],
                    op0=ALU.mult, op1=ALU.add), waits=[vval()])
                P.emit("vector", lambda e, cc=cc, isl=isl, ho=ho: e.scalar_tensor_tensor(
                    hconv[:, ho + 1:ho + 2], isl[:, 1:2], cw2c[:, cc:cc + 1], hconv[:, ho + 1:ho + 2],
                    op0=ALU.mult, op1=ALU.add), waits=[vval()])
                Wc = SPB - 2
                P.emit("vector", lambda e, cc=cc, isl=isl, Wc=Wc, ho=ho: e.tensor_scalar(
                    hconv[:, ho + 2:ho + 2 + Wc], isl[:, 0:Wc], cw0c[:, cc:cc + 1], None,
                    op0=ALU.mult), waits=[vval()])
                P.emit("vector", lambda e, cc=cc, isl=isl, Wc=Wc, ho=ho: e.scalar_tensor_tensor(
                    hconv[:, ho + 2:ho + 2 + Wc], isl[:, 1:1 + Wc], cw1c[:, cc:cc + 1],
                    hconv[:, ho + 2:ho + 2 + Wc], op0=ALU.mult, op1=ALU.add), waits=[vval()])
                P.emit("vector", lambda e, cc=cc, isl=isl, Wc=Wc, ho=ho: e.scalar_tensor_tensor(
                    hconv[:, ho + 2:ho + 2 + Wc], isl[:, 2:2 + Wc], cw2c[:, cc:cc + 1],
                    hconv[:, ho + 2:ho + 2 + Wc], op0=ALU.mult, op1=ALU.add), waits=[vval()])
                P.emit("vector", lambda e, cc=cc, isl=isl: e.tensor_copy(
                    carry2[:, 2 * cc:2 * cc + 2], isl[:, SPB - 2:SPB]), inc=sm["vector"])
                vC = vval()
                P.emit("scalar", lambda e, cc=cc, ho=ho: e.activation(
                    hconv[:, ho:ho + SPB], hconv[:, ho:ho + SPB], AF.Silu,
                    bias=cbc[:, cc:cc + 1]), waits=[vC], inc=sm["scalar"])
                svH = sval()
                P.emit("vector", lambda e, ho=ho: e.tensor_copy(
                    hconv_bf[:, ho:ho + SPB], hconv[:, ho:ho + SPB]),
                    waits=[svH], inc=sm["vector"])
                vHB = vval()
                lo = t0 - 1
                if spn == 0:
                    h1s = store(s_hfm[128 * cc:128 * (cc + 1), 0:SPB - 1],
                                hconv[:, ho + 1:ho + SPB], waits=[svH])
                    h2s = store(s_hfm_bf[128 * cc:128 * (cc + 1), 0:SPB - 1],
                                hconv_bf[:, ho + 1:ho + SPB], waits=[vHB])
                else:
                    h1s = store(s_hfm[128 * cc:128 * (cc + 1), lo:lo + SPB],
                                hconv[:, ho:ho + SPB], waits=[svH])
                    h2s = store(s_hfm_bf[128 * cc:128 * (cc + 1), lo:lo + SPB],
                                hconv_bf[:, ho:ho + SPB], waits=[vHB])
                hc_g[sl] = [vHB, h1s, h2s]
            conv_g[0] = vval()
        fence(dma=True)
        hguard = [0]
        fc_hst = []
        for cc in range(16):
            P.emit("vector", lambda e, cc=cc: e.tensor_scalar(
                hconv[:, 0:1], carry2[:, 2 * cc:2 * cc + 1], cw0c[:, cc:cc + 1], None,
                op0=ALU.mult), waits=[sval(), (sm["vector"], hguard[0])] + fc_hst)
            P.emit("vector", lambda e, cc=cc: e.scalar_tensor_tensor(
                hconv[:, 0:1], carry2[:, 2 * cc + 1:2 * cc + 2], cw1c[:, cc:cc + 1],
                hconv[:, 0:1], op0=ALU.mult, op1=ALU.add), waits=[vval()],
                inc=sm["vector"])
            vC = vval()
            P.emit("scalar", lambda e, cc=cc: e.activation(
                hconv[:, 0:1], hconv[:, 0:1], AF.Silu, bias=cbc[:, cc:cc + 1]),
                waits=[vC], inc=sm["scalar"])
            svH = sval()
            P.emit("vector", lambda e: e.tensor_copy(hconv_bf[:, 0:1], hconv[:, 0:1]),
                   waits=[svH], inc=sm["vector"])
            vHB = vval()
            hguard[0] = vHB[1]
            f1s = store(s_hfm[128 * cc:128 * (cc + 1), T - 1:T], hconv[:, 0:1],
                        waits=[svH], slow=True)
            f2s = store(s_hfm_bf[128 * cc:128 * (cc + 1), T - 1:T], hconv_bf[:, 0:1],
                        waits=[vHB], slow=True)
            fc_hst = [f1s, f2s]

    phase_b1()
    fence()
    es_b1.close()
    if upto == "pb1":
        return _emit(nc, P, es, din, douts)

    P.scope("p_b23")
    es_b2 = ExitStack()
    B2 = lambda n, s, d=F32: sbR(es_b2, n, s, d)
    wt_tile = B2("wt_tile", (128, 2048), BF16)
    wst_sb = B2("wst_sb", (128, 16 * 32), BF16)
    wout_sb = B2("wout_sb", (128, 16 * 512), BF16)
    hbf_sp = B2("hbf_sp", (128, 16 * 512), BF16)
    dt_sb = B2("dt_sb", (128, 8 * 512))
    bbc_sb = B2("bbc_sb", (128, 16 * 512))
    sp_sb = B2("sp_sb", (32, 512))
    bbrows = B2("bbrows", (1, 2 * 512))
    hf_sp = B2("hf_sp", (128, 2 * 512))
    sg_sp = B2("sg_sp", (128, 2 * 512))
    g_sb = B2("g_sb", (128, 2 * 512))
    y_sb = B2("y_sb", (128, 2 * 512))
    dA_sb = B2("dA_sb", (128, 2 * 1026))
    gB_sb = B2("gB_sb", (128, 2 * 1026))
    mem_sb = B2("mem_sb", (128, 2 * 1026))
    x2t = B2("x2t", (128, 512))
    phtw = B2("phtw", (128, 4 * 512))
    scarry = B2("scarry", (128, 256))
    yg_bf = B2("yg_bf", (128, 16 * 512), BF16)
    wws = load(f32stage[:, 0:512].rearrange("p (kc s) -> p kc s", s=32),
               din["W_state"].rearrange("(kc p) s -> p kc s", p=128))
    P.emit("vector", lambda e: e.tensor_copy(wst_sb[:], f32stage[:, 0:512]), waits=[wws],
           inc=sm["vector"])
    P.emit("vector", lambda e: e.memset(scarry[:], 0.0), inc=sm["vector"])
    for off in (512, 1025, 1026 + 512, 1026 + 1025):
        P.emit("vector", lambda e, off=off: e.memset(dA_sb[:, off:off + 1], 0.0),
               inc=sm["vector"])
    for off in (1025, 1026 + 1025):
        P.emit("vector", lambda e, off=off: e.memset(gB_sb[:, off:off + 1], 0.0),
               inc=sm["vector"])
    P.wait("tensor", vval())

    def phase_b23():
        x3_st = None
        vx2guard = [None]
        last_scan = [None, None]
        last_scan2 = [None, None]
        last_acc = [None, None]
        last_carry = [None, None]
        last_gB = [None, None]
        last_yg = [None, None]
        vG = [None, None]
        yinit = [None, None]
        hfree = [None, None]
        sgfree = [None, None]
        carry_ready = {}
        bb_t = [0, 0]
        wsg_c = [None, None]
        wt_guard = [None]
        dt_ready = [None] * 16
        yg_guard = [None]
        dtadd_g = [None, None]
        dt_free = {}
        hbf_guard = [None]
        for spn in range(T // 512):
            t0 = spn * 512
            whb = load(hbf_sp[:],
                       s_hfm_bf.rearrange("(kc p) t -> p kc t", p=128)[:, :, t0:t0 + 512],
                       waits=[hbf_guard[0]] if hbf_guard[0] else ())
            for kc in range(16):
                mm(psf[4][0:32, 0:512], 4, wst_sb[:, 32 * kc:32 * (kc + 1)],
                   hbf_sp[:, 512 * kc:512 * (kc + 1)], kc == 0, kc == 15,
                   waits=[whb] if kc == 0 else ())
            tv = tval()
            evac("vector", 4, lambda e: e.tensor_copy(sp_sb[:], psf[4][0:32, 0:512]),
                 waits=[tv])
            vSP = vval()
            bb_tv = [None] * 16
            for s_i in range(16):
                bsl2 = bbrows[0:1, 512 * (s_i % 2):512 * (s_i % 2 + 1)]
                wBR = load(bsl2, sp_sb[16 + s_i:17 + s_i, :],
                           waits=[vSP] + ([(sm["tensor"], bb_t[s_i % 2])]
                                          if bb_t[s_i % 2] else []))
                bid5 = 4 + (s_i % 2)
                mm(psf[bid5][:, 0:512], bid5, ones_row[:], bsl2, True, True,
                   waits=[wBR])
                bb_t[s_i % 2] = sm["tensor"].n
                bb_tv[s_i] = tval()
                if s_i >= 1:
                    j = s_i - 1
                    evac("vector", 4 + (j % 2), lambda e, j=j: e.tensor_copy(
                        bbc_sb[:, 512 * j:512 * (j + 1)],
                        psf[4 + (j % 2)][:, 0:512]), waits=[bb_tv[j]])
            evac("vector", 5, lambda e: e.tensor_copy(
                bbc_sb[:, 512 * 15:512 * 16], psf[5][:, 0:512]), waits=[bb_tv[15]])
            vBB = vval()

            wt_sv = [None] * 16
            wt_tv = [None] * 16

            def emit_wtmm(cc):
                wt_ld = load(wt_tile[:], s_wtt[:, 2048 * cc:2048 * (cc + 1)], q="scalar",
                             waits=[wt_guard[0]] if wt_guard[0] else ())
                bid = cc % 4
                ps = psf[bid]
                for kc in range(16):
                    mm(ps[:, 0:512], bid,
                       wt_tile[:, 128 * kc:128 * (kc + 1)],
                       hbf_sp[:, 512 * kc:512 * (kc + 1)], kc == 0, kc == 15,
                       waits=[whb, wt_ld] if kc == 0 else ())
                wt_guard[0] = tval()
                if cc == 15:
                    hbf_guard[0] = tval()
                wt_tv[cc] = tval()

            def emit_wtact(cc):
                bid = cc % 4
                ps = psf[bid]
                po = 1024 * (cc % 2)
                P.emit("scalar", lambda e, cc=cc, ps=ps, po=po: e.activation(
                    phtw[:, po:po + 512], ps[:, 0:512], AF.Abs,
                    bias=btimec[:, cc:cc + 1]),
                    waits=[wt_tv[cc]] + ([dtadd_g[cc % 2]] if dtadd_g[cc % 2] else []),
                    inc=sm["scalar"])
                P.emit("scalar", lambda e, po=po: e.activation(
                    phtw[:, po:po + 512], phtw[:, po:po + 512], AF.Exp, scale=-1.0),
                    waits=[sval()])
                P.emit("scalar", lambda e, po=po: e.activation(
                    phtw[:, po:po + 512], phtw[:, po:po + 512], AF.Ln, bias=1.0),
                    waits=[sval()])
                evac("scalar", bid, lambda e, cc=cc, ps=ps, po=po: e.activation(
                    phtw[:, po + 512:po + 1024], ps[:, 0:512], AF.Relu,
                    bias=btimec[:, cc:cc + 1]))
                wt_sv[cc] = sval()

            def emit_wtadd(cc):
                po = 1024 * (cc % 2)
                sl8 = cc % 8
                P.emit("vector", lambda e, sl8=sl8, po=po: e.tensor_add(
                    dt_sb[:, 512 * sl8:512 * (sl8 + 1)], phtw[:, po:po + 512],
                    phtw[:, po + 512:po + 1024]),
                    waits=[wt_sv[cc]] + (dt_free.get((cc + 8) % 16) or []),
                    inc=sm["vector"])
                dt_ready[cc] = vval()
                dtadd_g[cc % 2] = dt_ready[cc]

            for cc4 in range(4):
                emit_wtmm(cc4)
            emit_wtact(0)
            emit_wtact(1)
            emit_wtadd(0)
            emit_wtadd(1)
            emit_wtact(2)
            emit_wtact(3)
            for k in range(8):
                if 2 * k + 4 < 16:
                    emit_wtmm(2 * k + 4)
                if 2 * k + 5 < 16:
                    emit_wtmm(2 * k + 5)
                for ch in (0, 1):
                    cc = 2 * k + ch
                    hsl = hf_sp[:, 512 * ch:512 * (ch + 1)]
                    sgsl = sg_sp[:, 512 * ch:512 * (ch + 1)]
                    gsl = g_sb[:, 512 * ch:512 * (ch + 1)]
                    ysl = y_sb[:, 512 * ch:512 * (ch + 1)]
                    whf = load(hsl, s_hfm[128 * cc:128 * (cc + 1), t0:t0 + 512],
                               waits=hfree[ch] or ())
                    wsg_c[ch] = load(sgsl, s_sg[128 * cc:128 * (cc + 1), t0:t0 + 512],
                                     waits=[sgfree[ch]] if sgfree[ch] else ())
                    P.emit("vector", lambda e, sl8=cc % 8, hsl=hsl, gsl=gsl:
                           e.tensor_tensor(
                        gsl, dt_sb[:, 512 * sl8:512 * (sl8 + 1)], hsl, op=ALU.mult),
                        waits=[whf, dt_ready[cc]] + ([last_gB[ch]] if last_gB[ch] else []),
                        inc=sm["vector"])
                    vG[ch] = vval()
                    P.emit("scalar", lambda e, cc=cc, hsl=hsl, ysl=ysl: e.activation(
                        ysl, hsl, AF.Copy, scale=skipc[:, cc:cc + 1]),
                        waits=[whf] + ([last_yg[ch]] if last_yg[ch] else []),
                        inc=sm["scalar"])
                    yinit[ch] = sval()
                if 2 * k + 2 < 16:
                    emit_wtadd(2 * k + 2)
                if 2 * k + 3 < 16:
                    emit_wtadd(2 * k + 3)
                for s_i in range(0, 16, 2):
                    CH = (0, 1)
                    fi_c = [2 * k * 16 + s_i, (2 * k + 1) * 16 + s_i]
                    dsl_c = [dA_sb[:, 1026 * ch:1026 * (ch + 1)] for ch in CH]
                    bsl_c = [gB_sb[:, 1026 * ch:1026 * (ch + 1)] for ch in CH]
                    msl_c = [mem_sb[:, 1026 * ch:1026 * (ch + 1)] for ch in CH]
                    ysl_c = [y_sb[:, 512 * ch:512 * (ch + 1)] for ch in CH]
                    svA_c = [None, None]
                    vgB_c = [None, None]
                    for ch in CH:
                        cc = 2 * k + ch
                        dsl = dsl_c[ch]
                        w_s = [dt_ready[cc]] + ([last_scan[ch]] if last_scan[ch] else [])
                        P.emit("scalar", lambda e, cc=cc, s_i=s_i, dsl=dsl,
                               sl8=cc % 8: e.activation(
                            dsl[:, 0:512], dt_sb[:, 512 * sl8:512 * (sl8 + 1)], AF.Exp,
                            scale=A_sb[:, 16 * cc + s_i:16 * cc + s_i + 1]),
                            waits=w_s, inc=sm["scalar"])
                        P.emit("scalar", lambda e, cc=cc, s_i=s_i, dsl=dsl,
                               sl8=cc % 8: e.activation(
                            dsl[:, 513:1025], dt_sb[:, 512 * sl8:512 * (sl8 + 1)], AF.Exp,
                            scale=A_sb[:, 16 * cc + s_i + 1:16 * cc + s_i + 2]),
                            inc=sm["scalar"])
                        svA_c[ch] = sval()
                    for ch in CH:
                        fi = fi_c[ch]
                        bsl = bsl_c[ch]
                        P.emit("vector", lambda e, fi=fi, bsl=bsl: e.tensor_copy(
                            bsl[:, 512:513], scarry[:, fi + 1:fi + 2]),
                            waits=([carry_ready[fi + 1]] if fi + 1 in carry_ready
                                   else []) +
                                  ([last_scan[ch]] if last_scan[ch] else []),
                            inc=sm["vector"])
                    for ch in CH:
                        gsl = g_sb[:, 512 * ch:512 * (ch + 1)]
                        bsl = bsl_c[ch]
                        grep2 = gsl.rearrange("p (one t) -> p one t", one=1
                                              ).broadcast_to((128, 2, 512))
                        w_g = [vG[ch], vBB] + ([last_scan[ch]] if last_scan[ch] else [])
                        P.emit("vector", lambda e, s_i=s_i, bsl=bsl, grep2=grep2:
                               e.tensor_tensor(
                                   bsl[:, 0:1026].rearrange("p (r t) -> p r t",
                                                            t=513)[:, :, 0:512],
                                   grep2,
                                   bbc_sb[:, 512 * s_i:512 * (s_i + 2)].rearrange(
                                       "p (r t) -> p r t", t=512),
                                   op=ALU.mult),
                               waits=w_g, inc=sm["vector"])
                        vgB_c[ch] = vval()
                    for ch in CH:
                        fi = fi_c[ch]
                        w_sc = [svA_c[ch], vgB_c[ch]]
                        if last_acc[ch]:
                            w_sc.append(last_acc[ch])
                        if last_carry[ch]:
                            w_sc.append(last_carry[ch])
                        if fi in carry_ready:
                            w_sc.append(carry_ready[fi])
                        P.emit("vector", lambda e, fi=fi, dsl=dsl_c[ch], bsl=bsl_c[ch],
                               msl=msl_c[ch]: e.tensor_tensor_scan(
                                   msl, dsl, bsl, scarry[:, fi:fi + 1],
                                   op0=ALU.mult, op1=ALU.add),
                               waits=w_sc, inc=sm["vector"])
                        last_scan2[ch] = last_scan[ch]
                        last_scan[ch] = vval()
                    ya1 = [None, None]
                    for ch in CH:
                        fi = fi_c[ch]
                        msl = msl_c[ch]
                        P.emit("vector", lambda e, fi=fi, msl=msl: e.tensor_copy(
                            scarry[:, fi:fi + 2].rearrange("p (r one) -> p r one", one=1),
                            msl[:, 0:1026].rearrange("p (r t) -> p r t",
                                                     t=513)[:, :, 511:512]),
                            waits=[last_scan[ch]], inc=sm["vector"])
                        last_carry[ch] = vval()
                        carry_ready[fi] = last_carry[ch]
                        carry_ready[fi + 1] = last_carry[ch]
                        P.emit("vector", lambda e, msl=msl, ysl=ysl_c[ch]: e.tensor_tensor(
                            ysl, ysl, msl[:, 0:512], op=ALU.add),
                            waits=[last_scan[ch]], inc=sm["vector"])
                        ya1[ch] = vval()
                    for ch in CH:
                        P.emit("vector", lambda e, msl=msl_c[ch], ysl=ysl_c[ch]:
                               e.tensor_tensor(ysl, ysl, msl[:, 513:1025], op=ALU.add),
                               waits=[ya1[ch]], inc=sm["vector"])
                        last_acc[ch] = vval()
                for ch in (0, 1):
                    cc = 2 * k + ch
                    dt_free[cc] = [sval(), vG[ch]]
                    sgsl = sg_sp[:, 512 * ch:512 * (ch + 1)]
                    ysl = y_sb[:, 512 * ch:512 * (ch + 1)]
                    P.emit("vector", lambda e, cc=cc, sgsl=sgsl, ysl=ysl: e.tensor_tensor(
                        yg_bf[:, 512 * cc:512 * (cc + 1)], ysl, sgsl, op=ALU.mult),
                        waits=[last_acc[ch], wsg_c[ch], yg_guard[0]] if yg_guard[0] else
                              [last_acc[ch], wsg_c[ch]], inc=sm["vector"])
                    last_yg[ch] = vval()
                    hfree[ch] = [vG[ch], yinit[ch]]
                    sgfree[ch] = last_yg[ch]
                if 2 * k + 4 < 16:
                    emit_wtact(2 * k + 4)
                if 2 * k + 5 < 16:
                    emit_wtact(2 * k + 5)
            vYG = vval()
            for nb2 in range(2):
                sl = slice(512 * nb2, 512 * (nb2 + 1))
                wo_ld = [load(wout_sb[:, 512 * kc:512 * (kc + 1)],
                              s_woutbf[128 * kc:128 * (kc + 1),
                                       512 * nb2:512 * (nb2 + 1)],
                              q="scalar", waits=[tval()] if kc == 0 else ())
                         for kc in range(16)]
                for tt in range(4):
                    wx2 = load(x2t[:], s_x2[t0 + 128 * tt:t0 + 128 * (tt + 1), sl],
                               waits=[vx2guard[0]] if vx2guard[0] else ())
                    bid = tt % 2
                    ps = psf[bid]
                    for kc in range(16):
                        mm(ps[:, 0:512], bid,
                           yg_bf[:, 512 * kc + 128 * tt:512 * kc + 128 * (tt + 1)],
                           wout_sb[:, 512 * kc:512 * (kc + 1)],
                           kc == 0, kc == 15,
                           waits=[vYG] + wo_ld if (kc == 0 and tt == 0) else ())
                    tv = tval()
                    evac("vector", bid, lambda e, ps=ps: e.tensor_add(
                        x2t[:], ps[:, 0:512], x2t[:]),
                        waits=[tv, wx2] + ([x3_st] if x3_st else []))
                    vX3 = vval()
                    x3_st = store(s_x3[t0 + 128 * tt:t0 + 128 * (tt + 1), sl],
                                  x2t[:], waits=[vX3])
                    vx2guard[0] = x3_st
            yg_guard[0] = tval()

    phase_b23()
    es_b2.close()
    fence()
    if upto == "pb23":
        return _emit(nc, P, es, din, douts)

    P.scope("p_f2")
    do_ffn(s_x3, out_d, "f2")

    return _emit(nc, P, es, din, douts)


def _emit(nc, P, es, din, douts):
    P.wait("sync", *[(s, s.n) for nm, s in P.sems.items()
                     if nm.startswith("st_") and s.n > 0])
    block_cm = nc.Block()
    block = block_cm.__enter__()
    for eng in ENGINES:
        deco = getattr(block, eng)

        def mk(eng):
            def body(e):
                cur = [None]
                for fn in P.ops[eng]:
                    if isinstance(fn, tuple):
                        if cur[0] is not None:
                            cur[0].__exit__(None, None, None)
                        cur[0] = nc.named_scope(fn[1])
                        cur[0].__enter__()
                    else:
                        fn(e)
                if cur[0] is not None:
                    cur[0].__exit__(None, None, None)
            return body

        deco(mk(eng))
    block_cm.__exit__(None, None, None)
    es.close()
    return nc, list(din.keys()), list(douts.keys())


_CACHE = {}


def get_program(upto="all", dbg=()):
    key = (upto, tuple(sorted(dbg)))
    if key not in _CACHE:
        _CACHE[key] = build_core_program(upto, dbg)
    return _CACHE[key]


_REN = {
    "q_w": "L0_attn_a_q_w", "q_b": "L0_attn_a_q_b", "k_w": "L0_attn_a_k_w",
    "k_b": "L0_attn_a_k_b", "v_w": "L0_attn_a_v_w", "v_b": "L0_attn_a_v_b",
    "o_w": "L0_attn_a_o_w", "o_b": "L0_attn_a_o_b", "ln1_w": "L0_attn_a_ln1_w",
    "ln1_b": "L0_attn_a_ln1_b", "ln2_w": "L0_attn_a_ln2_w", "ln2_b": "L0_attn_a_ln2_b",
    "dec_w": "L0_attn_a_dec_w", "dec_b": "L0_attn_a_dec_b", "f1_w": "L0_attn_a_f1_w",
    "f1_b": "L0_attn_a_f1_b", "f2_w": "L0_attn_a_f2_w", "f2_b": "L0_attn_a_f2_b",
    "W_in": "L0_attn_b_W_in", "conv_w": "L0_attn_b_conv_w", "conv_b": "L0_attn_b_conv_b",
    "W_state": "L0_attn_b_W_state", "W_time": "L0_attn_b_W_time",
    "b_time": "L0_attn_b_b_time", "A_log": "L0_attn_b_A_log", "skip": "L0_attn_b_skip",
    "W_out": "L0_attn_b_W_out", "w_gate": "L0_ffn_w_gate", "w_up": "L0_ffn_w_up",
    "w_down": "L0_ffn_w_down",
}


def make_in_maps(inputs):
    consts = host_constants()
    x = np.ascontiguousarray(np.asarray(inputs["L0_attn_a_x"], np.float32))
    w_map = {}
    for k, src in _REN.items():
        w_map[k] = np.ascontiguousarray(np.asarray(inputs[src]))
    for k, v in consts.items():
        w_map[k] = v
    in_maps = []
    for b in range(8):
        m = dict(w_map)
        m["x"] = x[b]
        in_maps.append(m)
    return in_maps


def kernel(**inputs):
    nc, in_names, out_names = get_program()
    in_maps = make_in_maps(inputs)
    res = run_bass_kernel_spmd(nc, in_maps, core_ids=list(range(8)))
    return np.stack([np.asarray(r["out"], np.float32) for r in res.results], 0)


# version 08 (final)
# speedup vs baseline: 1.3190x; 1.0036x over previous
import numpy as np
from contextlib import ExitStack

import concourse.bass as bass
import concourse.mybir as mybir
from concourse.bass_utils import run_bass_kernel_spmd

F32 = mybir.dt.float32
BF16 = mybir.dt.bfloat16
I32 = mybir.dt.int32
AF = mybir.ActivationFunctionType
ALU = mybir.AluOpType

T, D, HN, KH = 4096, 1024, 16, 64
KV = KH + 1
NCH = 32
INNER, STATE = 2048, 16
FFH = 4096
FFI = 3072
NEG_LOG_LAM = float(np.log(np.float64(np.float32(1e-8))))
QS = float(np.float32(1.0 / 127.0))
LN_EPS = 1e-5

ENGINES = ("sync", "scalar", "vector", "tensor", "gpsimd")


def host_constants():
    return {
        "c_triu": np.triu(np.ones((128, 128), np.float32)),
        "c_trilm1": np.tril(np.ones((128, 128), np.float32), -1),
        "c_ones_row": np.ones((1, 128), np.float32),
        "c_ident": np.eye(128, dtype=np.float32),
    }


class SemC:
    __slots__ = ("h", "n")

    def __init__(self, h):
        self.h = h
        self.n = 0


class Prog:
    def __init__(self, nc, es):
        self.nc = nc
        self.es = es
        self.ops = {e: [] for e in ENGINES}
        self.sems = {}

    def sem(self, name):
        s = SemC(self.es.enter_context(self.nc.semaphore(name)))
        self.sems[name] = s
        return s

    def sb(self, name, shape, dt=F32):
        return self.es.enter_context(self.nc.sbuf_tensor(name, list(shape), dt))

    def emit(self, eng, fn, waits=(), inc="auto", inc_by=1):
        if inc == "auto":
            inc = self.sems.get("sm_" + eng) if eng != "sync" else None
        if inc is not None:
            inc.n += inc_by
        best = {}
        for s, v in waits:
            if v > 0 and (s.h not in best or best[s.h][1] < v):
                best[s.h] = (s, v)
        wl = tuple((s.h, v) for s, v in best.values())

        def g(e, fn=fn, wl=wl, inc=inc, inc_by=inc_by):
            for h, v in wl:
                e.wait_ge(h, v)
            r = fn(e)
            if inc is not None:
                if r is None:
                    r = e.nop()
                r.then_inc(inc.h, inc_by)

        self.ops[eng].append(g)

    def wait(self, eng, *waits):
        wl = tuple((s.h, v) for s, v in waits if v > 0)
        if not wl:
            return

        def g(e, wl=wl):
            for h, v in wl:
                e.wait_ge(h, v)

        self.ops[eng].append(g)

    def scope(self, name):
        for e in ENGINES:
            self.ops[e].append(("scope", name))


def build_core_program(upto="all", dbg=()):
    nc = bass.Bass("TRN2", target_bir_lowering=False, detect_race_conditions=True)
    es = ExitStack()
    P = Prog(nc, es)

    din = {}

    def ein(name, shape, dt=F32):
        din[name] = nc.dram_tensor(name, list(shape), dt, kind="ExternalInput").ap()
        return din[name]

    x_in = ein("x", (T, D))
    for nm in ("q_w", "k_w", "v_w", "o_w"):
        ein(nm, (D, D))
    for nm in ("q_b", "k_b", "v_b", "o_b", "ln1_w", "ln1_b", "ln2_w", "ln2_b", "f2_b"):
        ein(nm, (D,))
    ein("dec_w", (D, HN)); ein("dec_b", (HN,))
    ein("f1_w", (D, FFH)); ein("f1_b", (FFH,)); ein("f2_w", (FFH, D))
    ein("W_in", (D, 2 * INNER)); ein("conv_w", (INNER, 3)); ein("conv_b", (INNER,))
    ein("W_state", (INNER, 2 * STATE)); ein("W_time", (INNER, INNER))
    ein("b_time", (INNER,)); ein("A_log", (INNER, STATE)); ein("skip", (INNER,))
    ein("W_out", (INNER, D))
    ein("w_gate", (FFI, D), I32); ein("w_up", (FFI, D), I32); ein("w_down", (D, FFI), I32)
    for nm, arr in host_constants().items():
        ein(nm, arr.shape)

    dbg = set(dbg)
    douts = {}

    def scr(name, shape, dt=F32):
        kind = "ExternalOutput" if name in dbg else "Internal"
        t = nc.dram_tensor(name, list(shape), dt, kind=kind).ap()
        if name in dbg:
            douts[name] = t
        return t

    out_d = nc.dram_tensor("out", [T, D], F32, kind="ExternalOutput").ap()
    douts["out"] = out_d

    s_xn0 = scr("xn0", (T, D))
    s_x1 = scr("x1", (T, D))
    s_x2 = scr("x2", (T, D))
    s_x3 = scr("x3", (T, D))
    s_xmid = scr("x_mid", (T, D))
    s_phiq_fm = scr("phiq_fm", (D, T), BF16)
    s_phik_fm = scr("phik_fm", (D, T), BF16)
    s_phiq_tok = scr("phiq_tok", (T, D), BF16)
    s_phik_tok = scr("phik_tok", (T, D), BF16)
    s_v_tok = scr("v_tok", (T, D), BF16)
    s_attnfm = scr("attn_fm_d", (D, T), BF16)
    s_z3fm = scr("z3fm", (D, T), BF16)
    s_h1fm = scr("h1fm", (FFH, T), BF16)
    s_h2fm = scr("h2fm", (FFI, T), BF16)
    s_hfm = scr("h_fm", (INNER, T))
    s_hfm_bf = scr("h_fm_bf", (INNER, T), BF16)
    s_sg = scr("sg_fm", (INNER, T))
    s_winbf = scr("win_bf", (D, 2 * INNER), BF16)
    s_wtt = scr("wt_tiled", (128, 16 * 16 * 128), BF16)
    s_woutbf = scr("wout_bf", (INNER, D), BF16)
    s_wgT = scr("wgT", (D, FFI), BF16)
    s_wuT = scr("wuT", (D, FFI), BF16)
    s_wdT = scr("wdT", (FFI, D), BF16)
    s_f1bf = scr("f1_bf", (D, FFH), BF16)
    s_dbg_misc = scr("dbg_misc", (128, 2048)) if "dbg_misc" in dbg else None

    def sbR(es_, n, s, d=F32):
        return es_.enter_context(nc.sbuf_tensor(n, list(s), d, side="right"))

    es_lion = ExitStack()
    LI = lambda n, s, d=F32: sbR(es_lion, n, s, d)
    ident_f = P.sb("ident_f", (128, 128))
    ident_b = P.sb("ident_b", (128, 128), BF16)
    triu_f = LI("triu_f", (128, 128))
    trilm1_f = LI("trilm1_f", (128, 128))
    ones_row = P.sb("ones_row", (1, 128))
    g1c = P.sb("g1c", (128, 8)); b1c = P.sb("b1c", (128, 8))
    g2c = P.sb("g2c", (128, 8)); b2c = P.sb("b2c", (128, 8))
    bqc = P.sb("bqc", (128, 8)); nbqc = P.sb("nbqc", (128, 8))
    bkc = P.sb("bkc", (128, 8)); nbkc = P.sb("nbkc", (128, 8))
    bvc = P.sb("bvc", (128, 8))
    bdecc = P.sb("bdecc", (16, 1)); nbdecc = P.sb("nbdecc", (16, 1))
    f1bc = P.sb("f1bc", (128, 32))
    btimec = P.sb("btimec", (128, 16))
    cw0c = P.sb("cw0c", (128, 16)); cw1c = P.sb("cw1c", (128, 16))
    cw2c = P.sb("cw2c", (128, 16)); cbc = P.sb("cbc", (128, 16))
    skipc = P.sb("skipc", (128, 16))
    A_sb = P.sb("A_sb", (128, 256))
    ob_bc = LI("ob_bc", (128, 1024))
    f2b_bc = LI("f2b_bc", (128, 1024))
    lloc_tok = LI("lloc_tok", (128, NCH * HN))
    af_tok = LI("af_tok", (128, NCH * HN))
    ab_tok = LI("ab_tok", (128, NCH * HN))
    gam_bc = LI("gam_bc", (128, NCH * HN))
    gamrow = LI("gamrow", (1, NCH * HN))
    eps_col = P.sb("eps_col", (128, 1))
    ln_m = P.sb("ln_m", (128, 1)); ln_s2 = P.sb("ln_s2", (128, 1))
    ln_v = P.sb("ln_v", (128, 1)); ln_r = P.sb("ln_r", (128, 1))
    ln_scr = P.sb("ln_scr", (128, 1024))
    ln_scr2 = P.sb("ln_scr2", (128, 1024))
    ln_guard = [0]
    lnb_m = P.sb("lnb_m", (128, 16)); lnb_s2 = P.sb("lnb_s2", (128, 16))
    lnb_v = P.sb("lnb_v", (128, 16)); lnb_r = P.sb("lnb_r", (128, 16))
    f32stage = P.sb("f32stage", (128, 4096))
    bfstage = P.sb("bfstage", (128, 4096), BF16)
    phtmp1 = P.sb("phtmp1", (128, 512))
    phtmp2 = P.sb("phtmp2", (128, 512))

    psf = [P.es.enter_context(nc.psum_tensor(f"psf{i}", [128, 512], F32)) for i in range(6)]
    psb = [P.es.enter_context(nc.psum_tensor(f"psb{i}", [128, 1024], BF16)) for i in range(2)]
    bankfree = {}

    def bfree(bid):
        w = bankfree.get(bid)
        return [w] if w else []

    sm = {e: P.sem("sm_" + e) for e in ENGINES}
    ld_pool = {"sync": [P.sem(f"ldS{i}") for i in range(16)],
               "scalar": [P.sem(f"ldA{i}") for i in range(10)]}
    ld_idx = {"sync": 0, "scalar": 0}
    store_sems = {}
    fence_sem = P.sem("fence")

    def all_dma_vals():
        vals = [(s, s.n) for s in ld_pool["sync"] if s.n > 0]
        vals += [(s, s.n) for s in ld_pool["scalar"] if s.n > 0]
        vals += [(s, s.n) for pair in store_sems.values() for s in pair if s.n > 0]
        return vals

    def fence(dma=True):
        vals = []
        for e in ENGINES:
            P.emit(e, lambda en: en.nop(), waits=[(sm[e], sm[e].n)], inc=sm[e])
            vals.append((sm[e], sm[e].n))
        if dma:
            P.emit("sync", lambda en: en.nop(), waits=vals + all_dma_vals(),
                   inc=fence_sem)
            fv = (fence_sem, fence_sem.n)
            for e in ENGINES:
                P.wait(e, fv)
        else:
            for e in ENGINES:
                P.wait(e, *vals)

    def efence(*engs):
        vals = []
        for e in engs:
            P.emit(e, lambda en: en.nop(), waits=[(sm[e], sm[e].n)], inc=sm[e])
            vals.append((sm[e], sm[e].n))
        for e in engs:
            P.wait(e, *vals)

    def load(sbuf_ap, dram_ap, q="sync", waits=(), slow=False):
        s = ld_pool[q][ld_idx[q] % len(ld_pool[q])]
        ld_idx[q] += 1
        w = list(waits)
        if s.n > 0:
            w.append((s, s.n))
        nm = dram_ap.tensor.name
        if nm in store_sems:
            for ss in store_sems[nm]:
                if ss.n > 0:
                    w.append((ss, ss.n))
        P.emit(q, lambda e, o=sbuf_ap, i=dram_ap, sl=slow: e.dma_start(
            out=o, in_=i, allow_slow_non_contiguous=sl),
               waits=w, inc=s, inc_by=16)
        return (s, s.n)

    store_idx = {}

    def store(dram_ap, sbuf_ap, waits=(), q="sync", slow=False):
        nm = dram_ap.tensor.name
        if nm not in store_sems:
            store_sems[nm] = [P.sem("st_" + nm + "a"), P.sem("st_" + nm + "b")]
            store_idx[nm] = 0
        k = store_idx[nm]
        store_idx[nm] = 1 - k
        s = store_sems[nm][k]
        w = list(waits)
        if s.n > 0:
            w.append((s, s.n))
        P.emit(q, lambda e, o=dram_ap, i=sbuf_ap, sl=slow: e.dma_start(
            out=o, in_=i, allow_slow_non_contiguous=sl),
               waits=w, inc=s, inc_by=16)
        return (s, s.n)

    def tval():
        return (sm["tensor"], sm["tensor"].n)

    def vval():
        return (sm["vector"], sm["vector"].n)

    def sval():
        return (sm["scalar"], sm["scalar"].n)

    def gval():
        return (sm["gpsimd"], sm["gpsimd"].n)

    def mm(ps_ap, bid, lhsT, rhs, start, stop, waits=(), inc=True):
        w = list(waits)
        if start:
            w += bfree(bid)
        P.emit("tensor", lambda e, o=ps_ap, l=lhsT, r=rhs, st=start, sp=stop:
               e.matmul(o, l, r, start=st, stop=sp), waits=w,
               inc=sm["tensor"] if inc else None)

    def evac(eng, bid, fn, waits=()):
        s = sm[eng]
        P.emit(eng, fn, waits=waits, inc=s)
        bankfree[bid] = (s, s.n)
        return (s, s.n)

    def layer_norm(dst_ap, src_ap, W, waits=()):
        P.emit("vector", lambda e: e.tensor_scalar(
            ln_scr[:, 0:W], src_ap, 1.0, 0.0, op0=ALU.mult, op1=ALU.add,
            accum_out=ln_m[:, 0:1]), waits=list(waits) + [ (sm["vector"], ln_guard[0]) ])
        P.emit("vector", lambda e: e.scalar_tensor_tensor(
            ln_scr2[:, 0:W], src_ap, 1.0, src_ap, op0=ALU.mult, op1=ALU.mult,
            accum_out=ln_s2[:, 0:1]))
        v2 = vval()
        P.emit("vector", lambda e: e.tensor_tensor(ln_v[:, 0:1], ln_m[:, 0:1],
                                                   ln_m[:, 0:1], op=ALU.mult),
               waits=[v2])
        v3 = vval()
        P.emit("vector", lambda e, W=W: e.scalar_tensor_tensor(
            ln_v[:, 0:1], ln_v[:, 0:1], -1.0 / W, ln_s2[:, 0:1],
            op0=ALU.mult, op1=ALU.add), waits=[v3])
        v4 = vval()
        P.emit("scalar", lambda e, W=W: e.activation(
            ln_r[:, 0:1], ln_v[:, 0:1], AF.Ln, bias=eps_col[:, 0:1], scale=1.0 / W),
            waits=[v4])
        s1 = sval()
        P.emit("scalar", lambda e: e.activation(ln_r[:, 0:1], ln_r[:, 0:1], AF.Exp,
                                                scale=-0.5), waits=[s1])
        sv = sval()
        P.emit("vector", lambda e, W=W: e.tensor_scalar(
            ln_m[:, 0:1], ln_m[:, 0:1], 1.0 / W, None, op0=ALU.mult), waits=[v4])
        v5 = vval()
        P.emit("vector", lambda e: e.tensor_scalar(
            dst_ap, src_ap, ln_m[:, 0:1], ln_r[:, 0:1], op0=ALU.subtract, op1=ALU.mult),
            waits=[sv, v5])
        ln_guard[0] = sm["vector"].n
        return vval()

    def layer_norm_span(src_f, NT, W, waits_f):
        for i in range(NT):
            P.emit("vector", lambda e, i=i: e.tensor_scalar(
                ln_scr[:, 0:W], src_f(i), 1.0, 0.0, op0=ALU.mult, op1=ALU.add,
                accum_out=lnb_m[:, i:i + 1]),
                waits=list(waits_f(i)) + ([(sm["vector"], ln_guard[0])]
                                          if i == 0 else [vval()]))
            P.emit("vector", lambda e, i=i: e.scalar_tensor_tensor(
                ln_scr2[:, 0:W], src_f(i), 1.0, src_f(i), op0=ALU.mult, op1=ALU.mult,
                accum_out=lnb_s2[:, i:i + 1]), waits=[vval()] if i else ())
        v2 = vval()
        P.emit("vector", lambda e: e.tensor_tensor(
            lnb_v[:, 0:NT], lnb_m[:, 0:NT], lnb_m[:, 0:NT], op=ALU.mult), waits=[v2])
        v3 = vval()
        P.emit("vector", lambda e, W=W: e.scalar_tensor_tensor(
            lnb_v[:, 0:NT], lnb_v[:, 0:NT], -1.0 / W, lnb_s2[:, 0:NT],
            op0=ALU.mult, op1=ALU.add), waits=[v3])
        v4 = vval()
        P.emit("scalar", lambda e, W=W: e.activation(
            lnb_r[:, 0:NT], lnb_v[:, 0:NT], AF.Ln, bias=eps_col[:, 0:1], scale=1.0 / W),
            waits=[v4])
        s1 = sval()
        P.emit("scalar", lambda e: e.activation(lnb_r[:, 0:NT], lnb_r[:, 0:NT], AF.Exp,
                                                scale=-0.5), waits=[s1])
        svr = sval()
        P.emit("vector", lambda e, W=W: e.tensor_scalar(
            lnb_m[:, 0:NT], lnb_m[:, 0:NT], 1.0 / W, None, op0=ALU.mult), waits=[v4])
        v5 = vval()
        first = [True]

        def norm(i, dst_ap, waits=()):
            w = list(waits) + ([svr, v5] if first[0] else [])
            first[0] = False
            P.emit("vector", lambda e, i=i, d=dst_ap: e.tensor_scalar(
                d, src_f(i), lnb_m[:, i:i + 1], lnb_r[:, i:i + 1],
                op0=ALU.subtract, op1=ALU.mult), waits=w)
            ln_guard[0] = sm["vector"].n
            return vval()

        return norm

    w1 = load(ident_f[:], din["c_ident"][:])
    load(triu_f[:], din["c_triu"][:])
    load(trilm1_f[:], din["c_trilm1"][:])
    load(ones_row[:], din["c_ones_row"][:])
    for sb_t, nm in [(g1c, "ln1_w"), (b1c, "ln1_b"), (g2c, "ln2_w"), (b2c, "ln2_b"),
                     (btimec, "b_time"), (cbc, "conv_b"), (skipc, "skip")]:
        load(sb_t[:], din[nm].rearrange("(c p) -> p c", p=128), slow=True)
    load(bdecc[:, 0:1], din["dec_b"].rearrange("(c p) -> p c", p=16), slow=True)
    for j, sb_t in enumerate((cw0c, cw1c, cw2c)):
        load(sb_t[:].rearrange("p (c one) -> p c one", one=1),
             din["conv_w"][:, j:j + 1].rearrange("(c p) one -> p c one", p=128),
             slow=True)
    wA = load(A_sb[:].rearrange("p (c s) -> p c s", s=16),
              din["A_log"].rearrange("(c p) s -> p c s", p=128))
    P.emit("vector", lambda e: e.memset(eps_col[:], LN_EPS), inc=sm["vector"])
    P.emit("vector", lambda e: e.tensor_copy(ident_b[:], ident_f[:]), waits=[w1],
           inc=sm["vector"])
    P.emit("scalar", lambda e: e.activation(A_sb[:], A_sb[:], AF.Exp), waits=[wA],
           inc=sm["scalar"])
    P.emit("vector", lambda e: e.tensor_scalar_mul(A_sb[:], A_sb[:], -1.0),
           waits=[sval()], inc=sm["vector"])

    P.scope("p0_weights")
    es_p0 = ExitStack()
    tstage8 = sbR(es_p0, "tstage8", (128, 8 * 3072), BF16)
    i32stage = sbR(es_p0, "i32stage", (128, 3072), I32)

    def transpose_int_w(dst, src, rows, cols):
        P.wait("vector", *[(s, s.n) for pair in store_sems.values() for s in pair
                           if s.n > 0])
        for rb in range(rows // 128):
            wld = load(i32stage[:, 0:cols], src[128 * rb:128 * (rb + 1), :], q="scalar")
            P.emit("vector", lambda e, c=cols: e.tensor_copy(
                bfstage[:, 0:c], i32stage[:, 0:c]), waits=[wld, tval()],
                inc=sm["vector"])
            vv = vval()
            P.wait("scalar", vv)
            for cb in range(cols // 128):
                bid = 6 + cb % 2
                P.emit("tensor", lambda e, cb=cb: e.transpose(
                    psb[cb % 2][:, 0:128], bfstage[:, 128 * cb:128 * (cb + 1)],
                    ident_b[:]), waits=[vv] + bfree(bid), inc=sm["tensor"])
                tv = tval()
                evac("vector", bid, lambda e, cb=cb, rb=rb, rows=rows: e.tensor_copy(
                    tstage8[:, cb * rows + 128 * rb:cb * rows + 128 * (rb + 1)],
                    psb[cb % 2][:, 0:128]), waits=[tv])
        vv = vval()
        for cb in range(cols // 128):
            store(dst[128 * cb:128 * (cb + 1), 0:rows],
                  tstage8[:, cb * rows:(cb + 1) * rows], waits=[vv])

    transpose_int_w(s_wgT, din["w_gate"], FFI, D)
    transpose_int_w(s_wuT, din["w_up"], FFI, D)
    transpose_int_w(s_wdT, din["w_down"], D, FFI)
    fence()
    es_p0.close()

    def cast_rows(dst, src, rows, cols, tiled=False):
        for rb in range(rows // 128):
            wld = load(f32stage[:, 0:cols], src[128 * rb:128 * (rb + 1), :], q="scalar")
            P.emit("vector", lambda e, c=cols: e.tensor_copy(bfstage[:, 0:c],
                                                             f32stage[:, 0:c]),
                   waits=[wld], inc=sm["vector"])
            vv = vval()
            if tiled:
                src3 = bfstage[:, 0:cols].rearrange("p (mc j) -> p mc j", j=128)
                dst3 = s_wtt[:, :].rearrange("p (mc kc j) -> p mc (kc j)", kc=16, j=128)[
                    :, :, 128 * rb:128 * (rb + 1)]
                st = store(dst3, src3, waits=[vv])
            else:
                st = store(dst[128 * rb:128 * (rb + 1), :], bfstage[:, 0:cols], waits=[vv])
            P.wait("scalar", st)
            P.wait("vector", st)
        fence()

    cast_rows(s_winbf, din["W_in"], D, 2 * INNER)
    cast_rows(s_woutbf, din["W_out"], INNER, D)
    cast_rows(None, din["W_time"], INNER, INNER, tiled=True)

    es_a1 = ExitStack()
    A1 = lambda n, s, d=F32: sbR(es_a1, n, s, d)
    wq_bf = A1("wq_bf", (128, 8 * 1024), BF16)
    wk_bf = A1("wk_bf", (128, 8 * 1024), BF16)
    wv_bf = A1("wv_bf", (128, 8 * 1024), BF16)
    wdec_bf = A1("wdec_bf", (128, 8 * 16), BF16)
    es_fold = ExitStack()
    FO = lambda n, s, d=F32: sbR(es_fold, n, s, d)
    rowbuf = FO("rowbuf", (1, 1024))
    rowbuf2 = FO("rowbuf2", (1, 1024))
    brow = FO("brow", (1, 4096))
    bprow = FO("bprow", (1, 4096))
    btmp = FO("btmp", (128, 128))
    w_ob = load(rowbuf[:], din["o_b"][None, :])
    w_f2b = load(rowbuf2[:], din["f2_b"][None, :])

    def bcast_row(dst_sb, row_sb, n, wait):
        for i in range((n + 511) // 512):
            nn = min(512, n - 512 * i)
            bid = i % 2
            mm(psf[bid][:, 0:nn], bid, ones_row[:], row_sb[0:1, 512 * i:512 * i + nn],
               True, True, waits=[wait])
            tv = tval()
            evac("vector", bid, lambda e, b=bid, i=i, nn=nn: e.tensor_copy(
                dst_sb[:, 512 * i:512 * i + nn], psf[b][:, 0:nn]), waits=[tv])
        return vval()

    v_obbc = bcast_row(ob_bc, rowbuf, 1024, w_ob)
    v_f2bc = bcast_row(f2b_bc, rowbuf2, 1024, w_f2b)

    def fold_prep(w_name, n_out, gcol, betacol, b_name, wbf_sb, bias_col, neg_bias_col,
                  wbf_dram=None, bias_npart=128):
        nkc = 8
        npass = max(1, n_out // 2048) if n_out >= 2048 else 1
        ncols_pass = min(n_out, 2048)
        for ip in range(npass):
            for kc in range(nkc):
                wld = load(f32stage[:, 0:n_out], din[w_name][128 * kc:128 * (kc + 1), :],
                           q="scalar")
                if ip == 0:
                    P.emit("vector", lambda e, kc=kc, n=n_out: e.tensor_scalar(
                        bfstage[:, 0:n], f32stage[:, 0:n], gcol[:, kc:kc + 1], None,
                        op0=ALU.mult), waits=[wld], inc=sm["vector"])
                    vv = vval()
                    if wbf_dram is not None:
                        st = store(wbf_dram[128 * kc:128 * (kc + 1), :], bfstage[:, 0:n_out],
                                   waits=[vv])
                        P.wait("vector", st)
                    else:
                        P.emit("vector", lambda e, kc=kc, n=n_out: e.tensor_copy(
                            wbf_sb[:, n * kc:n * (kc + 1)], bfstage[:, 0:n]),
                            waits=[vv], inc=sm["vector"])
                nb = (min(ncols_pass, n_out - 2048 * ip) + 511) // 512
                for ib in range(nb):
                    c0 = 2048 * ip + 512 * ib
                    cn = min(512, n_out - c0)
                    mm(psf[2 + ib][0:1, 0:cn], 2 + ib, betacol[:, kc:kc + 1],
                       f32stage[:, c0:c0 + cn], kc == 0, kc == nkc - 1,
                       waits=[wld, vval()])
                P.wait("scalar", tval(), vval())
            tv = tval()
            for ib in range(nb):
                c0 = 2048 * ip + 512 * ib
                cn = min(512, n_out - c0)
                evac("vector", 2 + ib, lambda e, c0=c0, cn=cn, ib=ib: e.tensor_copy(
                    brow[0:1, c0:c0 + cn], psf[2 + ib][0:1, 0:cn]), waits=[tv])
        wb = load(bprow[0:1, 0:n_out], din[b_name][None, :])
        P.emit("vector", lambda e, n=n_out: e.tensor_add(
            brow[0:1, 0:n], brow[0:1, 0:n], bprow[0:1, 0:n]), waits=[wb, vval()],
            inc=sm["vector"])
        vv = vval()
        nmc = max(1, n_out // 128) if n_out >= 128 else 1
        cw = min(128, n_out)
        for mc in range(nmc):
            P.emit("vector", lambda e, mc=mc, cw=cw: e.tensor_copy(
                btmp[0:1, 0:cw], brow[0:1, 128 * mc:128 * mc + cw]),
                waits=[vv] + bfree(5), inc=sm["vector"])
            vv2 = vval()
            P.emit("tensor", lambda e: e.transpose(psf[5][:, 0:128], btmp[:, 0:128],
                                                   ident_f[:]),
                   waits=[vv2] + bfree(5), inc=sm["tensor"])
            tv = tval()
            evac("vector", 5, lambda e, mc=mc, cw=cw: e.tensor_copy(
                bias_col[0:cw, mc:mc + 1], psf[5][0:cw, 0:1]), waits=[tv])
            if neg_bias_col is not None:
                P.emit("vector", lambda e, mc=mc, cw=cw: e.tensor_scalar_mul(
                    neg_bias_col[0:cw, mc:mc + 1], bias_col[0:cw, mc:mc + 1], -1.0),
                    waits=[vval()], inc=sm["vector"])
        fence()

    fold_prep("q_w", D, g1c, b1c, "q_b", wq_bf, bqc, nbqc)
    fold_prep("k_w", D, g1c, b1c, "k_b", wk_bf, bkc, nbkc)
    fold_prep("v_w", D, g1c, b1c, "v_b", wv_bf, bvc, None)
    fold_prep("dec_w", HN, g1c, b1c, "dec_b", wdec_bf, bdecc, nbdecc, bias_npart=16)
    fold_prep("f1_w", FFH, g2c, b2c, "f1_b", None, f1bc, None, wbf_dram=s_f1bf)
    fence()
    es_fold.close()

    SPA = 512
    NTT = SPA // 128
    x_span = A1("x_span", (128, NTT * 1024))
    z2_bf = A1("z2_bf", (128, NTT * 1024), BF16)
    z2fm = A1("z2fm", (128, 8 * SPA), BF16)
    stg_fm = A1("stg_fm", (128, SPA), BF16)
    vbf_fm = A1("vbf_fm", (128, 8 * SPA), BF16)
    stg_tok = [A1(f"stg_tok{i}", (128, 1024), BF16) for i in range(2)]
    stg_tok_st = [None, None]
    stg_fm_st = None
    dptmp = A1("dptmp", (16, 1024))
    dptmp2 = A1("dptmp2", (16, 1024))
    logl_buf = A1("logl_buf", (16, 1024))
    logl_tokc = A1("logl_tokc", (128, 16))

    P.scope("p_a1")
    def phase_a1():
        nonlocal stg_fm_st
        for sp in range(T // SPA):
            t0 = sp * SPA
            fence(dma=False)
            for tt in range(NTT):
                wld = load(x_span[:, 1024 * tt:1024 * (tt + 1)],
                           x_in[t0 + 128 * tt:t0 + 128 * (tt + 1), :])
                layer_norm(x_span[:, 1024 * tt:1024 * (tt + 1)],
                           x_span[:, 1024 * tt:1024 * (tt + 1)], 1024, waits=[wld])
                vX = vval()
                store(s_xn0[t0 + 128 * tt:t0 + 128 * (tt + 1), :],
                      x_span[:, 1024 * tt:1024 * (tt + 1)], waits=[vX])
                layer_norm(z2_bf[:, 1024 * tt:1024 * (tt + 1)],
                           x_span[:, 1024 * tt:1024 * (tt + 1)], 1024, waits=[vX])
                vZ = vval()
                for kc in range(8):
                    bid = 6 + kc % 2
                    P.emit("tensor", lambda e, tt=tt, kc=kc: e.transpose(
                        psb[kc % 2][:, 0:128],
                        z2_bf[:, 1024 * tt + 128 * kc:1024 * tt + 128 * (kc + 1)],
                        ident_b[:]), waits=[vZ] + bfree(bid), inc=sm["tensor"])
                    tv = tval()
                    evac("vector", bid, lambda e, tt=tt, kc=kc: e.tensor_copy(
                        z2fm[:, SPA * kc + 128 * tt:SPA * kc + 128 * (tt + 1)],
                        psb[kc % 2][:, 0:128]), waits=[tv])
            vZF = vval()

            for (wsb, bcol, nbcol, fmd, tokd, is_v) in (
                    (wq_bf, bqc, nbqc, s_phiq_fm, s_phiq_tok, False),
                    (wk_bf, bkc, nbkc, s_phik_fm, s_phik_tok, False),
                    (wv_bf, bvc, None, None, s_v_tok, True)):
                for mc in range(8):
                    for nt in range(SPA // 512):
                        bid = (mc * 2 + nt) % 4
                        ps = psf[bid]
                        for kc in range(8):
                            mm(ps[:, 0:512], bid,
                               wsb[:, 1024 * kc + 128 * mc:1024 * kc + 128 * (mc + 1)],
                               z2fm[:, SPA * kc + 512 * nt:SPA * kc + 512 * (nt + 1)],
                               kc == 0, kc == 7, waits=[vZF] if kc == 0 else ())
                        tv = tval()
                        col = 512 * nt
                        if not is_v:
                            P.emit("scalar", lambda e, ps=ps, nb=nbcol, mc=mc: e.activation(
                                phtmp1[:, 0:512], ps[:, 0:512], AF.Relu,
                                bias=nb[:, mc:mc + 1], scale=-1.0),
                                waits=[tv, vval()], inc=sm["scalar"])
                            P.emit("scalar", lambda e: e.activation(
                                phtmp1[:, 0:512], phtmp1[:, 0:512], AF.Exp, scale=-1.0),
                                waits=[sval()])
                            P.emit("scalar", lambda e, ps=ps, b=bcol, mc=mc: e.activation(
                                phtmp2[:, 0:512], ps[:, 0:512], AF.Relu,
                                bias=b[:, mc:mc + 1]), inc=sm["scalar"])
                            sv = sval()
                            bankfree[bid] = sv
                            P.emit("vector", lambda e, col=col: e.tensor_add(
                                stg_fm[:, col:col + 512], phtmp1[:, 0:512],
                                phtmp2[:, 0:512]),
                                waits=[sv] + (stg_fm_st or []), inc=sm["vector"])
                        else:
                            evac("scalar", bid, lambda e, ps=ps, b=bcol, mc=mc, col=col:
                                 e.activation(stg_fm[:, col:col + 512], ps[:, 0:512],
                                              AF.Identity, bias=b[:, mc:mc + 1]),
                                 waits=[tv] + (stg_fm_st or []))
                    vv = vval()
                    svv = sval()
                    st_w = None
                    if fmd is not None:
                        st_w = store(fmd[128 * mc:128 * (mc + 1), t0:t0 + SPA],
                                     stg_fm[:, 0:SPA], waits=[vv, svv])
                    P.emit("vector", lambda e, mc=mc: e.tensor_copy(
                        vbf_fm[:, SPA * mc:SPA * (mc + 1)], stg_fm[:, 0:SPA]),
                        waits=[vv, svv], inc=sm["vector"])
                    stg_fm_st = [vval()] + ([st_w] if st_w else [])
                vv = vval()
                for tt in range(NTT):
                    sl = tt % 2
                    for mc in range(8):
                        bid = 6 + mc % 2
                        P.emit("tensor", lambda e, tt=tt, mc=mc: e.transpose(
                            psb[mc % 2][:, 128:256],
                            vbf_fm[:, SPA * mc + 128 * tt:SPA * mc + 128 * (tt + 1)],
                            ident_b[:]), waits=[vv] + bfree(bid), inc=sm["tensor"])
                        tv = tval()
                        evac("vector", bid, lambda e, sl=sl, mc=mc: e.tensor_copy(
                            stg_tok[sl][:, 128 * mc:128 * (mc + 1)],
                            psb[mc % 2][:, 128:256]),
                            waits=[tv] + ([stg_tok_st[sl]] if stg_tok_st[sl] else []))
                    vv2 = vval()
                    stg_tok_st[sl] = store(tokd[t0 + 128 * tt:t0 + 128 * (tt + 1), :],
                                           stg_tok[sl][:, 0:1024], waits=[vv2])

            for nt in range(SPA // 512):
                bid = 4
                ps = psf[4]
                for kc in range(8):
                    mm(ps[0:16, 0:512], bid, wdec_bf[:, 16 * kc:16 * (kc + 1)],
                       z2fm[:, SPA * kc + 512 * nt:SPA * kc + 512 * (nt + 1)],
                       kc == 0, kc == 7)
                tv = tval()
                P.emit("scalar", lambda e, ps=ps: e.activation(
                    dptmp[:, 0:512], ps[0:16, 0:512], AF.Abs, bias=bdecc[:, 0:1]),
                    waits=[tv, vval()], inc=sm["scalar"])
                P.emit("scalar", lambda e: e.activation(dptmp[:, 0:512], dptmp[:, 0:512],
                                                        AF.Exp, scale=-1.0),
                       waits=[sval()])
                P.emit("scalar", lambda e: e.activation(dptmp[:, 0:512], dptmp[:, 0:512],
                                                        AF.Ln, bias=1.0),
                       waits=[sval()])
                evac("scalar", bid, lambda e, ps=ps: e.activation(
                    dptmp2[:, 0:512], ps[0:16, 0:512], AF.Relu, bias=nbdecc[:, 0:1],
                    scale=-1.0))
                sv = sval()
                P.emit("vector", lambda e, nt=nt: e.tensor_add(
                    logl_buf[:, 512 * nt:512 * (nt + 1)], dptmp[:, 0:512], dptmp2[:, 0:512]),
                    waits=[sv])
                P.emit("vector", lambda e, nt=nt: e.tensor_scalar(
                    logl_buf[:, 512 * nt:512 * (nt + 1)],
                    logl_buf[:, 512 * nt:512 * (nt + 1)], -1.0, NEG_LOG_LAM,
                    op0=ALU.mult, op1=ALU.max), waits=[vval()], inc=sm["vector"])
            vLL = vval()
            for ci in range(SPA // 128):
                c = sp * (SPA // 128) + ci
                P.emit("tensor", lambda e, ci=ci: e.transpose(
                    psf[4][:, 0:16], logl_buf[0:16, 128 * ci:128 * (ci + 1)],
                    ident_f[0:16, 0:16]),
                    waits=[vLL] + bfree(4), inc=sm["tensor"])
                tv = tval()
                evac("vector", 4, lambda e: e.tensor_copy(logl_tokc[:], psf[4][:, 0:16]),
                     waits=[tv])
                vv = vval()
                mm(psf[5][:, 0:16], 5, triu_f[:], logl_tokc[:], True, True, waits=[vv])
                mm(psf[5][:, 16:32], 5, trilm1_f[:], logl_tokc[:], True, True)
                tv = tval()
                P.emit("scalar", lambda e, c=c: e.activation(
                    af_tok[:, 16 * c:16 * (c + 1)], psf[5][:, 0:16], AF.Exp),
                    waits=[tv], inc=sm["scalar"])
                P.emit("scalar", lambda e, c=c: e.activation(
                    ab_tok[:, 16 * c:16 * (c + 1)], psf[5][:, 16:32], AF.Exp),
                    inc=sm["scalar"])
                sv = sval()
                evac("vector", 5, lambda e, c=c: e.tensor_copy(
                    lloc_tok[:, 16 * c:16 * (c + 1)], psf[5][:, 0:16]), waits=[sv])
        vv = vval()
        svv = sval()
        wgr = load(gamrow[0:1, :], af_tok[127:128, :], waits=[svv, vv])
        mm(psf[0][:, 0:512], 0, ones_row[:], gamrow[0:1, :], True, True, waits=[wgr])
        tv = tval()
        evac("vector", 0, lambda e: e.tensor_copy(gam_bc[:], psf[0][:, 0:512]), waits=[tv])

    phase_a1()
    fence()
    es_a1.close()
    if s_dbg_misc is not None:
        for i, t in enumerate((lloc_tok, af_tok, ab_tok, gam_bc)):
            store(s_dbg_misc[:, 512 * i:512 * (i + 1)], t[:])
        fence()
    if upto == "pa1":
        return _emit(nc, P, es, din, douts)

    P.scope("p_a2")
    es_a2 = ExitStack()
    A2 = lambda n, s, d=F32: sbR(es_a2, n, s, d)
    att_acc = A2("att_acc", (64, 2 * T), BF16)
    phq_tok_h = A2("phq_tok_h", (128, 2 * NCH * KH), BF16)
    phk_tok_h = A2("phk_tok_h", (128, 2 * NCH * KH), BF16)
    phq_fm_h = A2("phq_fm_h", (64, 2 * T), BF16)
    phk_fm_h = A2("phk_fm_h", (64, 2 * T), BF16)
    v_tok_h = A2("v_tok_h", (128, 2 * NCH * KV), BF16)
    R_sb = A2("R_sb", (64, 2 * NCH * KV))
    R_bf = A2("R_bf", (64, 2 * NCH * KV), BF16)
    S_sb = A2("S_sb", (64, 2 * KV))
    S_bf = A2("S_bf", (64, 2 * KV), BF16)
    kscl = A2("kscl", (128, 2 * KH), BF16)
    qf_tok = A2("qf_tok", (128, 2 * KH), BF16)
    qb_tok = A2("qb_tok", (128, 2 * KH), BF16)
    qf_fm = A2("qf_fm", (64, 2 * 128), BF16)
    qb_fm = A2("qb_fm", (64, 2 * 128), BF16)
    D_sb = A2("D_sb", (128, 2 * 128))
    W_sb = A2("W_sb", (128, 2 * 128), BF16)
    numden = A2("numden", (128, 4))
    att_tok = A2("att_tok", (128, 2 * KH), BF16)
    lrow = A2("lrow", (1, 2 * 128))

    def phase_a2():
        P.emit("vector", lambda e: e.memset(
            v_tok_h[:].rearrange("p (c k) -> p c k", k=KV)[:, :, KH:KV], 1.0),
            inc=sm["vector"])
        SL = {}

        def ld_head(sl, h):
            hc = h * KH
            o = sl * NCH * KH
            o5 = sl * NCH * KV
            w = []
            w.append(load(phq_tok_h[:, o:o + NCH * KH].rearrange(
                "p (c k) -> p c k", k=KH),
                s_phiq_tok[:, hc:hc + KH].rearrange("(c p) k -> p c k", p=128)))
            w.append(load(phk_tok_h[:, o:o + NCH * KH].rearrange(
                "p (c k) -> p c k", k=KH),
                s_phik_tok[:, hc:hc + KH].rearrange("(c p) k -> p c k", p=128)))
            w.append(load(phq_fm_h[:, sl * T:(sl + 1) * T], s_phiq_fm[hc:hc + KH, :]))
            w.append(load(phk_fm_h[:, sl * T:(sl + 1) * T], s_phik_fm[hc:hc + KH, :]))
            w.append(load(v_tok_h[:, o5:o5 + NCH * KV].rearrange(
                "p (c k) -> p c k", k=KV)[:, :, 0:KH],
                s_v_tok[:, hc:hc + KH].rearrange("(c p) k -> p c k", p=128)))
            SL[(sl, "lds")] = w
            P.emit("vector", lambda e, sl=sl: e.memset(
                R_sb[:, 2 * NCH * KV * 0 + (sl * NCH + NCH - 1) * KV:
                     (sl * NCH + NCH) * KV], 0.0), inc=sm["vector"])
            P.emit("vector", lambda e, sl=sl: e.memset(
                S_sb[:, KV * sl:KV * (sl + 1)], 0.0), inc=sm["vector"])
            P.emit("vector", lambda e, sl=sl: e.tensor_copy(
                R_bf[:, (sl * NCH + NCH - 1) * KV:(sl * NCH + NCH) * KV],
                R_sb[:, (sl * NCH + NCH - 1) * KV:(sl * NCH + NCH) * KV]),
                waits=[vval()], inc=sm["vector"])
            P.emit("vector", lambda e, sl=sl: e.tensor_copy(
                S_bf[:, KV * sl:KV * (sl + 1)], S_sb[:, KV * sl:KV * (sl + 1)]),
                waits=[vval()], inc=sm["vector"])
            SL[(sl, "init")] = vval()

        def bwd_step(sl, h, c):
            sc = c * HN + h
            ksl = kscl[:, KH * sl:KH * (sl + 1)]
            ro = sl * NCH * KV
            P.emit("vector", lambda e, c=c, sc=sc, sl=sl, ksl=ksl: e.tensor_scalar(
                ksl, phq_tok_h[:, 0:KH] if False else
                phk_tok_h[:, sl * NCH * KH + KH * c:sl * NCH * KH + KH * (c + 1)],
                af_tok[:, sc:sc + 1], None, op0=ALU.mult),
                waits=[SL[(sl, "kfree")]] if (sl, "kfree") in SL else
                      SL[(sl, "lds")] + [SL[(sl, "init")]], inc=sm["vector"])
            vv = vval()
            yield
            bid = 4 + sl
            mm(psf[bid][0:64, 0:KV], bid, ksl,
               v_tok_h[:, sl * NCH * KV + KV * c:sl * NCH * KV + KV * (c + 1)],
               True, True, waits=[vv])
            tv = tval()
            SL[(sl, "kfree")] = tv
            yield
            evac("vector", bid, lambda e, c=c, sc=sc, ro=ro, bid=bid:
                 e.scalar_tensor_tensor(
                     R_sb[:, ro + KV * (c - 1):ro + KV * c],
                     R_sb[:, ro + KV * c:ro + KV * (c + 1)],
                     gam_bc[0:64, sc:sc + 1], psf[bid][0:64, 0:KV],
                     op0=ALU.mult, op1=ALU.add), waits=[tv, vval()])
            yield
            P.emit("vector", lambda e, c=c, ro=ro: e.tensor_copy(
                R_bf[:, ro + KV * (c - 1):ro + KV * c],
                R_sb[:, ro + KV * (c - 1):ro + KV * c]),
                waits=[vval()], inc=sm["vector"])

        def fwd_step(sl, h, c):
            sc = c * HN + h
            tc = c * 128
            o = sl * NCH * KH
            o5 = sl * NCH * KV
            qo = sl * T
            dsl_ = D_sb[:, 128 * sl:128 * (sl + 1)]
            wsl_ = W_sb[:, 128 * sl:128 * (sl + 1)]
            P.emit("tensor", lambda e, sc=sc, sl=sl: e.transpose(
                psf[5][0:1, 128 * sl:128 * sl + 128], lloc_tok[:, sc:sc + 1],
                ident_f[:]), waits=bfree(5), inc=sm["tensor"])
            tvL = tval()
            evac("vector", 5, lambda e, sl=sl: e.tensor_copy(
                lrow[0:1, 128 * sl:128 * (sl + 1)],
                psf[5][0:1, 128 * sl:128 * sl + 128]), waits=[tvL, vval()])
            vL = vval()
            yield
            bidD = sl
            mm(psf[bidD][:, 0:128], bidD, ones_row[:],
               lrow[0:1, 128 * sl:128 * (sl + 1)], True, True, waits=[vL])
            tv1 = tval()
            bidS = 2 + sl
            mm(psf[bidS][:, 0:128], bidS, phk_fm_h[:, qo + tc:qo + tc + 128],
               phq_fm_h[:, qo + tc:qo + tc + 128], True, True)
            tv2 = tval()
            yield
            P.emit("scalar", lambda e, sc=sc, dsl_=dsl_, bidD=bidD: e.activation(
                dsl_, psf[bidD][:, 0:128], AF.Abs, bias=lloc_tok[:, sc:sc + 1],
                scale=-1.0), waits=[tv1] + ([SL[(sl, "dfree")]]
                                            if (sl, "dfree") in SL else []),
                inc=sm["scalar"])
            bankfree[bidD] = sval()
            P.emit("scalar", lambda e, dsl_=dsl_: e.activation(
                dsl_, dsl_, AF.Exp, scale=-1.0), waits=[sval()], inc=sm["scalar"])
            svD = sval()
            yield
            qfsl = qf_tok[:, KH * sl:KH * (sl + 1)]
            qbsl = qb_tok[:, KH * sl:KH * (sl + 1)]
            P.emit("vector", lambda e, c=c, sc=sc, o=o, qfsl=qfsl: e.tensor_scalar(
                qfsl, phq_tok_h[:, o + KH * c:o + KH * (c + 1)],
                af_tok[:, sc:sc + 1], None, op0=ALU.mult),
                waits=[SL[(sl, "qtfree")]] if (sl, "qtfree") in SL else [],
                inc=sm["vector"])
            vq1 = vval()
            yield
            P.emit("tensor", lambda e, sl=sl, qfsl=qfsl: e.transpose(
                psb[0][0:64, 128 * sl:128 * sl + 128], qfsl, ident_b[:]),
                waits=[vq1] + bfree(6), inc=sm["tensor"])
            tq1 = tval()
            evac("vector", 6, lambda e, sl=sl: e.tensor_copy(
                qf_fm[:, 128 * sl:128 * (sl + 1)],
                psb[0][0:64, 128 * sl:128 * sl + 128]), waits=[tq1])
            P.emit("vector", lambda e, c=c, sc=sc, o=o, qbsl=qbsl: e.tensor_scalar(
                qbsl, phq_tok_h[:, o + KH * c:o + KH * (c + 1)],
                ab_tok[:, sc:sc + 1], None, op0=ALU.mult), inc=sm["vector"])
            vq2 = vval()
            yield
            evac("vector", bidS, lambda e, dsl_=dsl_, wsl_=wsl_, bidS=bidS:
                 e.tensor_tensor(wsl_, dsl_, psf[bidS][:, 0:128], op=ALU.mult),
                 waits=[svD, tv2])
            SL[(sl, "dfree")] = vval()
            P.emit("tensor", lambda e, sl=sl, qbsl=qbsl: e.transpose(
                psb[1][0:64, 128 * sl:128 * sl + 128], qbsl, ident_b[:]),
                waits=[vq2] + bfree(7), inc=sm["tensor"])
            tq2 = tval()
            SL[(sl, "qtfree")] = tq2
            evac("vector", 7, lambda e, sl=sl: e.tensor_copy(
                qb_fm[:, 128 * sl:128 * (sl + 1)],
                psb[1][0:64, 128 * sl:128 * sl + 128]), waits=[tq2])
            vq = vval()
            yield
            bidO = 2 + sl
            mm(psf[bidO][:, 0:KV], bidO, wsl_,
               v_tok_h[:, o5 + KV * c:o5 + KV * (c + 1)], True, False, waits=[vq])
            mm(psf[bidO][:, 0:KV], bidO, qf_fm[:, 128 * sl:128 * (sl + 1)],
               S_bf[:, KV * sl:KV * (sl + 1)], False, False)
            mm(psf[bidO][:, 0:KV], bidO, qb_fm[:, 128 * sl:128 * (sl + 1)],
               R_bf[:, sl * NCH * KV + KV * c:sl * NCH * KV + KV * (c + 1)],
               False, True)
            tvO = tval()
            ksl = kscl[:, KH * sl:KH * (sl + 1)]
            P.emit("vector", lambda e, c=c, sc=sc, o=o, ksl=ksl: e.tensor_scalar(
                ksl, phk_tok_h[:, o + KH * c:o + KH * (c + 1)],
                ab_tok[:, sc:sc + 1], None, op0=ALU.mult),
                waits=[SL[(sl, "kfree")]] if (sl, "kfree") in SL else [],
                inc=sm["vector"])
            vk = vval()
            yield
            bidU = 4 + sl
            mm(psf[bidU][0:64, 0:KV], bidU, ksl,
               v_tok_h[:, o5 + KV * c:o5 + KV * (c + 1)], True, True, waits=[vk])
            tvS = tval()
            SL[(sl, "kfree")] = tvS
            yield
            evac("vector", bidU, lambda e, sc=sc, sl=sl, bidU=bidU:
                 e.scalar_tensor_tensor(
                     S_sb[:, KV * sl:KV * (sl + 1)], S_sb[:, KV * sl:KV * (sl + 1)],
                     gam_bc[0:64, sc:sc + 1], psf[bidU][0:64, 0:KV],
                     op0=ALU.mult, op1=ALU.add), waits=[tvS, vval()])
            P.emit("vector", lambda e, sl=sl: e.tensor_copy(
                S_bf[:, KV * sl:KV * (sl + 1)], S_sb[:, KV * sl:KV * (sl + 1)]),
                waits=[vval()], inc=sm["vector"])
            P.emit("vector", lambda e, sl=sl, bidO=bidO: e.tensor_scalar_max(
                numden[:, 2 * sl:2 * sl + 1], psf[bidO][:, KH:KV], 1e-6),
                waits=[tvO, vval()], inc=sm["vector"])
            P.emit("vector", lambda e, sl=sl: e.reciprocal(
                numden[:, 2 * sl + 1:2 * sl + 2], numden[:, 2 * sl:2 * sl + 1]),
                waits=[vval()])
            yield
            atsl = att_tok[:, KH * sl:KH * (sl + 1)]
            evac("vector", bidO, lambda e, sl=sl, bidO=bidO, atsl=atsl: e.tensor_scalar(
                atsl, psf[bidO][:, 0:KH], numden[:, 2 * sl + 1:2 * sl + 2], None,
                op0=ALU.mult), waits=[vval()])
            va = vval()
            yield
            P.emit("tensor", lambda e, sl=sl, atsl=atsl: e.transpose(
                psb[0][0:64, 384 + 128 * sl:384 + 128 * sl + 128] if False else
                psb[1][0:64, 256 + 128 * sl:256 + 128 * sl + 128],
                atsl, ident_b[:]), waits=[va] + bfree(7), inc=sm["tensor"])
            tva = tval()
            evac("vector", 7, lambda e, tc=tc, sl=sl: e.tensor_copy(
                att_acc[:, sl * T + tc:sl * T + tc + 128],
                psb[1][0:64, 256 + 128 * sl:256 + 128 * sl + 128]),
                waits=[tva] + ([SL[(sl, "atfree")]] if (sl, "atfree") in SL else []))

        def zipper(g0, g1):
            a0, a1 = True, True
            while a0 or a1:
                if a0:
                    a0 = next(g0, "end") != "end"
                if a1:
                    a1 = next(g1, "end") != "end"

        for hp in range(HN // 2):
            fence(dma=False)
            ld_head(0, 2 * hp)
            ld_head(1, 2 * hp + 1)
            for c in range(NCH - 1, 0, -1):
                zipper(bwd_step(0, 2 * hp, c), bwd_step(1, 2 * hp + 1, c))
            P.wait("tensor", vval())
            P.wait("scalar", vval())
            for c in range(NCH):
                zipper(fwd_step(0, 2 * hp, c), fwd_step(1, 2 * hp + 1, c))
            vA = vval()
            for sl in (0, 1):
                h = 2 * hp + sl
                SL[(sl, "atfree")] = store(
                    s_attnfm[h * KH:(h + 1) * KH, :], att_acc[:, sl * T:(sl + 1) * T],
                    waits=[vA])
            for key in ("kfree", "qtfree", "dfree"):
                SL.pop((0, key), None)
                SL.pop((1, key), None)

    phase_a2()
    fence()
    if upto == "pa2":
        return _emit(nc, P, es, din, douts)

    P.scope("p_a3a")
    ow_bf = A2("ow_bf", (128, 8 * 1024), BF16)
    attn_t = A2("attn_t", (128, 2 * 8 * 128), BF16)
    xm_t = A2("xm_t", (128, 2 * 1024))
    z3_bf = A2("z3_bf", (128, 2 * 1024), BF16)
    xn0_t = A2("xn0_t", (128, 2 * 1024))
    stg_a3 = A2("stg_a3", (128, 2 * 1024), BF16)
    for kc in range(8):
        wld = load(f32stage[:, 0:1024], din["o_w"][128 * kc:128 * (kc + 1), :], q="scalar")
        P.emit("vector", lambda e, kc=kc: e.tensor_copy(
            ow_bf[:, 1024 * kc:1024 * (kc + 1)], f32stage[:, 0:1024]),
            waits=[wld], inc=sm["vector"])
        P.wait("scalar", vval())
    fence(dma=False)

    def phase_a3a():
        xm_st = [None, None]
        z3_st = [None, None]
        at_guard = [None, None]
        xn_g = [None, None]
        ztg3 = [None, None]
        for tt in range(T // 128):
            t0 = tt * 128
            sl2 = tt % 2
            ao = 1024 * sl2
            xo = 1024 * sl2
            wld = load(xn0_t[:, xo:xo + 1024], s_xn0[t0:t0 + 128, :],
                       waits=[xn_g[sl2]] if xn_g[sl2] else ())
            wat = load(attn_t[:, ao:ao + 1024].rearrange("p (kc t) -> p kc t", t=128),
                       s_attnfm[:, t0:t0 + 128].rearrange("(kc p) t -> p kc t", p=128),
                       waits=[at_guard[sl2]] if at_guard[sl2] else ())
            for kc in range(8):
                for nb2 in range(2):
                    mm(psf[nb2][:, 0:512], nb2,
                       attn_t[:, ao + 128 * kc:ao + 128 * (kc + 1)],
                       ow_bf[:, 1024 * kc + 512 * nb2:1024 * kc + 512 * (nb2 + 1)],
                       kc == 0, kc == 7,
                       waits=[wat] if (kc == 0 and nb2 == 0) else ())
            at_guard[sl2] = tval()
            tv = tval()
            for nb2 in range(2):
                sl = slice(xo + 512 * nb2, xo + 512 * (nb2 + 1))
                so = slice(512 * nb2, 512 * (nb2 + 1))
                evac("vector", nb2, lambda e, ps=psf[nb2], sl=sl, so=so: e.tensor_add(
                    xm_t[:, sl], ps[:, 0:512], ob_bc[:, so]),
                    waits=[tv, wld] + ([xm_st[sl2]] if (nb2 == 0 and xm_st[sl2])
                                       else []))
                P.emit("vector", lambda e, sl=sl: e.tensor_add(
                    xm_t[:, sl], xm_t[:, sl], xn0_t[:, sl]), waits=[vval()],
                    inc=sm["vector"])
            xn_g[sl2] = vval()
            vXM = vval()
            xm_st[sl2] = store(s_xmid[t0:t0 + 128, :], xm_t[:, xo:xo + 1024],
                               waits=[vXM])
            layer_norm(z3_bf[:, xo:xo + 1024], xm_t[:, xo:xo + 1024], 1024,
                       waits=[vXM] + ([ztg3[sl2]] if ztg3[sl2] else []))
            vZ3 = vval()
            for kc in range(8):
                bid = 6 + kc % 2
                P.emit("tensor", lambda e, kc=kc, xo=xo: e.transpose(
                    psb[kc % 2][:, 0:128], z3_bf[:, xo + 128 * kc:xo + 128 * (kc + 1)],
                    ident_b[:]),
                    waits=[vZ3] + bfree(bid), inc=sm["tensor"])
                tv = tval()
                evac("vector", bid, lambda e, kc=kc, xo=xo: e.tensor_copy(
                    stg_a3[:, xo + 128 * kc:xo + 128 * (kc + 1)],
                    psb[kc % 2][:, 0:128]),
                    waits=[tv] + ([z3_st[sl2]] if z3_st[sl2] and kc == 0 else []))
            ztg3[sl2] = tval()
            vv = vval()
            z3_st[sl2] = store(
                s_z3fm.rearrange("(kc p) t -> p kc t", p=128)[:, :, t0:t0 + 128],
                stg_a3[:, xo:xo + 1024].rearrange("p (kc j) -> p kc j", j=128),
                waits=[vv])

    phase_a3a()
    es_a2.close()
    fence()
    if upto == "pa3a":
        return _emit(nc, P, es, din, douts)

    P.scope("p_a3b")
    es_f = ExitStack()
    A3 = lambda n, s, d=F32: sbR(es_f, n, s, d)
    fw_sb = A3("fw_sb", (128, 8 * FFH), BF16)
    zsp_fm = A3("zsp_fm", (128, 8 * 2048), BF16)
    h1stg = [A3(f"h1stg{i}", (128, 2048), BF16) for i in range(2)]
    h1_st = [None, None]
    wF1l = [load(fw_sb[:, FFH * kc:FFH * (kc + 1)], s_f1bf[128 * kc:128 * (kc + 1), :],
                 q="scalar") for kc in range(8)]
    P.wait("tensor", *wF1l)

    def phase_a3b():
        SPB = 2048
        for spn in range(T // SPB):
            t0 = spn * SPB
            efence("tensor", "vector", "scalar", "sync")
            wz = [load(zsp_fm[:, SPB * kc:SPB * (kc + 1)],
                       s_z3fm[128 * kc:128 * (kc + 1), t0:t0 + SPB]) for kc in range(8)]
            for mc in range(FFH // 128):
                for kc in range(8):
                    for nt in range(4):
                        mm(psf[nt][:, 0:512], nt,
                           fw_sb[:, FFH * kc + 128 * mc:FFH * kc + 128 * (mc + 1)],
                           zsp_fm[:, SPB * kc + 512 * nt:SPB * kc + 512 * (nt + 1)],
                           kc == 0, kc == 7,
                           waits=wz if (kc == 0 and mc == 0 and nt == 0) else ())
                tv = tval()
                sl = mc % 2
                for nt in range(4):
                    evac("scalar", nt, lambda e, nt=nt, sl=sl, mc=mc: e.activation(
                        h1stg[sl][:, 512 * nt:512 * (nt + 1)], psf[nt][:, 0:512],
                        AF.Gelu, bias=f1bc[:, mc:mc + 1]),
                        waits=[tv] + ([h1_st[sl]] if (nt == 0 and h1_st[sl]) else []))
                sv = sval()
                h1_st[sl] = store(s_h1fm[128 * mc:128 * (mc + 1), t0:t0 + SPB],
                                  h1stg[sl][:, 0:SPB], waits=[sv])

    phase_a3b()
    fence()

    P.scope("p_a3c")
    f2_sb = fw_sb
    h1t = A3("h1t", (128, 32 * 128), BF16)
    xin_t = A3("xin_t", (128, 1024))
    xm2_t = A3("xm2_t", (128, 1024))
    for kc in range(32):
        wld = load(f32stage[:, 0:1024], din["f2_w"][128 * kc:128 * (kc + 1), :], q="scalar")
        P.emit("vector", lambda e, kc=kc: e.tensor_copy(
            f2_sb[:, 1024 * kc:1024 * (kc + 1)], f32stage[:, 0:1024]), waits=[wld],
            inc=sm["vector"])
        P.wait("scalar", vval())
    fence(dma=False)

    def phase_a3c():
        x1_st = None
        for tt in range(T // 128):
            t0 = tt * 128
            efence("tensor", "vector", "scalar", "sync")
            wh = load(h1t[:], s_h1fm.rearrange("(kc p) t -> p kc t", p=128)[:, :, t0:t0 + 128])
            wx = load(xin_t[:], x_in[t0:t0 + 128, :])
            wxm = load(xm2_t[:], s_xmid[t0:t0 + 128, :],
                       waits=[x1_st] if x1_st else ())
            for kc in range(32):
                for nb2 in range(2):
                    mm(psf[nb2][:, 0:512], nb2, h1t[:, 128 * kc:128 * (kc + 1)],
                       f2_sb[:, 1024 * kc + 512 * nb2:1024 * kc + 512 * (nb2 + 1)],
                       kc == 0, kc == 31,
                       waits=[wh] if (kc == 0 and nb2 == 0) else ())
            tv = tval()
            for nb2 in range(2):
                sl = slice(512 * nb2, 512 * (nb2 + 1))
                evac("vector", nb2, lambda e, ps=psf[nb2], sl=sl: e.tensor_add(
                    xm2_t[:, sl], xm2_t[:, sl], ps[:, 0:512]),
                    waits=[tv, wxm] + ([x1_st] if (nb2 == 0 and x1_st) else []))
                P.emit("vector", lambda e, sl=sl: e.tensor_add(
                    xm2_t[:, sl], xm2_t[:, sl], f2b_bc[:, sl]), waits=[vval()])
                P.emit("vector", lambda e, sl=sl: e.tensor_add(
                    xm2_t[:, sl], xm2_t[:, sl], xin_t[:, sl]), waits=[wx, vval()],
                    inc=sm["vector"])
            vX1 = vval()
            x1_st = store(s_x1[t0:t0 + 128, :], xm2_t[:], waits=[vX1])

    phase_a3c()
    es_f.close()
    es_lion.close()
    fence()
    if upto == "pa3":
        return _emit(nc, P, es, din, douts)

    def do_ffn(src_d, dst_d, tag):
        es1 = ExitStack()
        es_gu = ExitStack()
        FF = lambda n, s, d=F32: sbR(es1, n + tag, s, d)
        GU = lambda n, s, d=F32: sbR(es_gu, n + tag, s, d)
        xsp = FF("xsp", (128, 16 * 1024))
        zsp = FF("zsp", (128, 2 * 1024), BF16)
        xg = [None] * 16
        ztg = [None, None]
        gstmp = FF("gs", (128, 2048))
        h2stg = [FF(f"h2stg{i}", (128, 2048), BF16) for i in range(2)]
        wgu_sb = GU("wgu", (128, 2 * 2 * 1024), BF16)
        zfm = GU("zfm", (128, 2 * 8 * 2048), BF16)
        h2_st = [None, None]
        h2_prev_v = None
        wgu_ld = [None, None]
        wgu_tg = [None, None]

        def ld_wgu(mc):
            sl2 = mc % 2
            o = 2048 * sl2
            w1 = load(wgu_sb[:, o:o + 1024].rearrange("p (kc m) -> p kc m", m=128),
                      s_wgT.rearrange("(kc p) m -> p kc m", p=128)[
                          :, :, 128 * (mc % 24):128 * (mc % 24) + 128],
                      q="scalar", waits=[wgu_tg[sl2]] if wgu_tg[sl2] else ())
            w2 = load(wgu_sb[:, o + 1024:o + 2048].rearrange("p (kc m) -> p kc m", m=128),
                      s_wuT.rearrange("(kc p) m -> p kc m", p=128)[
                          :, :, 128 * (mc % 24):128 * (mc % 24) + 128],
                      q="scalar")
            wgu_ld[sl2] = [w1, w2]

        ld_wgu(0)
        SPB = 2048
        NT = SPB // 128
        xl = [None] * NT
        vZF_s = [None, None]

        def ffn_ln_span(spn):
            t0 = spn * SPB
            zo = (spn % 2) * 8 * SPB
            for tt in range(NT):
                xl[tt] = load(xsp[:, 1024 * tt:1024 * (tt + 1)],
                              src_d[t0 + 128 * tt:t0 + 128 * (tt + 1), :],
                              waits=[xg[tt]] if xg[tt] else ())
            norm = layer_norm_span(
                lambda i: xsp[:, 1024 * i:1024 * (i + 1)],
                NT, 1024, lambda i: [xl[i]])
            for tt in range(NT):
                zs = tt % 2
                lnv = norm(tt, zsp[:, 1024 * zs:1024 * (zs + 1)],
                           waits=[ztg[zs]] if ztg[zs] else ())
                xg[tt] = lnv
                for kc in range(8):
                    bid = 6 + kc % 2
                    P.emit("tensor", lambda e, kc=kc, zs=zs: e.transpose(
                        psb[kc % 2][:, 0:128],
                        zsp[:, 1024 * zs + 128 * kc:1024 * zs + 128 * (kc + 1)],
                        ident_b[:]), waits=[lnv] + bfree(bid), inc=sm["tensor"])
                    tv = tval()
                    evac("vector", bid, lambda e, kc=kc, tt=tt, zo=zo: e.tensor_copy(
                        zfm[:, zo + SPB * kc + 128 * tt:zo + SPB * kc + 128 * (tt + 1)],
                        psb[kc % 2][:, 0:128]), waits=[tv])
                ztg[zs] = tval()
            vZF_s[spn % 2] = vval()

        ffn_ln_span(0)
        for spn in range(T // SPB):
            t0 = spn * SPB
            zo = (spn % 2) * 8 * SPB
            vZF = vZF_s[spn % 2]
            if spn + 1 < T // SPB:
                ffn_ln_span(spn + 1)
            for mc in range(FFI // 128):
                sl2 = mc % 2
                o2 = 2048 * sl2
                wld2 = wgu_ld[sl2]
                if spn * 24 + mc + 1 < 48:
                    ld_wgu(mc + 1)
                for kc in range(8):
                    for nt in range(4):
                        mm(psf[nt][:, 0:512], nt,
                           wgu_sb[:, o2 + 128 * kc:o2 + 128 * (kc + 1)],
                           zfm[:, zo + SPB * kc + 512 * nt:zo + SPB * kc + 512 * (nt + 1)],
                           kc == 0, kc == 7,
                           waits=([vZF] + wld2) if (kc == 0 and nt == 0) else ())
                for nt in range(4):
                    evac("scalar", nt, lambda e, nt=nt: e.activation(
                        gstmp[:, 512 * nt:512 * (nt + 1)], psf[nt][:, 0:512],
                        AF.Silu, scale=QS),
                        waits=[tval()] + ([h2_prev_v] if h2_prev_v else []))
                svG = sval()
                for kc in range(8):
                    for nt in range(4):
                        mm(psf[nt][:, 0:512], nt,
                           wgu_sb[:, o2 + 1024 + 128 * kc:o2 + 1024 + 128 * (kc + 1)],
                           zfm[:, zo + SPB * kc + 512 * nt:zo + SPB * kc + 512 * (nt + 1)],
                           kc == 0, kc == 7)
                tv = tval()
                wgu_tg[sl2] = tv
                sl = mc % 2
                for nt in range(4):
                    evac("vector", nt, lambda e, nt=nt, sl=sl: e.tensor_tensor(
                        h2stg[sl][:, 512 * nt:512 * (nt + 1)],
                        gstmp[:, 512 * nt:512 * (nt + 1)], psf[nt][:, 0:512],
                        op=ALU.mult),
                        waits=[tv, svG] + ([h2_st[sl]] if (nt == 0 and h2_st[sl])
                                           else []))
                vv = vval()
                h2_prev_v = vv
                h2_st[sl] = store(s_h2fm[128 * mc:128 * (mc + 1), t0:t0 + SPB],
                                  h2stg[sl][:, 0:SPB], waits=[vv])
        efence("tensor", "vector", "scalar", "sync")
        es_gu.close()
        wd_sb = FF("wd", (128, 24 * 1024), BF16)
        h2t = FF("h2t", (128, 24 * 128), BF16)
        xdn = FF("xdn", (128, 1024))
        wd_l = [load(wd_sb[:, 1024 * kc:1024 * (kc + 1)],
                     s_wdT[128 * kc:128 * (kc + 1), :], q="scalar")
                for kc in range(24)]
        P.wait("tensor", *wd_l)
        s2c = QS * QS
        xo_st = None
        for tt in range(T // 128):
            t0 = tt * 128
            efence("tensor", "vector", "scalar", "sync")
            wh = load(h2t[:], s_h2fm.rearrange("(kc p) t -> p kc t", p=128)[:, :, t0:t0 + 128])
            wx = load(xdn[:], src_d[t0:t0 + 128, :],
                      waits=[xo_st] if xo_st else ())
            for kc in range(24):
                for nb2 in range(2):
                    mm(psf[4 + nb2][:, 0:512], 4 + nb2, h2t[:, 128 * kc:128 * (kc + 1)],
                       wd_sb[:, 1024 * kc + 512 * nb2:1024 * kc + 512 * (nb2 + 1)],
                       kc == 0, kc == 23, waits=[wh] if (kc == 0 and nb2 == 0) else ())
            tv = tval()
            for nb2 in range(2):
                sl = slice(512 * nb2, 512 * (nb2 + 1))
                evac("vector", 4 + nb2, lambda e, ps=psf[4 + nb2], sl=sl:
                     e.scalar_tensor_tensor(
                         xdn[:, sl], ps[:, 0:512], s2c, xdn[:, sl],
                         op0=ALU.mult, op1=ALU.add),
                    waits=[tv, wx] + ([xo_st] if (nb2 == 0 and xo_st) else []))
            vXO = vval()
            xo_st = store(dst_d[t0:t0 + 128, :], xdn[:], waits=[vXO])
        es1.close()
        fence()

    P.scope("p_f1")
    do_ffn(s_x1, s_x2, "f1")
    if upto == "pf1":
        return _emit(nc, P, es, din, douts)

    P.scope("p_b1")
    es_b1 = ExitStack()
    BB = lambda n, s, d=F32: sbR(es_b1, n, s, d)
    SPB1 = 512
    win_full = BB("win_full", (128, 8 * 4096), BF16)
    zb_fm = BB("zb_fm", (128, 2 * 8 * SPB1), BF16)
    xb_sp = BB("xb_sp", (128, 4 * 1024))
    zb_sp = BB("zb_sp", (128, 2 * 1024), BF16)
    inner_sb = BB("inner_sb", (128, 16 * SPB1))
    carry2 = BB("carry2", (128, 16 * 2))
    hconv = BB("hconv", (128, 2 * SPB1))
    hconv_bf = BB("hconv_bf", (128, 2 * SPB1), BF16)
    sg_stg = BB("sg_stg", (128, SPB1))
    win_l = [load(win_full[:, 4096 * kc:4096 * (kc + 1)],
                  s_winbf[128 * kc:128 * (kc + 1), :], q="scalar")
             for kc in range(8)]
    P.wait("tensor", *win_l)
    P.emit("vector", lambda e: e.memset(carry2[:], 0.0), inc=sm["vector"])

    def phase_b1():
        SPB = SPB1
        sg_done = None
        xg_b = [None] * 4
        ztg_b = [None, None]
        zfm_g = [None, None]
        vZF_b = [None, None]
        hc_g = [[], []]
        conv_g = [None]

        def b1_ln_span(spn):
            t0 = spn * SPB
            zo = (spn % 2) * 8 * SPB
            xl = [None] * 4
            for tt in range(4):
                xl[tt] = load(xb_sp[:, 1024 * tt:1024 * (tt + 1)],
                              s_x2[t0 + 128 * tt:t0 + 128 * (tt + 1), :],
                              waits=[xg_b[tt]] if xg_b[tt] else ())
            norm = layer_norm_span(
                lambda i: xb_sp[:, 1024 * i:1024 * (i + 1)],
                4, 1024, lambda i: [xl[i]])
            for tt in range(4):
                zs = tt % 2
                lnv = norm(tt, zb_sp[:, 1024 * zs:1024 * (zs + 1)],
                           waits=[ztg_b[zs]] if ztg_b[zs] else ())
                xg_b[tt] = lnv
                for kc in range(8):
                    bid = 6 + kc % 2
                    P.emit("tensor", lambda e, kc=kc, zs=zs: e.transpose(
                        psb[kc % 2][:, 0:128],
                        zb_sp[:, 1024 * zs + 128 * kc:1024 * zs + 128 * (kc + 1)],
                        ident_b[:]), waits=[lnv] + bfree(bid), inc=sm["tensor"])
                    tv = tval()
                    evac("vector", bid, lambda e, kc=kc, tt=tt, zo=zo: e.tensor_copy(
                        zb_fm[:, zo + SPB * kc + 128 * tt:zo + SPB * kc + 128 * (tt + 1)],
                        psb[kc % 2][:, 0:128]),
                        waits=[tv] + ([zfm_g[spn % 2]] if (tt == 0 and kc == 0
                                                           and zfm_g[spn % 2]) else []))
                ztg_b[zs] = tval()
            vZF_b[spn % 2] = vval()

        b1_ln_span(0)
        for spn in range(T // SPB):
            t0 = spn * SPB
            zo = (spn % 2) * 8 * SPB
            vZF = vZF_b[spn % 2]
            for mc in range(32):
                bid = mc % 4
                ps = psf[bid]
                for kc in range(8):
                    mm(ps[:, 0:512], bid,
                       win_full[:, 4096 * kc + 128 * mc:4096 * kc + 128 * (mc + 1)],
                       zb_fm[:, zo + SPB * kc:zo + SPB * (kc + 1)],
                       kc == 0, kc == 7, waits=[vZF] if kc == 0 else ())
                tv = tval()
                if mc < 16:
                    evac("vector", bid, lambda e, mc=mc, ps=ps: e.tensor_copy(
                        inner_sb[:, SPB * mc:SPB * (mc + 1)], ps[:, 0:512]),
                        waits=[tv] + ([conv_g[0]] if (mc == 0 and conv_g[0]) else []))
                else:
                    cc = mc - 16
                    evac("scalar", bid, lambda e, ps=ps: e.activation(
                        sg_stg[:, 0:SPB], ps[:, 0:512], AF.Silu),
                        waits=[tv] + ([sg_done] if sg_done else []))
                    sv = sval()
                    sg_done = store(s_sg[128 * cc:128 * (cc + 1), t0:t0 + SPB],
                                    sg_stg[:, 0:SPB], waits=[sv])
            zfm_g[spn % 2] = tval()
            vIN = vval()
            if spn + 1 < T // SPB:
                b1_ln_span(spn + 1)
            for cc in range(16):
                isl = inner_sb[:, SPB * cc:SPB * (cc + 1)]
                sl = cc % 2
                ho = SPB * sl
                wv = ([vIN] if cc < 2 else []) + hc_g[sl]
                P.emit("vector", lambda e, cc=cc, ho=ho: e.tensor_scalar(
                    hconv[:, ho:ho + 1], carry2[:, 2 * cc:2 * cc + 1], cw0c[:, cc:cc + 1],
                    None, op0=ALU.mult), waits=wv)
                P.emit("vector", lambda e, cc=cc, ho=ho: e.scalar_tensor_tensor(
                    hconv[:, ho:ho + 1], carry2[:, 2 * cc + 1:2 * cc + 2], cw1c[:, cc:cc + 1],
                    hconv[:, ho:ho + 1], op0=ALU.mult, op1=ALU.add), waits=[vval()])
                P.emit("vector", lambda e, cc=cc, isl=isl, ho=ho: e.scalar_tensor_tensor(
                    hconv[:, ho:ho + 1], isl[:, 0:1], cw2c[:, cc:cc + 1], hconv[:, ho:ho + 1],
                    op0=ALU.mult, op1=ALU.add), waits=[vval()])
                P.emit("vector", lambda e, cc=cc, ho=ho: e.tensor_scalar(
                    hconv[:, ho + 1:ho + 2], carry2[:, 2 * cc + 1:2 * cc + 2], cw0c[:, cc:cc + 1],
                    None, op0=ALU.mult), waits=[vval()])
                P.emit("vector", lambda e, cc=cc, isl=isl, ho=ho: e.scalar_tensor_tensor(
                    hconv[:, ho + 1:ho + 2], isl[:, 0:1], cw1c[:, cc:cc + 1], hconv[:, ho + 1:ho + 2],
                    op0=ALU.mult, op1=ALU.add), waits=[vval()])
                P.emit("vector", lambda e, cc=cc, isl=isl, ho=ho: e.scalar_tensor_tensor(
                    hconv[:, ho + 1:ho + 2], isl[:, 1:2], cw2c[:, cc:cc + 1], hconv[:, ho + 1:ho + 2],
                    op0=ALU.mult, op1=ALU.add), waits=[vval()])
                Wc = SPB - 2
                P.emit("vector", lambda e, cc=cc, isl=isl, Wc=Wc, ho=ho: e.tensor_scalar(
                    hconv[:, ho + 2:ho + 2 + Wc], isl[:, 0:Wc], cw0c[:, cc:cc + 1], None,
                    op0=ALU.mult), waits=[vval()])
                P.emit("vector", lambda e, cc=cc, isl=isl, Wc=Wc, ho=ho: e.scalar_tensor_tensor(
                    hconv[:, ho + 2:ho + 2 + Wc], isl[:, 1:1 + Wc], cw1c[:, cc:cc + 1],
                    hconv[:, ho + 2:ho + 2 + Wc], op0=ALU.mult, op1=ALU.add), waits=[vval()])
                P.emit("vector", lambda e, cc=cc, isl=isl, Wc=Wc, ho=ho: e.scalar_tensor_tensor(
                    hconv[:, ho + 2:ho + 2 + Wc], isl[:, 2:2 + Wc], cw2c[:, cc:cc + 1],
                    hconv[:, ho + 2:ho + 2 + Wc], op0=ALU.mult, op1=ALU.add), waits=[vval()])
                P.emit("vector", lambda e, cc=cc, isl=isl: e.tensor_copy(
                    carry2[:, 2 * cc:2 * cc + 2], isl[:, SPB - 2:SPB]), inc=sm["vector"])
                vC = vval()
                P.emit("scalar", lambda e, cc=cc, ho=ho: e.activation(
                    hconv[:, ho:ho + SPB], hconv[:, ho:ho + SPB], AF.Silu,
                    bias=cbc[:, cc:cc + 1]), waits=[vC], inc=sm["scalar"])
                svH = sval()
                P.emit("vector", lambda e, ho=ho: e.tensor_copy(
                    hconv_bf[:, ho:ho + SPB], hconv[:, ho:ho + SPB]),
                    waits=[svH], inc=sm["vector"])
                vHB = vval()
                lo = t0 - 1
                if spn == 0:
                    h1s = store(s_hfm[128 * cc:128 * (cc + 1), 0:SPB - 1],
                                hconv[:, ho + 1:ho + SPB], waits=[svH])
                    h2s = store(s_hfm_bf[128 * cc:128 * (cc + 1), 0:SPB - 1],
                                hconv_bf[:, ho + 1:ho + SPB], waits=[vHB])
                else:
                    h1s = store(s_hfm[128 * cc:128 * (cc + 1), lo:lo + SPB],
                                hconv[:, ho:ho + SPB], waits=[svH])
                    h2s = store(s_hfm_bf[128 * cc:128 * (cc + 1), lo:lo + SPB],
                                hconv_bf[:, ho:ho + SPB], waits=[vHB])
                hc_g[sl] = [vHB, h1s, h2s]
            conv_g[0] = vval()
        fence(dma=True)
        hguard = [0]
        fc_hst = []
        for cc in range(16):
            P.emit("vector", lambda e, cc=cc: e.tensor_scalar(
                hconv[:, 0:1], carry2[:, 2 * cc:2 * cc + 1], cw0c[:, cc:cc + 1], None,
                op0=ALU.mult), waits=[sval(), (sm["vector"], hguard[0])] + fc_hst)
            P.emit("vector", lambda e, cc=cc: e.scalar_tensor_tensor(
                hconv[:, 0:1], carry2[:, 2 * cc + 1:2 * cc + 2], cw1c[:, cc:cc + 1],
                hconv[:, 0:1], op0=ALU.mult, op1=ALU.add), waits=[vval()],
                inc=sm["vector"])
            vC = vval()
            P.emit("scalar", lambda e, cc=cc: e.activation(
                hconv[:, 0:1], hconv[:, 0:1], AF.Silu, bias=cbc[:, cc:cc + 1]),
                waits=[vC], inc=sm["scalar"])
            svH = sval()
            P.emit("vector", lambda e: e.tensor_copy(hconv_bf[:, 0:1], hconv[:, 0:1]),
                   waits=[svH], inc=sm["vector"])
            vHB = vval()
            hguard[0] = vHB[1]
            f1s = store(s_hfm[128 * cc:128 * (cc + 1), T - 1:T], hconv[:, 0:1],
                        waits=[svH], slow=True)
            f2s = store(s_hfm_bf[128 * cc:128 * (cc + 1), T - 1:T], hconv_bf[:, 0:1],
                        waits=[vHB], slow=True)
            fc_hst = [f1s, f2s]

    phase_b1()
    fence()
    es_b1.close()
    if upto == "pb1":
        return _emit(nc, P, es, din, douts)

    P.scope("p_b23")
    es_b2 = ExitStack()
    B2 = lambda n, s, d=F32: sbR(es_b2, n, s, d)
    wt_tile = B2("wt_tile", (128, 2048), BF16)
    wst_sb = B2("wst_sb", (128, 16 * 32), BF16)
    wout_sb = B2("wout_sb", (128, 16 * 512), BF16)
    hbf_sp = B2("hbf_sp", (128, 16 * 512), BF16)
    dt_sb = B2("dt_sb", (128, 8 * 512))
    bbc_sb = B2("bbc_sb", (128, 16 * 512))
    sp_sb = B2("sp_sb", (32, 512))
    bbrows = B2("bbrows", (1, 2 * 512))
    hf_sp = B2("hf_sp", (128, 2 * 512))
    sg_sp = B2("sg_sp", (128, 2 * 512))
    g_sb = B2("g_sb", (128, 2 * 512))
    y_sb = B2("y_sb", (128, 2 * 512))
    dA_sb = B2("dA_sb", (128, 2 * 1026))
    gB_sb = B2("gB_sb", (128, 2 * 1026))
    mem_sb = B2("mem_sb", (128, 2 * 1026))
    x2t = B2("x2t", (128, 512))
    phtw = B2("phtw", (128, 4 * 512))
    scarry = B2("scarry", (128, 256))
    yg_bf = B2("yg_bf", (128, 16 * 512), BF16)
    wws = load(f32stage[:, 0:512].rearrange("p (kc s) -> p kc s", s=32),
               din["W_state"].rearrange("(kc p) s -> p kc s", p=128))
    P.emit("vector", lambda e: e.tensor_copy(wst_sb[:], f32stage[:, 0:512]), waits=[wws],
           inc=sm["vector"])
    P.emit("vector", lambda e: e.memset(scarry[:], 0.0), inc=sm["vector"])
    for off in (512, 1025, 1026 + 512, 1026 + 1025):
        P.emit("vector", lambda e, off=off: e.memset(dA_sb[:, off:off + 1], 0.0),
               inc=sm["vector"])
    for off in (1025, 1026 + 1025):
        P.emit("vector", lambda e, off=off: e.memset(gB_sb[:, off:off + 1], 0.0),
               inc=sm["vector"])
    P.wait("tensor", vval())

    def phase_b23():
        x3_st = None
        vx2guard = [None]
        last_scan = [None, None]
        last_scan2 = [None, None]
        last_acc = [None, None]
        last_carry = [None, None]
        last_gB = [None, None]
        last_yg = [None, None]
        vG = [None, None]
        yinit = [None, None]
        hfree = [None, None]
        sgfree = [None, None]
        carry_ready = {}
        bb_t = [0, 0]
        wsg_c = [None, None]
        wt_guard = [None]
        dt_ready = [None] * 16
        yg_guard = [None]
        dtadd_g = [None, None]
        dt_free = {}
        hbf_guard = [None]
        for spn in range(T // 512):
            t0 = spn * 512
            whb = load(hbf_sp[:],
                       s_hfm_bf.rearrange("(kc p) t -> p kc t", p=128)[:, :, t0:t0 + 512],
                       waits=[hbf_guard[0]] if hbf_guard[0] else ())
            for kc in range(16):
                mm(psf[4][0:32, 0:512], 4, wst_sb[:, 32 * kc:32 * (kc + 1)],
                   hbf_sp[:, 512 * kc:512 * (kc + 1)], kc == 0, kc == 15,
                   waits=[whb] if kc == 0 else ())
            tv = tval()
            evac("vector", 4, lambda e: e.tensor_copy(sp_sb[:], psf[4][0:32, 0:512]),
                 waits=[tv])
            vSP = vval()
            bb_tv = [None] * 16
            for s_i in range(16):
                bsl2 = bbrows[0:1, 512 * (s_i % 2):512 * (s_i % 2 + 1)]
                wBR = load(bsl2, sp_sb[16 + s_i:17 + s_i, :],
                           waits=[vSP] + ([(sm["tensor"], bb_t[s_i % 2])]
                                          if bb_t[s_i % 2] else []))
                bid5 = 4 + (s_i % 2)
                mm(psf[bid5][:, 0:512], bid5, ones_row[:], bsl2, True, True,
                   waits=[wBR])
                bb_t[s_i % 2] = sm["tensor"].n
                bb_tv[s_i] = tval()
                if s_i >= 1:
                    j = s_i - 1
                    evac("vector", 4 + (j % 2), lambda e, j=j: e.tensor_copy(
                        bbc_sb[:, 512 * j:512 * (j + 1)],
                        psf[4 + (j % 2)][:, 0:512]), waits=[bb_tv[j]])
            evac("vector", 5, lambda e: e.tensor_copy(
                bbc_sb[:, 512 * 15:512 * 16], psf[5][:, 0:512]), waits=[bb_tv[15]])
            vBB = vval()

            wt_sv = [None] * 16
            wt_tv = [None] * 16

            def emit_wtmm(cc):
                wt_ld = load(wt_tile[:], s_wtt[:, 2048 * cc:2048 * (cc + 1)], q="scalar",
                             waits=[wt_guard[0]] if wt_guard[0] else ())
                bid = cc % 4
                ps = psf[bid]
                for kc in range(16):
                    mm(ps[:, 0:512], bid,
                       wt_tile[:, 128 * kc:128 * (kc + 1)],
                       hbf_sp[:, 512 * kc:512 * (kc + 1)], kc == 0, kc == 15,
                       waits=[whb, wt_ld] if kc == 0 else ())
                wt_guard[0] = tval()
                if cc == 15:
                    hbf_guard[0] = tval()
                wt_tv[cc] = tval()

            def emit_wtact(cc):
                bid = cc % 4
                ps = psf[bid]
                po = 1024 * (cc % 2)
                P.emit("scalar", lambda e, cc=cc, ps=ps, po=po: e.activation(
                    phtw[:, po:po + 512], ps[:, 0:512], AF.Abs,
                    bias=btimec[:, cc:cc + 1]),
                    waits=[wt_tv[cc]] + ([dtadd_g[cc % 2]] if dtadd_g[cc % 2] else []),
                    inc=sm["scalar"])
                P.emit("scalar", lambda e, po=po: e.activation(
                    phtw[:, po:po + 512], phtw[:, po:po + 512], AF.Exp, scale=-1.0),
                    waits=[sval()])
                P.emit("scalar", lambda e, po=po: e.activation(
                    phtw[:, po:po + 512], phtw[:, po:po + 512], AF.Ln, bias=1.0),
                    waits=[sval()])
                evac("scalar", bid, lambda e, cc=cc, ps=ps, po=po: e.activation(
                    phtw[:, po + 512:po + 1024], ps[:, 0:512], AF.Relu,
                    bias=btimec[:, cc:cc + 1]))
                wt_sv[cc] = sval()

            def emit_wtadd(cc):
                po = 1024 * (cc % 2)
                sl8 = cc % 8
                P.emit("vector", lambda e, sl8=sl8, po=po: e.tensor_add(
                    dt_sb[:, 512 * sl8:512 * (sl8 + 1)], phtw[:, po:po + 512],
                    phtw[:, po + 512:po + 1024]),
                    waits=[wt_sv[cc]] + (dt_free.get((cc + 8) % 16) or []),
                    inc=sm["vector"])
                dt_ready[cc] = vval()
                dtadd_g[cc % 2] = dt_ready[cc]

            for cc4 in range(4):
                emit_wtmm(cc4)
            emit_wtact(0)
            emit_wtact(1)
            emit_wtadd(0)
            emit_wtadd(1)
            emit_wtact(2)
            emit_wtact(3)
            for k in range(8):
                if 2 * k + 4 < 16:
                    emit_wtmm(2 * k + 4)
                if 2 * k + 5 < 16:
                    emit_wtmm(2 * k + 5)
                for ch in (0, 1):
                    cc = 2 * k + ch
                    hsl = hf_sp[:, 512 * ch:512 * (ch + 1)]
                    sgsl = sg_sp[:, 512 * ch:512 * (ch + 1)]
                    gsl = g_sb[:, 512 * ch:512 * (ch + 1)]
                    ysl = y_sb[:, 512 * ch:512 * (ch + 1)]
                    whf = load(hsl, s_hfm[128 * cc:128 * (cc + 1), t0:t0 + 512],
                               waits=hfree[ch] or ())
                    wsg_c[ch] = load(sgsl, s_sg[128 * cc:128 * (cc + 1), t0:t0 + 512],
                                     waits=[sgfree[ch]] if sgfree[ch] else ())
                    P.emit("vector", lambda e, sl8=cc % 8, hsl=hsl, gsl=gsl:
                           e.tensor_tensor(
                        gsl, dt_sb[:, 512 * sl8:512 * (sl8 + 1)], hsl, op=ALU.mult),
                        waits=[whf, dt_ready[cc]] + ([last_gB[ch]] if last_gB[ch] else []),
                        inc=sm["vector"])
                    vG[ch] = vval()
                    P.emit("scalar", lambda e, cc=cc, hsl=hsl, ysl=ysl: e.activation(
                        ysl, hsl, AF.Copy, scale=skipc[:, cc:cc + 1]),
                        waits=[whf] + ([last_yg[ch]] if last_yg[ch] else []),
                        inc=sm["scalar"])
                    yinit[ch] = sval()
                if 2 * k + 2 < 16:
                    emit_wtadd(2 * k + 2)
                if 2 * k + 3 < 16:
                    emit_wtadd(2 * k + 3)
                for s_i in range(0, 16, 2):
                    CH = (0, 1)
                    fi_c = [2 * k * 16 + s_i, (2 * k + 1) * 16 + s_i]
                    dsl_c = [dA_sb[:, 1026 * ch:1026 * (ch + 1)] for ch in CH]
                    bsl_c = [gB_sb[:, 1026 * ch:1026 * (ch + 1)] for ch in CH]
                    msl_c = [mem_sb[:, 1026 * ch:1026 * (ch + 1)] for ch in CH]
                    ysl_c = [y_sb[:, 512 * ch:512 * (ch + 1)] for ch in CH]
                    svA_c = [None, None]
                    vgB_c = [None, None]
                    for ch in CH:
                        cc = 2 * k + ch
                        dsl = dsl_c[ch]
                        w_s = [dt_ready[cc]] + ([last_scan[ch]] if last_scan[ch] else [])
                        P.emit("scalar", lambda e, cc=cc, s_i=s_i, dsl=dsl,
                               sl8=cc % 8: e.activation(
                            dsl[:, 0:512], dt_sb[:, 512 * sl8:512 * (sl8 + 1)], AF.Exp,
                            scale=A_sb[:, 16 * cc + s_i:16 * cc + s_i + 1]),
                            waits=w_s, inc=sm["scalar"])
                        P.emit("scalar", lambda e, cc=cc, s_i=s_i, dsl=dsl,
                               sl8=cc % 8: e.activation(
                            dsl[:, 513:1025], dt_sb[:, 512 * sl8:512 * (sl8 + 1)], AF.Exp,
                            scale=A_sb[:, 16 * cc + s_i + 1:16 * cc + s_i + 2]),
                            inc=sm["scalar"])
                        svA_c[ch] = sval()
                    for ch in CH:
                        fi = fi_c[ch]
                        bsl = bsl_c[ch]
                        P.emit("vector", lambda e, fi=fi, bsl=bsl: e.tensor_copy(
                            bsl[:, 512:513], scarry[:, fi + 1:fi + 2]),
                            waits=([carry_ready[fi + 1]] if fi + 1 in carry_ready
                                   else []) +
                                  ([last_scan[ch]] if last_scan[ch] else []),
                            inc=sm["vector"])
                    for ch in CH:
                        gsl = g_sb[:, 512 * ch:512 * (ch + 1)]
                        bsl = bsl_c[ch]
                        grep2 = gsl.rearrange("p (one t) -> p one t", one=1
                                              ).broadcast_to((128, 2, 512))
                        w_g = [vG[ch], vBB] + ([last_scan[ch]] if last_scan[ch] else [])
                        P.emit("vector", lambda e, s_i=s_i, bsl=bsl, grep2=grep2:
                               e.tensor_tensor(
                                   bsl[:, 0:1026].rearrange("p (r t) -> p r t",
                                                            t=513)[:, :, 0:512],
                                   grep2,
                                   bbc_sb[:, 512 * s_i:512 * (s_i + 2)].rearrange(
                                       "p (r t) -> p r t", t=512),
                                   op=ALU.mult),
                               waits=w_g, inc=sm["vector"])
                        vgB_c[ch] = vval()
                    for ch in CH:
                        fi = fi_c[ch]
                        w_sc = [svA_c[ch], vgB_c[ch]]
                        if last_acc[ch]:
                            w_sc.append(last_acc[ch])
                        if last_carry[ch]:
                            w_sc.append(last_carry[ch])
                        if fi in carry_ready:
                            w_sc.append(carry_ready[fi])
                        P.emit("vector", lambda e, fi=fi, dsl=dsl_c[ch], bsl=bsl_c[ch],
                               msl=msl_c[ch]: e.tensor_tensor_scan(
                                   msl, dsl, bsl, scarry[:, fi:fi + 1],
                                   op0=ALU.mult, op1=ALU.add),
                               waits=w_sc, inc=sm["vector"])
                        last_scan2[ch] = last_scan[ch]
                        last_scan[ch] = vval()
                    ya1 = [None, None]
                    for ch in CH:
                        fi = fi_c[ch]
                        msl = msl_c[ch]
                        P.emit("vector", lambda e, fi=fi, msl=msl: e.tensor_copy(
                            scarry[:, fi:fi + 2].rearrange("p (r one) -> p r one", one=1),
                            msl[:, 0:1026].rearrange("p (r t) -> p r t",
                                                     t=513)[:, :, 511:512]),
                            waits=[last_scan[ch]], inc=sm["vector"])
                        last_carry[ch] = vval()
                        carry_ready[fi] = last_carry[ch]
                        carry_ready[fi + 1] = last_carry[ch]
                        P.emit("vector", lambda e, msl=msl, ysl=ysl_c[ch]: e.tensor_tensor(
                            ysl, ysl, msl[:, 0:512], op=ALU.add),
                            waits=[last_scan[ch]], inc=sm["vector"])
                        ya1[ch] = vval()
                    for ch in CH:
                        P.emit("vector", lambda e, msl=msl_c[ch], ysl=ysl_c[ch]:
                               e.tensor_tensor(ysl, ysl, msl[:, 513:1025], op=ALU.add),
                               waits=[ya1[ch]], inc=sm["vector"])
                        last_acc[ch] = vval()
                for ch in (0, 1):
                    cc = 2 * k + ch
                    dt_free[cc] = [sval(), vG[ch]]
                    sgsl = sg_sp[:, 512 * ch:512 * (ch + 1)]
                    ysl = y_sb[:, 512 * ch:512 * (ch + 1)]
                    P.emit("vector", lambda e, cc=cc, sgsl=sgsl, ysl=ysl: e.tensor_tensor(
                        yg_bf[:, 512 * cc:512 * (cc + 1)], ysl, sgsl, op=ALU.mult),
                        waits=[last_acc[ch], wsg_c[ch], yg_guard[0]] if yg_guard[0] else
                              [last_acc[ch], wsg_c[ch]], inc=sm["vector"])
                    last_yg[ch] = vval()
                    hfree[ch] = [vG[ch], yinit[ch]]
                    sgfree[ch] = last_yg[ch]
                if 2 * k + 4 < 16:
                    emit_wtact(2 * k + 4)
                if 2 * k + 5 < 16:
                    emit_wtact(2 * k + 5)
            vYG = vval()
            for nb2 in range(2):
                sl = slice(512 * nb2, 512 * (nb2 + 1))
                wo_ld = [load(wout_sb[:, 512 * kc:512 * (kc + 1)],
                              s_woutbf[128 * kc:128 * (kc + 1),
                                       512 * nb2:512 * (nb2 + 1)],
                              q="scalar", waits=[tval()] if kc == 0 else ())
                         for kc in range(16)]
                for tt in range(4):
                    wx2 = load(x2t[:], s_x2[t0 + 128 * tt:t0 + 128 * (tt + 1), sl],
                               waits=[vx2guard[0]] if vx2guard[0] else ())
                    bid = tt % 2
                    ps = psf[bid]
                    for kc in range(16):
                        mm(ps[:, 0:512], bid,
                           yg_bf[:, 512 * kc + 128 * tt:512 * kc + 128 * (tt + 1)],
                           wout_sb[:, 512 * kc:512 * (kc + 1)],
                           kc == 0, kc == 15,
                           waits=[vYG] + wo_ld if (kc == 0 and tt == 0) else ())
                    tv = tval()
                    evac("vector", bid, lambda e, ps=ps: e.tensor_add(
                        x2t[:], ps[:, 0:512], x2t[:]),
                        waits=[tv, wx2] + ([x3_st] if x3_st else []))
                    vX3 = vval()
                    x3_st = store(s_x3[t0 + 128 * tt:t0 + 128 * (tt + 1), sl],
                                  x2t[:], waits=[vX3])
                    vx2guard[0] = x3_st
            yg_guard[0] = tval()

    phase_b23()
    es_b2.close()
    fence()
    if upto == "pb23":
        return _emit(nc, P, es, din, douts)

    P.scope("p_f2")
    do_ffn(s_x3, out_d, "f2")

    return _emit(nc, P, es, din, douts)


def _emit(nc, P, es, din, douts):
    P.wait("sync", *[(s, s.n) for nm, s in P.sems.items()
                     if nm.startswith("st_") and s.n > 0])
    block_cm = nc.Block()
    block = block_cm.__enter__()
    for eng in ENGINES:
        deco = getattr(block, eng)

        def mk(eng):
            def body(e):
                cur = [None]
                for fn in P.ops[eng]:
                    if isinstance(fn, tuple):
                        if cur[0] is not None:
                            cur[0].__exit__(None, None, None)
                        cur[0] = nc.named_scope(fn[1])
                        cur[0].__enter__()
                    else:
                        fn(e)
                if cur[0] is not None:
                    cur[0].__exit__(None, None, None)
            return body

        deco(mk(eng))
    block_cm.__exit__(None, None, None)
    es.close()
    return nc, list(din.keys()), list(douts.keys())


_CACHE = {}


def get_program(upto="all", dbg=()):
    key = (upto, tuple(sorted(dbg)))
    if key not in _CACHE:
        _CACHE[key] = build_core_program(upto, dbg)
    return _CACHE[key]


_REN = {
    "q_w": "L0_attn_a_q_w", "q_b": "L0_attn_a_q_b", "k_w": "L0_attn_a_k_w",
    "k_b": "L0_attn_a_k_b", "v_w": "L0_attn_a_v_w", "v_b": "L0_attn_a_v_b",
    "o_w": "L0_attn_a_o_w", "o_b": "L0_attn_a_o_b", "ln1_w": "L0_attn_a_ln1_w",
    "ln1_b": "L0_attn_a_ln1_b", "ln2_w": "L0_attn_a_ln2_w", "ln2_b": "L0_attn_a_ln2_b",
    "dec_w": "L0_attn_a_dec_w", "dec_b": "L0_attn_a_dec_b", "f1_w": "L0_attn_a_f1_w",
    "f1_b": "L0_attn_a_f1_b", "f2_w": "L0_attn_a_f2_w", "f2_b": "L0_attn_a_f2_b",
    "W_in": "L0_attn_b_W_in", "conv_w": "L0_attn_b_conv_w", "conv_b": "L0_attn_b_conv_b",
    "W_state": "L0_attn_b_W_state", "W_time": "L0_attn_b_W_time",
    "b_time": "L0_attn_b_b_time", "A_log": "L0_attn_b_A_log", "skip": "L0_attn_b_skip",
    "W_out": "L0_attn_b_W_out", "w_gate": "L0_ffn_w_gate", "w_up": "L0_ffn_w_up",
    "w_down": "L0_ffn_w_down",
}


def make_in_maps(inputs):
    consts = host_constants()
    x = np.ascontiguousarray(np.asarray(inputs["L0_attn_a_x"], np.float32))
    w_map = {}
    for k, src in _REN.items():
        w_map[k] = np.ascontiguousarray(np.asarray(inputs[src]))
    for k, v in consts.items():
        w_map[k] = v
    in_maps = []
    for b in range(8):
        m = dict(w_map)
        m["x"] = x[b]
        in_maps.append(m)
    return in_maps


def kernel(**inputs):
    nc, in_names, out_names = get_program()
    in_maps = make_in_maps(inputs)
    res = run_bass_kernel_spmd(nc, in_maps, core_ids=list(range(8)))
    return np.stack([np.asarray(r["out"], np.float32) for r in res.results], 0)


# version 09
# speedup vs baseline: 1.3337x; 1.0112x over previous
import numpy as np
from contextlib import ExitStack

import concourse.bass as bass
import concourse.mybir as mybir
from concourse.bass_utils import run_bass_kernel_spmd

F32 = mybir.dt.float32
BF16 = mybir.dt.bfloat16
I32 = mybir.dt.int32
AF = mybir.ActivationFunctionType
ALU = mybir.AluOpType

T, D, HN, KH = 4096, 1024, 16, 64
KV = KH + 1
NCH = 32
INNER, STATE = 2048, 16
FFH = 4096
FFI = 3072
NEG_LOG_LAM = float(np.log(np.float64(np.float32(1e-8))))
QS = float(np.float32(1.0 / 127.0))
LN_EPS = 1e-5

ENGINES = ("sync", "scalar", "vector", "tensor", "gpsimd")


def host_constants():
    return {
        "c_triu": np.triu(np.ones((128, 128), np.float32)),
        "c_trilm1": np.tril(np.ones((128, 128), np.float32), -1),
        "c_ones_row": np.ones((1, 128), np.float32),
        "c_ident": np.eye(128, dtype=np.float32),
    }


class SemC:
    __slots__ = ("h", "n")

    def __init__(self, h):
        self.h = h
        self.n = 0


class Prog:
    def __init__(self, nc, es):
        self.nc = nc
        self.es = es
        self.ops = {e: [] for e in ENGINES}
        self.sems = {}

    def sem(self, name):
        s = SemC(self.es.enter_context(self.nc.semaphore(name)))
        self.sems[name] = s
        return s

    def sb(self, name, shape, dt=F32):
        return self.es.enter_context(self.nc.sbuf_tensor(name, list(shape), dt))

    def emit(self, eng, fn, waits=(), inc="auto", inc_by=1):
        if inc == "auto":
            inc = self.sems.get("sm_" + eng) if eng != "sync" else None
        if inc is not None:
            inc.n += inc_by
        best = {}
        for s, v in waits:
            if v > 0 and (s.h not in best or best[s.h][1] < v):
                best[s.h] = (s, v)
        wl = tuple((s.h, v) for s, v in best.values())

        def g(e, fn=fn, wl=wl, inc=inc, inc_by=inc_by):
            for h, v in wl:
                e.wait_ge(h, v)
            r = fn(e)
            if inc is not None:
                if r is None:
                    r = e.nop()
                r.then_inc(inc.h, inc_by)

        self.ops[eng].append(g)

    def wait(self, eng, *waits):
        wl = tuple((s.h, v) for s, v in waits if v > 0)
        if not wl:
            return

        def g(e, wl=wl):
            for h, v in wl:
                e.wait_ge(h, v)

        self.ops[eng].append(g)

    def scope(self, name):
        for e in ENGINES:
            self.ops[e].append(("scope", name))


def build_core_program(upto="all", dbg=()):
    nc = bass.Bass("TRN2", target_bir_lowering=False, detect_race_conditions=True)
    es = ExitStack()
    P = Prog(nc, es)

    din = {}

    def ein(name, shape, dt=F32):
        din[name] = nc.dram_tensor(name, list(shape), dt, kind="ExternalInput").ap()
        return din[name]

    x_in = ein("x", (T, D))
    for nm in ("q_w", "k_w", "v_w", "o_w"):
        ein(nm, (D, D))
    for nm in ("q_b", "k_b", "v_b", "o_b", "ln1_w", "ln1_b", "ln2_w", "ln2_b", "f2_b"):
        ein(nm, (D,))
    ein("dec_w", (D, HN)); ein("dec_b", (HN,))
    ein("f1_w", (D, FFH)); ein("f1_b", (FFH,)); ein("f2_w", (FFH, D))
    ein("W_in", (D, 2 * INNER)); ein("conv_w", (INNER, 3)); ein("conv_b", (INNER,))
    ein("W_state", (INNER, 2 * STATE)); ein("W_time", (INNER, INNER))
    ein("b_time", (INNER,)); ein("A_log", (INNER, STATE)); ein("skip", (INNER,))
    ein("W_out", (INNER, D))
    ein("w_gate", (FFI, D), I32); ein("w_up", (FFI, D), I32); ein("w_down", (D, FFI), I32)
    for nm, arr in host_constants().items():
        ein(nm, arr.shape)

    dbg = set(dbg)
    douts = {}

    def scr(name, shape, dt=F32):
        kind = "ExternalOutput" if name in dbg else "Internal"
        t = nc.dram_tensor(name, list(shape), dt, kind=kind).ap()
        if name in dbg:
            douts[name] = t
        return t

    out_d = nc.dram_tensor("out", [T, D], F32, kind="ExternalOutput").ap()
    douts["out"] = out_d

    s_xn0 = scr("xn0", (T, D))
    s_x1 = scr("x1", (T, D))
    s_x2 = scr("x2", (T, D))
    s_x3 = scr("x3", (T, D))
    s_xmid = scr("x_mid", (T, D))
    s_phiq_fm = scr("phiq_fm", (D, T), BF16)
    s_phik_fm = scr("phik_fm", (D, T), BF16)
    s_phiq_tok = scr("phiq_tok", (T, D), BF16)
    s_phik_tok = scr("phik_tok", (T, D), BF16)
    s_v_tok = scr("v_tok", (T, D), BF16)
    s_attnfm = scr("attn_fm_d", (D, T), BF16)
    s_z3fm = scr("z3fm", (D, T), BF16)
    s_h1fm = scr("h1fm", (FFH, T), BF16)
    s_h2fm = scr("h2fm", (FFI, T), BF16)
    s_hfm = scr("h_fm", (INNER, T))
    s_hfm_bf = scr("h_fm_bf", (INNER, T), BF16)
    s_sg = scr("sg_fm", (INNER, T))
    s_winbf = scr("win_bf", (D, 2 * INNER), BF16)
    s_wtt = scr("wt_tiled", (128, 16 * 16 * 128), BF16)
    s_woutbf = scr("wout_bf", (INNER, D), BF16)
    s_wgT = scr("wgT", (D, FFI), BF16)
    s_wuT = scr("wuT", (D, FFI), BF16)
    s_wdT = scr("wdT", (FFI, D), BF16)
    s_f1bf = scr("f1_bf", (D, FFH), BF16)
    s_dbg_misc = scr("dbg_misc", (128, 2048)) if "dbg_misc" in dbg else None

    def sbR(es_, n, s, d=F32):
        return es_.enter_context(nc.sbuf_tensor(n, list(s), d, side="right"))

    es_lion = ExitStack()
    LI = lambda n, s, d=F32: sbR(es_lion, n, s, d)
    ident_f = P.sb("ident_f", (128, 128))
    ident_b = P.sb("ident_b", (128, 128), BF16)
    triu_f = LI("triu_f", (128, 128))
    trilm1_f = LI("trilm1_f", (128, 128))
    ones_row = P.sb("ones_row", (1, 128))
    g1c = P.sb("g1c", (128, 8)); b1c = P.sb("b1c", (128, 8))
    g2c = P.sb("g2c", (128, 8)); b2c = P.sb("b2c", (128, 8))
    bqc = P.sb("bqc", (128, 8)); nbqc = P.sb("nbqc", (128, 8))
    bkc = P.sb("bkc", (128, 8)); nbkc = P.sb("nbkc", (128, 8))
    bvc = P.sb("bvc", (128, 8))
    bdecc = P.sb("bdecc", (16, 1)); nbdecc = P.sb("nbdecc", (16, 1))
    f1bc = P.sb("f1bc", (128, 32))
    btimec = P.sb("btimec", (128, 16))
    cw0c = P.sb("cw0c", (128, 16)); cw1c = P.sb("cw1c", (128, 16))
    cw2c = P.sb("cw2c", (128, 16)); cbc = P.sb("cbc", (128, 16))
    skipc = P.sb("skipc", (128, 16))
    A_sb = P.sb("A_sb", (128, 256))
    ob_bc = LI("ob_bc", (128, 1024))
    f2b_bc = LI("f2b_bc", (128, 1024))
    lloc_tok = LI("lloc_tok", (128, NCH * HN))
    af_tok = LI("af_tok", (128, NCH * HN))
    ab_tok = LI("ab_tok", (128, NCH * HN))
    gam_bc = LI("gam_bc", (128, NCH * HN))
    gamrow = LI("gamrow", (1, NCH * HN))
    eps_col = P.sb("eps_col", (128, 1))
    ln_m = P.sb("ln_m", (128, 1)); ln_s2 = P.sb("ln_s2", (128, 1))
    ln_v = P.sb("ln_v", (128, 1)); ln_r = P.sb("ln_r", (128, 1))
    ln_scr = P.sb("ln_scr", (128, 1024))
    ln_scr2 = P.sb("ln_scr2", (128, 1024))
    ln_guard = [0]
    lnb_m = P.sb("lnb_m", (128, 16)); lnb_s2 = P.sb("lnb_s2", (128, 16))
    lnb_v = P.sb("lnb_v", (128, 16)); lnb_r = P.sb("lnb_r", (128, 16))
    f32stage = P.sb("f32stage", (128, 4096))
    bfstage = P.sb("bfstage", (128, 4096), BF16)
    phtmp1 = P.sb("phtmp1", (128, 512))
    phtmp2 = P.sb("phtmp2", (128, 512))

    psf = [P.es.enter_context(nc.psum_tensor(f"psf{i}", [128, 512], F32)) for i in range(6)]
    psb = [P.es.enter_context(nc.psum_tensor(f"psb{i}", [128, 1024], BF16)) for i in range(2)]
    bankfree = {}

    def bfree(bid):
        w = bankfree.get(bid)
        return [w] if w else []

    sm = {e: P.sem("sm_" + e) for e in ENGINES}
    ld_pool = {"sync": [P.sem(f"ldS{i}") for i in range(16)],
               "scalar": [P.sem(f"ldA{i}") for i in range(10)]}
    ld_idx = {"sync": 0, "scalar": 0}
    store_sems = {}
    fence_sem = P.sem("fence")

    def all_dma_vals():
        vals = [(s, s.n) for s in ld_pool["sync"] if s.n > 0]
        vals += [(s, s.n) for s in ld_pool["scalar"] if s.n > 0]
        vals += [(s, s.n) for pair in store_sems.values() for s in pair if s.n > 0]
        return vals

    def fence(dma=True):
        vals = []
        for e in ENGINES:
            P.emit(e, lambda en: en.nop(), waits=[(sm[e], sm[e].n)], inc=sm[e])
            vals.append((sm[e], sm[e].n))
        if dma:
            P.emit("sync", lambda en: en.nop(), waits=vals + all_dma_vals(),
                   inc=fence_sem)
            fv = (fence_sem, fence_sem.n)
            for e in ENGINES:
                P.wait(e, fv)
        else:
            for e in ENGINES:
                P.wait(e, *vals)

    def efence(*engs):
        vals = []
        for e in engs:
            P.emit(e, lambda en: en.nop(), waits=[(sm[e], sm[e].n)], inc=sm[e])
            vals.append((sm[e], sm[e].n))
        for e in engs:
            P.wait(e, *vals)

    def load(sbuf_ap, dram_ap, q="sync", waits=(), slow=False):
        s = ld_pool[q][ld_idx[q] % len(ld_pool[q])]
        ld_idx[q] += 1
        w = list(waits)
        if s.n > 0:
            w.append((s, s.n))
        nm = dram_ap.tensor.name
        if nm in store_sems:
            for ss in store_sems[nm]:
                if ss.n > 0:
                    w.append((ss, ss.n))
        P.emit(q, lambda e, o=sbuf_ap, i=dram_ap, sl=slow: e.dma_start(
            out=o, in_=i, allow_slow_non_contiguous=sl),
               waits=w, inc=s, inc_by=16)
        return (s, s.n)

    store_idx = {}

    def store(dram_ap, sbuf_ap, waits=(), q="sync", slow=False):
        nm = dram_ap.tensor.name
        if nm not in store_sems:
            store_sems[nm] = [P.sem("st_" + nm + "a"), P.sem("st_" + nm + "b")]
            store_idx[nm] = 0
        k = store_idx[nm]
        store_idx[nm] = 1 - k
        s = store_sems[nm][k]
        w = list(waits)
        if s.n > 0:
            w.append((s, s.n))
        P.emit(q, lambda e, o=dram_ap, i=sbuf_ap, sl=slow: e.dma_start(
            out=o, in_=i, allow_slow_non_contiguous=sl),
               waits=w, inc=s, inc_by=16)
        return (s, s.n)

    def tval():
        return (sm["tensor"], sm["tensor"].n)

    def vval():
        return (sm["vector"], sm["vector"].n)

    def sval():
        return (sm["scalar"], sm["scalar"].n)

    def gval():
        return (sm["gpsimd"], sm["gpsimd"].n)

    def mm(ps_ap, bid, lhsT, rhs, start, stop, waits=(), inc=True):
        w = list(waits)
        if start:
            w += bfree(bid)
        P.emit("tensor", lambda e, o=ps_ap, l=lhsT, r=rhs, st=start, sp=stop:
               e.matmul(o, l, r, start=st, stop=sp), waits=w,
               inc=sm["tensor"] if inc else None)

    def evac(eng, bid, fn, waits=()):
        s = sm[eng]
        P.emit(eng, fn, waits=waits, inc=s)
        bankfree[bid] = (s, s.n)
        return (s, s.n)

    def layer_norm(dst_ap, src_ap, W, waits=()):
        P.emit("vector", lambda e: e.tensor_scalar(
            ln_scr[:, 0:W], src_ap, 1.0, 0.0, op0=ALU.mult, op1=ALU.add,
            accum_out=ln_m[:, 0:1]), waits=list(waits) + [ (sm["vector"], ln_guard[0]) ])
        P.emit("vector", lambda e: e.scalar_tensor_tensor(
            ln_scr2[:, 0:W], src_ap, 1.0, src_ap, op0=ALU.mult, op1=ALU.mult,
            accum_out=ln_s2[:, 0:1]))
        v2 = vval()
        P.emit("vector", lambda e: e.tensor_tensor(ln_v[:, 0:1], ln_m[:, 0:1],
                                                   ln_m[:, 0:1], op=ALU.mult),
               waits=[v2])
        v3 = vval()
        P.emit("vector", lambda e, W=W: e.scalar_tensor_tensor(
            ln_v[:, 0:1], ln_v[:, 0:1], -1.0 / W, ln_s2[:, 0:1],
            op0=ALU.mult, op1=ALU.add), waits=[v3])
        v4 = vval()
        P.emit("scalar", lambda e, W=W: e.activation(
            ln_r[:, 0:1], ln_v[:, 0:1], AF.Ln, bias=eps_col[:, 0:1], scale=1.0 / W),
            waits=[v4])
        s1 = sval()
        P.emit("scalar", lambda e: e.activation(ln_r[:, 0:1], ln_r[:, 0:1], AF.Exp,
                                                scale=-0.5), waits=[s1])
        sv = sval()
        P.emit("vector", lambda e, W=W: e.tensor_scalar(
            ln_m[:, 0:1], ln_m[:, 0:1], 1.0 / W, None, op0=ALU.mult), waits=[v4])
        v5 = vval()
        P.emit("vector", lambda e: e.tensor_scalar(
            dst_ap, src_ap, ln_m[:, 0:1], ln_r[:, 0:1], op0=ALU.subtract, op1=ALU.mult),
            waits=[sv, v5])
        ln_guard[0] = sm["vector"].n
        return vval()

    def layer_norm_span(src_f, NT, W, waits_f):
        for i in range(NT):
            P.emit("vector", lambda e, i=i: e.tensor_scalar(
                ln_scr[:, 0:W], src_f(i), 1.0, 0.0, op0=ALU.mult, op1=ALU.add,
                accum_out=lnb_m[:, i:i + 1]),
                waits=list(waits_f(i)) + ([(sm["vector"], ln_guard[0])]
                                          if i == 0 else [vval()]))
            P.emit("vector", lambda e, i=i: e.scalar_tensor_tensor(
                ln_scr2[:, 0:W], src_f(i), 1.0, src_f(i), op0=ALU.mult, op1=ALU.mult,
                accum_out=lnb_s2[:, i:i + 1]), waits=[vval()] if i else ())
        v2 = vval()
        P.emit("vector", lambda e: e.tensor_tensor(
            lnb_v[:, 0:NT], lnb_m[:, 0:NT], lnb_m[:, 0:NT], op=ALU.mult), waits=[v2])
        v3 = vval()
        P.emit("vector", lambda e, W=W: e.scalar_tensor_tensor(
            lnb_v[:, 0:NT], lnb_v[:, 0:NT], -1.0 / W, lnb_s2[:, 0:NT],
            op0=ALU.mult, op1=ALU.add), waits=[v3])
        v4 = vval()
        P.emit("scalar", lambda e, W=W: e.activation(
            lnb_r[:, 0:NT], lnb_v[:, 0:NT], AF.Ln, bias=eps_col[:, 0:1], scale=1.0 / W),
            waits=[v4])
        s1 = sval()
        P.emit("scalar", lambda e: e.activation(lnb_r[:, 0:NT], lnb_r[:, 0:NT], AF.Exp,
                                                scale=-0.5), waits=[s1])
        svr = sval()
        P.emit("vector", lambda e, W=W: e.tensor_scalar(
            lnb_m[:, 0:NT], lnb_m[:, 0:NT], 1.0 / W, None, op0=ALU.mult), waits=[v4])
        v5 = vval()
        first = [True]

        def norm(i, dst_ap, waits=()):
            w = list(waits) + ([svr, v5] if first[0] else [])
            first[0] = False
            P.emit("vector", lambda e, i=i, d=dst_ap: e.tensor_scalar(
                d, src_f(i), lnb_m[:, i:i + 1], lnb_r[:, i:i + 1],
                op0=ALU.subtract, op1=ALU.mult), waits=w)
            ln_guard[0] = sm["vector"].n
            return vval()

        return norm

    w1 = load(ident_f[:], din["c_ident"][:])
    load(triu_f[:], din["c_triu"][:])
    load(trilm1_f[:], din["c_trilm1"][:])
    load(ones_row[:], din["c_ones_row"][:])
    for sb_t, nm in [(g1c, "ln1_w"), (b1c, "ln1_b"), (g2c, "ln2_w"), (b2c, "ln2_b"),
                     (btimec, "b_time"), (cbc, "conv_b"), (skipc, "skip")]:
        load(sb_t[:], din[nm].rearrange("(c p) -> p c", p=128), slow=True)
    load(bdecc[:, 0:1], din["dec_b"].rearrange("(c p) -> p c", p=16), slow=True)
    for j, sb_t in enumerate((cw0c, cw1c, cw2c)):
        load(sb_t[:].rearrange("p (c one) -> p c one", one=1),
             din["conv_w"][:, j:j + 1].rearrange("(c p) one -> p c one", p=128),
             slow=True)
    wA = load(A_sb[:].rearrange("p (c s) -> p c s", s=16),
              din["A_log"].rearrange("(c p) s -> p c s", p=128))
    P.emit("vector", lambda e: e.memset(eps_col[:], LN_EPS), inc=sm["vector"])
    P.emit("vector", lambda e: e.tensor_copy(ident_b[:], ident_f[:]), waits=[w1],
           inc=sm["vector"])
    P.emit("scalar", lambda e: e.activation(A_sb[:], A_sb[:], AF.Exp), waits=[wA],
           inc=sm["scalar"])
    P.emit("vector", lambda e: e.tensor_scalar_mul(A_sb[:], A_sb[:], -1.0),
           waits=[sval()], inc=sm["vector"])

    P.scope("p0_weights")
    es_p0 = ExitStack()
    tstage8 = sbR(es_p0, "tstage8", (128, 8 * 3072), BF16)
    i32stage = sbR(es_p0, "i32stage", (128, 2 * 3072), I32)
    bf2stage = sbR(es_p0, "bf2stage", (128, 2 * 3072), BF16)
    tiw_cast = [None, None]
    tiw_tp = [None, None]

    def transpose_int_w(dst, src, rows, cols):
        P.wait("vector", *[(s, s.n) for pair in store_sems.values() for s in pair
                           if s.n > 0])
        for rb in range(rows // 128):
            sl2 = rb % 2
            io = 3072 * sl2
            wld = load(i32stage[:, io:io + cols], src[128 * rb:128 * (rb + 1), :],
                       q="scalar", waits=[tiw_cast[sl2]] if tiw_cast[sl2] else ())
            P.emit("vector", lambda e, c=cols, io=io: e.tensor_copy(
                bf2stage[:, io:io + c], i32stage[:, io:io + c]),
                waits=[wld] + ([tiw_tp[sl2]] if tiw_tp[sl2] else []),
                inc=sm["vector"])
            vv = vval()
            tiw_cast[sl2] = vv
            for cb in range(cols // 128):
                bid = 6 + cb % 2
                P.emit("tensor", lambda e, cb=cb, io=io: e.transpose(
                    psb[cb % 2][:, 0:128], bf2stage[:, io + 128 * cb:io + 128 * (cb + 1)],
                    ident_b[:]), waits=[vv] + bfree(bid), inc=sm["tensor"])
                tv = tval()
                evac("vector", bid, lambda e, cb=cb, rb=rb, rows=rows: e.tensor_copy(
                    tstage8[:, cb * rows + 128 * rb:cb * rows + 128 * (rb + 1)],
                    psb[cb % 2][:, 0:128]), waits=[tv])
            tiw_tp[sl2] = tval()
        vv = vval()
        for cb in range(cols // 128):
            store(dst[128 * cb:128 * (cb + 1), 0:rows],
                  tstage8[:, cb * rows:(cb + 1) * rows], waits=[vv])

    transpose_int_w(s_wgT, din["w_gate"], FFI, D)
    transpose_int_w(s_wuT, din["w_up"], FFI, D)
    transpose_int_w(s_wdT, din["w_down"], D, FFI)
    fence()
    es_p0.close()

    es_cr = ExitStack()
    c32 = sbR(es_cr, "c32", (128, 2 * 4096))
    cbf = sbR(es_cr, "cbf", (128, 2 * 4096), BF16)
    cr_cast = [None, None]
    cr_st = [None, None]

    def cast_rows(dst, src, rows, cols, tiled=False):
        for rb in range(rows // 128):
            sl2 = rb % 2
            o = 4096 * sl2
            wld = load(c32[:, o:o + cols], src[128 * rb:128 * (rb + 1), :], q="scalar",
                       waits=[cr_cast[sl2]] if cr_cast[sl2] else ())
            P.emit("vector", lambda e, c=cols, o=o: e.tensor_copy(
                cbf[:, o:o + c], c32[:, o:o + c]),
                waits=[wld] + ([cr_st[sl2]] if cr_st[sl2] else []), inc=sm["vector"])
            vv = vval()
            cr_cast[sl2] = vv
            if tiled:
                src3 = cbf[:, o:o + cols].rearrange("p (mc j) -> p mc j", j=128)
                dst3 = s_wtt[:, :].rearrange("p (mc kc j) -> p mc (kc j)", kc=16, j=128)[
                    :, :, 128 * rb:128 * (rb + 1)]
                st = store(dst3, src3, waits=[vv])
            else:
                st = store(dst[128 * rb:128 * (rb + 1), :], cbf[:, o:o + cols],
                           waits=[vv])
            cr_st[sl2] = st
        fence()

    cast_rows(s_winbf, din["W_in"], D, 2 * INNER)
    cast_rows(s_woutbf, din["W_out"], INNER, D)
    cast_rows(None, din["W_time"], INNER, INNER, tiled=True)
    es_cr.close()

    es_a1 = ExitStack()
    A1 = lambda n, s, d=F32: sbR(es_a1, n, s, d)
    wq_bf = A1("wq_bf", (128, 8 * 1024), BF16)
    wk_bf = A1("wk_bf", (128, 8 * 1024), BF16)
    wv_bf = A1("wv_bf", (128, 8 * 1024), BF16)
    wdec_bf = A1("wdec_bf", (128, 8 * 16), BF16)
    es_fold = ExitStack()
    FO = lambda n, s, d=F32: sbR(es_fold, n, s, d)
    rowbuf = FO("rowbuf", (1, 1024))
    rowbuf2 = FO("rowbuf2", (1, 1024))
    brow = FO("brow", (1, 4096))
    bprow = FO("bprow", (1, 4096))
    btmp = FO("btmp", (128, 128))
    w_ob = load(rowbuf[:], din["o_b"][None, :])
    w_f2b = load(rowbuf2[:], din["f2_b"][None, :])

    def bcast_row(dst_sb, row_sb, n, wait):
        for i in range((n + 511) // 512):
            nn = min(512, n - 512 * i)
            bid = i % 2
            mm(psf[bid][:, 0:nn], bid, ones_row[:], row_sb[0:1, 512 * i:512 * i + nn],
               True, True, waits=[wait])
            tv = tval()
            evac("vector", bid, lambda e, b=bid, i=i, nn=nn: e.tensor_copy(
                dst_sb[:, 512 * i:512 * i + nn], psf[b][:, 0:nn]), waits=[tv])
        return vval()

    v_obbc = bcast_row(ob_bc, rowbuf, 1024, w_ob)
    v_f2bc = bcast_row(f2b_bc, rowbuf2, 1024, w_f2b)

    def fold_prep(w_name, n_out, gcol, betacol, b_name, wbf_sb, bias_col, neg_bias_col,
                  wbf_dram=None, bias_npart=128):
        nkc = 8
        npass = max(1, n_out // 2048) if n_out >= 2048 else 1
        ncols_pass = min(n_out, 2048)
        for ip in range(npass):
            for kc in range(nkc):
                wld = load(f32stage[:, 0:n_out], din[w_name][128 * kc:128 * (kc + 1), :],
                           q="scalar")
                if ip == 0:
                    P.emit("vector", lambda e, kc=kc, n=n_out: e.tensor_scalar(
                        bfstage[:, 0:n], f32stage[:, 0:n], gcol[:, kc:kc + 1], None,
                        op0=ALU.mult), waits=[wld], inc=sm["vector"])
                    vv = vval()
                    if wbf_dram is not None:
                        st = store(wbf_dram[128 * kc:128 * (kc + 1), :], bfstage[:, 0:n_out],
                                   waits=[vv])
                        P.wait("vector", st)
                    else:
                        P.emit("vector", lambda e, kc=kc, n=n_out: e.tensor_copy(
                            wbf_sb[:, n * kc:n * (kc + 1)], bfstage[:, 0:n]),
                            waits=[vv], inc=sm["vector"])
                nb = (min(ncols_pass, n_out - 2048 * ip) + 511) // 512
                for ib in range(nb):
                    c0 = 2048 * ip + 512 * ib
                    cn = min(512, n_out - c0)
                    mm(psf[2 + ib][0:1, 0:cn], 2 + ib, betacol[:, kc:kc + 1],
                       f32stage[:, c0:c0 + cn], kc == 0, kc == nkc - 1,
                       waits=[wld, vval()])
                P.wait("scalar", tval(), vval())
            tv = tval()
            for ib in range(nb):
                c0 = 2048 * ip + 512 * ib
                cn = min(512, n_out - c0)
                evac("vector", 2 + ib, lambda e, c0=c0, cn=cn, ib=ib: e.tensor_copy(
                    brow[0:1, c0:c0 + cn], psf[2 + ib][0:1, 0:cn]), waits=[tv])
        wb = load(bprow[0:1, 0:n_out], din[b_name][None, :])
        P.emit("vector", lambda e, n=n_out: e.tensor_add(
            brow[0:1, 0:n], brow[0:1, 0:n], bprow[0:1, 0:n]), waits=[wb, vval()],
            inc=sm["vector"])
        vv = vval()
        nmc = max(1, n_out // 128) if n_out >= 128 else 1
        cw = min(128, n_out)
        for mc in range(nmc):
            P.emit("vector", lambda e, mc=mc, cw=cw: e.tensor_copy(
                btmp[0:1, 0:cw], brow[0:1, 128 * mc:128 * mc + cw]),
                waits=[vv] + bfree(5), inc=sm["vector"])
            vv2 = vval()
            P.emit("tensor", lambda e: e.transpose(psf[5][:, 0:128], btmp[:, 0:128],
                                                   ident_f[:]),
                   waits=[vv2] + bfree(5), inc=sm["tensor"])
            tv = tval()
            evac("vector", 5, lambda e, mc=mc, cw=cw: e.tensor_copy(
                bias_col[0:cw, mc:mc + 1], psf[5][0:cw, 0:1]), waits=[tv])
            if neg_bias_col is not None:
                P.emit("vector", lambda e, mc=mc, cw=cw: e.tensor_scalar_mul(
                    neg_bias_col[0:cw, mc:mc + 1], bias_col[0:cw, mc:mc + 1], -1.0),
                    waits=[vval()], inc=sm["vector"])
        fence()

    fold_prep("q_w", D, g1c, b1c, "q_b", wq_bf, bqc, nbqc)
    fold_prep("k_w", D, g1c, b1c, "k_b", wk_bf, bkc, nbkc)
    fold_prep("v_w", D, g1c, b1c, "v_b", wv_bf, bvc, None)
    fold_prep("dec_w", HN, g1c, b1c, "dec_b", wdec_bf, bdecc, nbdecc, bias_npart=16)
    fold_prep("f1_w", FFH, g2c, b2c, "f1_b", None, f1bc, None, wbf_dram=s_f1bf)
    fence()
    es_fold.close()

    SPA = 512
    NTT = SPA // 128
    x_span = A1("x_span", (128, NTT * 1024))
    z2_bf = A1("z2_bf", (128, NTT * 1024), BF16)
    z2fm = A1("z2fm", (128, 8 * SPA), BF16)
    stg_fm = A1("stg_fm", (128, SPA), BF16)
    vbf_fm = A1("vbf_fm", (128, 8 * SPA), BF16)
    stg_tok = [A1(f"stg_tok{i}", (128, 1024), BF16) for i in range(2)]
    stg_tok_st = [None, None]
    stg_fm_st = None
    dptmp = A1("dptmp", (16, 1024))
    dptmp2 = A1("dptmp2", (16, 1024))
    logl_buf = A1("logl_buf", (16, 1024))
    logl_tokc = A1("logl_tokc", (128, 16))

    P.scope("p_a1")
    def phase_a1():
        nonlocal stg_fm_st
        for sp in range(T // SPA):
            t0 = sp * SPA
            fence(dma=False)
            for tt in range(NTT):
                wld = load(x_span[:, 1024 * tt:1024 * (tt + 1)],
                           x_in[t0 + 128 * tt:t0 + 128 * (tt + 1), :])
                layer_norm(x_span[:, 1024 * tt:1024 * (tt + 1)],
                           x_span[:, 1024 * tt:1024 * (tt + 1)], 1024, waits=[wld])
                vX = vval()
                store(s_xn0[t0 + 128 * tt:t0 + 128 * (tt + 1), :],
                      x_span[:, 1024 * tt:1024 * (tt + 1)], waits=[vX])
                layer_norm(z2_bf[:, 1024 * tt:1024 * (tt + 1)],
                           x_span[:, 1024 * tt:1024 * (tt + 1)], 1024, waits=[vX])
                vZ = vval()
                for kc in range(8):
                    bid = 6 + kc % 2
                    P.emit("tensor", lambda e, tt=tt, kc=kc: e.transpose(
                        psb[kc % 2][:, 0:128],
                        z2_bf[:, 1024 * tt + 128 * kc:1024 * tt + 128 * (kc + 1)],
                        ident_b[:]), waits=[vZ] + bfree(bid), inc=sm["tensor"])
                    tv = tval()
                    evac("vector", bid, lambda e, tt=tt, kc=kc: e.tensor_copy(
                        z2fm[:, SPA * kc + 128 * tt:SPA * kc + 128 * (tt + 1)],
                        psb[kc % 2][:, 0:128]), waits=[tv])
            vZF = vval()

            for (wsb, bcol, nbcol, fmd, tokd, is_v) in (
                    (wq_bf, bqc, nbqc, s_phiq_fm, s_phiq_tok, False),
                    (wk_bf, bkc, nbkc, s_phik_fm, s_phik_tok, False),
                    (wv_bf, bvc, None, None, s_v_tok, True)):
                for mc in range(8):
                    for nt in range(SPA // 512):
                        bid = (mc * 2 + nt) % 4
                        ps = psf[bid]
                        for kc in range(8):
                            mm(ps[:, 0:512], bid,
                               wsb[:, 1024 * kc + 128 * mc:1024 * kc + 128 * (mc + 1)],
                               z2fm[:, SPA * kc + 512 * nt:SPA * kc + 512 * (nt + 1)],
                               kc == 0, kc == 7, waits=[vZF] if kc == 0 else ())
                        tv = tval()
                        col = 512 * nt
                        if not is_v:
                            P.emit("scalar", lambda e, ps=ps, nb=nbcol, mc=mc: e.activation(
                                phtmp1[:, 0:512], ps[:, 0:512], AF.Relu,
                                bias=nb[:, mc:mc + 1], scale=-1.0),
                                waits=[tv, vval()], inc=sm["scalar"])
                            P.emit("scalar", lambda e: e.activation(
                                phtmp1[:, 0:512], phtmp1[:, 0:512], AF.Exp, scale=-1.0),
                                waits=[sval()])
                            P.emit("scalar", lambda e, ps=ps, b=bcol, mc=mc: e.activation(
                                phtmp2[:, 0:512], ps[:, 0:512], AF.Relu,
                                bias=b[:, mc:mc + 1]), inc=sm["scalar"])
                            sv = sval()
                            bankfree[bid] = sv
                            P.emit("vector", lambda e, col=col: e.tensor_add(
                                stg_fm[:, col:col + 512], phtmp1[:, 0:512],
                                phtmp2[:, 0:512]),
                                waits=[sv] + (stg_fm_st or []), inc=sm["vector"])
                        else:
                            evac("scalar", bid, lambda e, ps=ps, b=bcol, mc=mc, col=col:
                                 e.activation(stg_fm[:, col:col + 512], ps[:, 0:512],
                                              AF.Identity, bias=b[:, mc:mc + 1]),
                                 waits=[tv] + (stg_fm_st or []))
                    vv = vval()
                    svv = sval()
                    st_w = None
                    if fmd is not None:
                        st_w = store(fmd[128 * mc:128 * (mc + 1), t0:t0 + SPA],
                                     stg_fm[:, 0:SPA], waits=[vv, svv])
                    P.emit("vector", lambda e, mc=mc: e.tensor_copy(
                        vbf_fm[:, SPA * mc:SPA * (mc + 1)], stg_fm[:, 0:SPA]),
                        waits=[vv, svv], inc=sm["vector"])
                    stg_fm_st = [vval()] + ([st_w] if st_w else [])
                vv = vval()
                for tt in range(NTT):
                    sl = tt % 2
                    for mc in range(8):
                        bid = 6 + mc % 2
                        P.emit("tensor", lambda e, tt=tt, mc=mc: e.transpose(
                            psb[mc % 2][:, 128:256],
                            vbf_fm[:, SPA * mc + 128 * tt:SPA * mc + 128 * (tt + 1)],
                            ident_b[:]), waits=[vv] + bfree(bid), inc=sm["tensor"])
                        tv = tval()
                        evac("vector", bid, lambda e, sl=sl, mc=mc: e.tensor_copy(
                            stg_tok[sl][:, 128 * mc:128 * (mc + 1)],
                            psb[mc % 2][:, 128:256]),
                            waits=[tv] + ([stg_tok_st[sl]] if stg_tok_st[sl] else []))
                    vv2 = vval()
                    stg_tok_st[sl] = store(tokd[t0 + 128 * tt:t0 + 128 * (tt + 1), :],
                                           stg_tok[sl][:, 0:1024], waits=[vv2])

            for nt in range(SPA // 512):
                bid = 4
                ps = psf[4]
                for kc in range(8):
                    mm(ps[0:16, 0:512], bid, wdec_bf[:, 16 * kc:16 * (kc + 1)],
                       z2fm[:, SPA * kc + 512 * nt:SPA * kc + 512 * (nt + 1)],
                       kc == 0, kc == 7)
                tv = tval()
                P.emit("scalar", lambda e, ps=ps: e.activation(
                    dptmp[:, 0:512], ps[0:16, 0:512], AF.Abs, bias=bdecc[:, 0:1]),
                    waits=[tv, vval()], inc=sm["scalar"])
                P.emit("scalar", lambda e: e.activation(dptmp[:, 0:512], dptmp[:, 0:512],
                                                        AF.Exp, scale=-1.0),
                       waits=[sval()])
                P.emit("scalar", lambda e: e.activation(dptmp[:, 0:512], dptmp[:, 0:512],
                                                        AF.Ln, bias=1.0),
                       waits=[sval()])
                evac("scalar", bid, lambda e, ps=ps: e.activation(
                    dptmp2[:, 0:512], ps[0:16, 0:512], AF.Relu, bias=nbdecc[:, 0:1],
                    scale=-1.0))
                sv = sval()
                P.emit("vector", lambda e, nt=nt: e.tensor_add(
                    logl_buf[:, 512 * nt:512 * (nt + 1)], dptmp[:, 0:512], dptmp2[:, 0:512]),
                    waits=[sv])
                P.emit("vector", lambda e, nt=nt: e.tensor_scalar(
                    logl_buf[:, 512 * nt:512 * (nt + 1)],
                    logl_buf[:, 512 * nt:512 * (nt + 1)], -1.0, NEG_LOG_LAM,
                    op0=ALU.mult, op1=ALU.max), waits=[vval()], inc=sm["vector"])
            vLL = vval()
            for ci in range(SPA // 128):
                c = sp * (SPA // 128) + ci
                P.emit("tensor", lambda e, ci=ci: e.transpose(
                    psf[4][:, 0:16], logl_buf[0:16, 128 * ci:128 * (ci + 1)],
                    ident_f[0:16, 0:16]),
                    waits=[vLL] + bfree(4), inc=sm["tensor"])
                tv = tval()
                evac("vector", 4, lambda e: e.tensor_copy(logl_tokc[:], psf[4][:, 0:16]),
                     waits=[tv])
                vv = vval()
                mm(psf[5][:, 0:16], 5, triu_f[:], logl_tokc[:], True, True, waits=[vv])
                mm(psf[5][:, 16:32], 5, trilm1_f[:], logl_tokc[:], True, True)
                tv = tval()
                P.emit("scalar", lambda e, c=c: e.activation(
                    af_tok[:, 16 * c:16 * (c + 1)], psf[5][:, 0:16], AF.Exp),
                    waits=[tv], inc=sm["scalar"])
                P.emit("scalar", lambda e, c=c: e.activation(
                    ab_tok[:, 16 * c:16 * (c + 1)], psf[5][:, 16:32], AF.Exp),
                    inc=sm["scalar"])
                sv = sval()
                evac("vector", 5, lambda e, c=c: e.tensor_copy(
                    lloc_tok[:, 16 * c:16 * (c + 1)], psf[5][:, 0:16]), waits=[sv])
        vv = vval()
        svv = sval()
        wgr = load(gamrow[0:1, :], af_tok[127:128, :], waits=[svv, vv])
        mm(psf[0][:, 0:512], 0, ones_row[:], gamrow[0:1, :], True, True, waits=[wgr])
        tv = tval()
        evac("vector", 0, lambda e: e.tensor_copy(gam_bc[:], psf[0][:, 0:512]), waits=[tv])

    phase_a1()
    fence()
    es_a1.close()
    if s_dbg_misc is not None:
        for i, t in enumerate((lloc_tok, af_tok, ab_tok, gam_bc)):
            store(s_dbg_misc[:, 512 * i:512 * (i + 1)], t[:])
        fence()
    if upto == "pa1":
        return _emit(nc, P, es, din, douts)

    P.scope("p_a2")
    es_a2 = ExitStack()
    A2 = lambda n, s, d=F32: sbR(es_a2, n, s, d)
    att_acc = A2("att_acc", (64, 2 * T), BF16)
    phq_tok_h = A2("phq_tok_h", (128, 2 * NCH * KH), BF16)
    phk_tok_h = A2("phk_tok_h", (128, 2 * NCH * KH), BF16)
    phq_fm_h = A2("phq_fm_h", (64, 2 * T), BF16)
    phk_fm_h = A2("phk_fm_h", (64, 2 * T), BF16)
    v_tok_h = A2("v_tok_h", (128, 2 * NCH * KV), BF16)
    R_sb = A2("R_sb", (64, 2 * NCH * KV))
    R_bf = A2("R_bf", (64, 2 * NCH * KV), BF16)
    S_sb = A2("S_sb", (64, 2 * KV))
    S_bf = A2("S_bf", (64, 2 * KV), BF16)
    kscl = A2("kscl", (128, 2 * KH), BF16)
    qf_tok = A2("qf_tok", (128, 2 * KH), BF16)
    qb_tok = A2("qb_tok", (128, 2 * KH), BF16)
    qf_fm = A2("qf_fm", (64, 2 * 128), BF16)
    qb_fm = A2("qb_fm", (64, 2 * 128), BF16)
    D_sb = A2("D_sb", (128, 2 * 128))
    W_sb = A2("W_sb", (128, 2 * 128), BF16)
    numden = A2("numden", (128, 4))
    att_tok = A2("att_tok", (128, 2 * KH), BF16)
    lrow = A2("lrow", (1, 2 * 128))

    def phase_a2():
        P.emit("vector", lambda e: e.memset(
            v_tok_h[:].rearrange("p (c k) -> p c k", k=KV)[:, :, KH:KV], 1.0),
            inc=sm["vector"])
        SL = {}

        def ld_head(sl, h):
            hc = h * KH
            o = sl * NCH * KH
            o5 = sl * NCH * KV
            w = []
            w.append(load(phq_tok_h[:, o:o + NCH * KH].rearrange(
                "p (c k) -> p c k", k=KH),
                s_phiq_tok[:, hc:hc + KH].rearrange("(c p) k -> p c k", p=128)))
            w.append(load(phk_tok_h[:, o:o + NCH * KH].rearrange(
                "p (c k) -> p c k", k=KH),
                s_phik_tok[:, hc:hc + KH].rearrange("(c p) k -> p c k", p=128)))
            w.append(load(phq_fm_h[:, sl * T:(sl + 1) * T], s_phiq_fm[hc:hc + KH, :]))
            w.append(load(phk_fm_h[:, sl * T:(sl + 1) * T], s_phik_fm[hc:hc + KH, :]))
            w.append(load(v_tok_h[:, o5:o5 + NCH * KV].rearrange(
                "p (c k) -> p c k", k=KV)[:, :, 0:KH],
                s_v_tok[:, hc:hc + KH].rearrange("(c p) k -> p c k", p=128)))
            SL[(sl, "lds")] = w
            P.emit("vector", lambda e, sl=sl: e.memset(
                R_sb[:, 2 * NCH * KV * 0 + (sl * NCH + NCH - 1) * KV:
                     (sl * NCH + NCH) * KV], 0.0), inc=sm["vector"])
            P.emit("vector", lambda e, sl=sl: e.memset(
                S_sb[:, KV * sl:KV * (sl + 1)], 0.0), inc=sm["vector"])
            P.emit("vector", lambda e, sl=sl: e.tensor_copy(
                R_bf[:, (sl * NCH + NCH - 1) * KV:(sl * NCH + NCH) * KV],
                R_sb[:, (sl * NCH + NCH - 1) * KV:(sl * NCH + NCH) * KV]),
                waits=[vval()], inc=sm["vector"])
            P.emit("vector", lambda e, sl=sl: e.tensor_copy(
                S_bf[:, KV * sl:KV * (sl + 1)], S_sb[:, KV * sl:KV * (sl + 1)]),
                waits=[vval()], inc=sm["vector"])
            SL[(sl, "init")] = vval()

        def bwd_step(sl, h, c):
            sc = c * HN + h
            ksl = kscl[:, KH * sl:KH * (sl + 1)]
            ro = sl * NCH * KV
            P.emit("vector", lambda e, c=c, sc=sc, sl=sl, ksl=ksl: e.tensor_scalar(
                ksl, phq_tok_h[:, 0:KH] if False else
                phk_tok_h[:, sl * NCH * KH + KH * c:sl * NCH * KH + KH * (c + 1)],
                af_tok[:, sc:sc + 1], None, op0=ALU.mult),
                waits=[SL[(sl, "kfree")]] if (sl, "kfree") in SL else
                      SL[(sl, "lds")] + [SL[(sl, "init")]], inc=sm["vector"])
            vv = vval()
            yield
            bid = 4 + sl
            mm(psf[bid][0:64, 0:KV], bid, ksl,
               v_tok_h[:, sl * NCH * KV + KV * c:sl * NCH * KV + KV * (c + 1)],
               True, True, waits=[vv])
            tv = tval()
            SL[(sl, "kfree")] = tv
            yield
            evac("vector", bid, lambda e, c=c, sc=sc, ro=ro, bid=bid:
                 e.scalar_tensor_tensor(
                     R_sb[:, ro + KV * (c - 1):ro + KV * c],
                     R_sb[:, ro + KV * c:ro + KV * (c + 1)],
                     gam_bc[0:64, sc:sc + 1], psf[bid][0:64, 0:KV],
                     op0=ALU.mult, op1=ALU.add), waits=[tv, vval()])
            yield
            P.emit("vector", lambda e, c=c, ro=ro: e.tensor_copy(
                R_bf[:, ro + KV * (c - 1):ro + KV * c],
                R_sb[:, ro + KV * (c - 1):ro + KV * c]),
                waits=[vval()], inc=sm["vector"])

        def fwd_step(sl, h, c):
            sc = c * HN + h
            tc = c * 128
            o = sl * NCH * KH
            o5 = sl * NCH * KV
            qo = sl * T
            dsl_ = D_sb[:, 128 * sl:128 * (sl + 1)]
            wsl_ = W_sb[:, 128 * sl:128 * (sl + 1)]
            P.emit("tensor", lambda e, sc=sc, sl=sl: e.transpose(
                psf[5][0:1, 128 * sl:128 * sl + 128], lloc_tok[:, sc:sc + 1],
                ident_f[:]), waits=bfree(5), inc=sm["tensor"])
            tvL = tval()
            evac("vector", 5, lambda e, sl=sl: e.tensor_copy(
                lrow[0:1, 128 * sl:128 * (sl + 1)],
                psf[5][0:1, 128 * sl:128 * sl + 128]), waits=[tvL, vval()])
            vL = vval()
            yield
            bidD = sl
            mm(psf[bidD][:, 0:128], bidD, ones_row[:],
               lrow[0:1, 128 * sl:128 * (sl + 1)], True, True, waits=[vL])
            tv1 = tval()
            bidS = 2 + sl
            mm(psf[bidS][:, 0:128], bidS, phk_fm_h[:, qo + tc:qo + tc + 128],
               phq_fm_h[:, qo + tc:qo + tc + 128], True, True)
            tv2 = tval()
            yield
            P.emit("scalar", lambda e, sc=sc, dsl_=dsl_, bidD=bidD: e.activation(
                dsl_, psf[bidD][:, 0:128], AF.Abs, bias=lloc_tok[:, sc:sc + 1],
                scale=-1.0), waits=[tv1] + ([SL[(sl, "dfree")]]
                                            if (sl, "dfree") in SL else []),
                inc=sm["scalar"])
            bankfree[bidD] = sval()
            P.emit("scalar", lambda e, dsl_=dsl_: e.activation(
                dsl_, dsl_, AF.Exp, scale=-1.0), waits=[sval()], inc=sm["scalar"])
            svD = sval()
            yield
            qfsl = qf_tok[:, KH * sl:KH * (sl + 1)]
            qbsl = qb_tok[:, KH * sl:KH * (sl + 1)]
            P.emit("vector", lambda e, c=c, sc=sc, o=o, qfsl=qfsl: e.tensor_scalar(
                qfsl, phq_tok_h[:, o + KH * c:o + KH * (c + 1)],
                af_tok[:, sc:sc + 1], None, op0=ALU.mult),
                waits=[SL[(sl, "qtfree")]] if (sl, "qtfree") in SL else [],
                inc=sm["vector"])
            vq1 = vval()
            yield
            P.emit("tensor", lambda e, sl=sl, qfsl=qfsl: e.transpose(
                psb[0][0:64, 128 * sl:128 * sl + 128], qfsl, ident_b[:]),
                waits=[vq1] + bfree(6), inc=sm["tensor"])
            tq1 = tval()
            evac("vector", 6, lambda e, sl=sl: e.tensor_copy(
                qf_fm[:, 128 * sl:128 * (sl + 1)],
                psb[0][0:64, 128 * sl:128 * sl + 128]), waits=[tq1])
            P.emit("vector", lambda e, c=c, sc=sc, o=o, qbsl=qbsl: e.tensor_scalar(
                qbsl, phq_tok_h[:, o + KH * c:o + KH * (c + 1)],
                ab_tok[:, sc:sc + 1], None, op0=ALU.mult), inc=sm["vector"])
            vq2 = vval()
            yield
            evac("vector", bidS, lambda e, dsl_=dsl_, wsl_=wsl_, bidS=bidS:
                 e.tensor_tensor(wsl_, dsl_, psf[bidS][:, 0:128], op=ALU.mult),
                 waits=[svD, tv2])
            SL[(sl, "dfree")] = vval()
            P.emit("tensor", lambda e, sl=sl, qbsl=qbsl: e.transpose(
                psb[1][0:64, 128 * sl:128 * sl + 128], qbsl, ident_b[:]),
                waits=[vq2] + bfree(7), inc=sm["tensor"])
            tq2 = tval()
            SL[(sl, "qtfree")] = tq2
            evac("vector", 7, lambda e, sl=sl: e.tensor_copy(
                qb_fm[:, 128 * sl:128 * (sl + 1)],
                psb[1][0:64, 128 * sl:128 * sl + 128]), waits=[tq2])
            vq = vval()
            yield
            bidO = 2 + sl
            mm(psf[bidO][:, 0:KV], bidO, wsl_,
               v_tok_h[:, o5 + KV * c:o5 + KV * (c + 1)], True, False, waits=[vq])
            mm(psf[bidO][:, 0:KV], bidO, qf_fm[:, 128 * sl:128 * (sl + 1)],
               S_bf[:, KV * sl:KV * (sl + 1)], False, False)
            mm(psf[bidO][:, 0:KV], bidO, qb_fm[:, 128 * sl:128 * (sl + 1)],
               R_bf[:, sl * NCH * KV + KV * c:sl * NCH * KV + KV * (c + 1)],
               False, True)
            tvO = tval()
            ksl = kscl[:, KH * sl:KH * (sl + 1)]
            P.emit("vector", lambda e, c=c, sc=sc, o=o, ksl=ksl: e.tensor_scalar(
                ksl, phk_tok_h[:, o + KH * c:o + KH * (c + 1)],
                ab_tok[:, sc:sc + 1], None, op0=ALU.mult),
                waits=[SL[(sl, "kfree")]] if (sl, "kfree") in SL else [],
                inc=sm["vector"])
            vk = vval()
            yield
            bidU = 4 + sl
            mm(psf[bidU][0:64, 0:KV], bidU, ksl,
               v_tok_h[:, o5 + KV * c:o5 + KV * (c + 1)], True, True, waits=[vk])
            tvS = tval()
            SL[(sl, "kfree")] = tvS
            yield
            evac("vector", bidU, lambda e, sc=sc, sl=sl, bidU=bidU:
                 e.scalar_tensor_tensor(
                     S_sb[:, KV * sl:KV * (sl + 1)], S_sb[:, KV * sl:KV * (sl + 1)],
                     gam_bc[0:64, sc:sc + 1], psf[bidU][0:64, 0:KV],
                     op0=ALU.mult, op1=ALU.add), waits=[tvS, vval()])
            P.emit("vector", lambda e, sl=sl: e.tensor_copy(
                S_bf[:, KV * sl:KV * (sl + 1)], S_sb[:, KV * sl:KV * (sl + 1)]),
                waits=[vval()], inc=sm["vector"])
            P.emit("vector", lambda e, sl=sl, bidO=bidO: e.tensor_scalar_max(
                numden[:, 2 * sl:2 * sl + 1], psf[bidO][:, KH:KV], 1e-6),
                waits=[tvO, vval()], inc=sm["vector"])
            P.emit("vector", lambda e, sl=sl: e.reciprocal(
                numden[:, 2 * sl + 1:2 * sl + 2], numden[:, 2 * sl:2 * sl + 1]),
                waits=[vval()])
            yield
            atsl = att_tok[:, KH * sl:KH * (sl + 1)]
            evac("vector", bidO, lambda e, sl=sl, bidO=bidO, atsl=atsl: e.tensor_scalar(
                atsl, psf[bidO][:, 0:KH], numden[:, 2 * sl + 1:2 * sl + 2], None,
                op0=ALU.mult), waits=[vval()])
            va = vval()
            yield
            P.emit("tensor", lambda e, sl=sl, atsl=atsl: e.transpose(
                psb[0][0:64, 384 + 128 * sl:384 + 128 * sl + 128] if False else
                psb[1][0:64, 256 + 128 * sl:256 + 128 * sl + 128],
                atsl, ident_b[:]), waits=[va] + bfree(7), inc=sm["tensor"])
            tva = tval()
            evac("vector", 7, lambda e, tc=tc, sl=sl: e.tensor_copy(
                att_acc[:, sl * T + tc:sl * T + tc + 128],
                psb[1][0:64, 256 + 128 * sl:256 + 128 * sl + 128]),
                waits=[tva] + ([SL[(sl, "atfree")]] if (sl, "atfree") in SL else []))

        def zipper(g0, g1):
            a0, a1 = True, True
            while a0 or a1:
                if a0:
                    a0 = next(g0, "end") != "end"
                if a1:
                    a1 = next(g1, "end") != "end"

        for hp in range(HN // 2):
            fence(dma=False)
            ld_head(0, 2 * hp)
            ld_head(1, 2 * hp + 1)
            for c in range(NCH - 1, 0, -1):
                zipper(bwd_step(0, 2 * hp, c), bwd_step(1, 2 * hp + 1, c))
            P.wait("tensor", vval())
            P.wait("scalar", vval())
            for c in range(NCH):
                zipper(fwd_step(0, 2 * hp, c), fwd_step(1, 2 * hp + 1, c))
            vA = vval()
            for sl in (0, 1):
                h = 2 * hp + sl
                SL[(sl, "atfree")] = store(
                    s_attnfm[h * KH:(h + 1) * KH, :], att_acc[:, sl * T:(sl + 1) * T],
                    waits=[vA])
            for key in ("kfree", "qtfree", "dfree"):
                SL.pop((0, key), None)
                SL.pop((1, key), None)

    phase_a2()
    fence()
    if upto == "pa2":
        return _emit(nc, P, es, din, douts)

    P.scope("p_a3a")
    ow_bf = A2("ow_bf", (128, 8 * 1024), BF16)
    attn_t = A2("attn_t", (128, 2 * 8 * 128), BF16)
    xm_t = A2("xm_t", (128, 2 * 1024))
    z3_bf = A2("z3_bf", (128, 2 * 1024), BF16)
    xn0_t = A2("xn0_t", (128, 2 * 1024))
    stg_a3 = A2("stg_a3", (128, 2 * 1024), BF16)
    for kc in range(8):
        wld = load(f32stage[:, 0:1024], din["o_w"][128 * kc:128 * (kc + 1), :], q="scalar")
        P.emit("vector", lambda e, kc=kc: e.tensor_copy(
            ow_bf[:, 1024 * kc:1024 * (kc + 1)], f32stage[:, 0:1024]),
            waits=[wld], inc=sm["vector"])
        P.wait("scalar", vval())
    fence(dma=False)

    def phase_a3a():
        xm_st = [None, None]
        z3_st = [None, None]
        at_guard = [None, None]
        xn_g = [None, None]
        ztg3 = [None, None]
        for tt in range(T // 128):
            t0 = tt * 128
            sl2 = tt % 2
            ao = 1024 * sl2
            xo = 1024 * sl2
            wld = load(xn0_t[:, xo:xo + 1024], s_xn0[t0:t0 + 128, :],
                       waits=[xn_g[sl2]] if xn_g[sl2] else ())
            wat = load(attn_t[:, ao:ao + 1024].rearrange("p (kc t) -> p kc t", t=128),
                       s_attnfm[:, t0:t0 + 128].rearrange("(kc p) t -> p kc t", p=128),
                       waits=[at_guard[sl2]] if at_guard[sl2] else ())
            for kc in range(8):
                for nb2 in range(2):
                    mm(psf[nb2][:, 0:512], nb2,
                       attn_t[:, ao + 128 * kc:ao + 128 * (kc + 1)],
                       ow_bf[:, 1024 * kc + 512 * nb2:1024 * kc + 512 * (nb2 + 1)],
                       kc == 0, kc == 7,
                       waits=[wat] if (kc == 0 and nb2 == 0) else ())
            at_guard[sl2] = tval()
            tv = tval()
            for nb2 in range(2):
                sl = slice(xo + 512 * nb2, xo + 512 * (nb2 + 1))
                so = slice(512 * nb2, 512 * (nb2 + 1))
                evac("vector", nb2, lambda e, ps=psf[nb2], sl=sl, so=so: e.tensor_add(
                    xm_t[:, sl], ps[:, 0:512], ob_bc[:, so]),
                    waits=[tv, wld] + ([xm_st[sl2]] if (nb2 == 0 and xm_st[sl2])
                                       else []))
                P.emit("vector", lambda e, sl=sl: e.tensor_add(
                    xm_t[:, sl], xm_t[:, sl], xn0_t[:, sl]), waits=[vval()],
                    inc=sm["vector"])
            xn_g[sl2] = vval()
            vXM = vval()
            xm_st[sl2] = store(s_xmid[t0:t0 + 128, :], xm_t[:, xo:xo + 1024],
                               waits=[vXM])
            layer_norm(z3_bf[:, xo:xo + 1024], xm_t[:, xo:xo + 1024], 1024,
                       waits=[vXM] + ([ztg3[sl2]] if ztg3[sl2] else []))
            vZ3 = vval()
            for kc in range(8):
                bid = 6 + kc % 2
                P.emit("tensor", lambda e, kc=kc, xo=xo: e.transpose(
                    psb[kc % 2][:, 0:128], z3_bf[:, xo + 128 * kc:xo + 128 * (kc + 1)],
                    ident_b[:]),
                    waits=[vZ3] + bfree(bid), inc=sm["tensor"])
                tv = tval()
                evac("vector", bid, lambda e, kc=kc, xo=xo: e.tensor_copy(
                    stg_a3[:, xo + 128 * kc:xo + 128 * (kc + 1)],
                    psb[kc % 2][:, 0:128]),
                    waits=[tv] + ([z3_st[sl2]] if z3_st[sl2] and kc == 0 else []))
            ztg3[sl2] = tval()
            vv = vval()
            z3_st[sl2] = store(
                s_z3fm.rearrange("(kc p) t -> p kc t", p=128)[:, :, t0:t0 + 128],
                stg_a3[:, xo:xo + 1024].rearrange("p (kc j) -> p kc j", j=128),
                waits=[vv])

    phase_a3a()
    es_a2.close()
    fence()
    if upto == "pa3a":
        return _emit(nc, P, es, din, douts)

    P.scope("p_a3b")
    es_f = ExitStack()
    A3 = lambda n, s, d=F32: sbR(es_f, n, s, d)
    fw_sb = A3("fw_sb", (128, 8 * FFH), BF16)
    zsp_fm = A3("zsp_fm", (128, 8 * 2048), BF16)
    h1stg = [A3(f"h1stg{i}", (128, 2048), BF16) for i in range(2)]
    h1_st = [None, None]
    wF1l = [load(fw_sb[:, FFH * kc:FFH * (kc + 1)], s_f1bf[128 * kc:128 * (kc + 1), :],
                 q="scalar") for kc in range(8)]
    P.wait("tensor", *wF1l)

    def phase_a3b():
        SPB = 2048
        for spn in range(T // SPB):
            t0 = spn * SPB
            efence("tensor", "vector", "scalar", "sync")
            wz = [load(zsp_fm[:, SPB * kc:SPB * (kc + 1)],
                       s_z3fm[128 * kc:128 * (kc + 1), t0:t0 + SPB]) for kc in range(8)]
            for mc in range(FFH // 128):
                for kc in range(8):
                    for nt in range(4):
                        mm(psf[nt][:, 0:512], nt,
                           fw_sb[:, FFH * kc + 128 * mc:FFH * kc + 128 * (mc + 1)],
                           zsp_fm[:, SPB * kc + 512 * nt:SPB * kc + 512 * (nt + 1)],
                           kc == 0, kc == 7,
                           waits=wz if (kc == 0 and mc == 0 and nt == 0) else ())
                tv = tval()
                sl = mc % 2
                for nt in range(4):
                    evac("scalar", nt, lambda e, nt=nt, sl=sl, mc=mc: e.activation(
                        h1stg[sl][:, 512 * nt:512 * (nt + 1)], psf[nt][:, 0:512],
                        AF.Gelu, bias=f1bc[:, mc:mc + 1]),
                        waits=[tv] + ([h1_st[sl]] if (nt == 0 and h1_st[sl]) else []))
                sv = sval()
                h1_st[sl] = store(s_h1fm[128 * mc:128 * (mc + 1), t0:t0 + SPB],
                                  h1stg[sl][:, 0:SPB], waits=[sv])

    phase_a3b()
    fence()

    P.scope("p_a3c")
    f2_sb = fw_sb
    h1t = A3("h1t", (128, 32 * 128), BF16)
    xin_t = A3("xin_t", (128, 1024))
    xm2_t = A3("xm2_t", (128, 1024))
    for kc in range(32):
        wld = load(f32stage[:, 0:1024], din["f2_w"][128 * kc:128 * (kc + 1), :], q="scalar")
        P.emit("vector", lambda e, kc=kc: e.tensor_copy(
            f2_sb[:, 1024 * kc:1024 * (kc + 1)], f32stage[:, 0:1024]), waits=[wld],
            inc=sm["vector"])
        P.wait("scalar", vval())
    fence(dma=False)

    def phase_a3c():
        x1_st = None
        for tt in range(T // 128):
            t0 = tt * 128
            efence("tensor", "vector", "scalar", "sync")
            wh = load(h1t[:], s_h1fm.rearrange("(kc p) t -> p kc t", p=128)[:, :, t0:t0 + 128])
            wx = load(xin_t[:], x_in[t0:t0 + 128, :])
            wxm = load(xm2_t[:], s_xmid[t0:t0 + 128, :],
                       waits=[x1_st] if x1_st else ())
            for kc in range(32):
                for nb2 in range(2):
                    mm(psf[nb2][:, 0:512], nb2, h1t[:, 128 * kc:128 * (kc + 1)],
                       f2_sb[:, 1024 * kc + 512 * nb2:1024 * kc + 512 * (nb2 + 1)],
                       kc == 0, kc == 31,
                       waits=[wh] if (kc == 0 and nb2 == 0) else ())
            tv = tval()
            for nb2 in range(2):
                sl = slice(512 * nb2, 512 * (nb2 + 1))
                evac("vector", nb2, lambda e, ps=psf[nb2], sl=sl: e.tensor_add(
                    xm2_t[:, sl], xm2_t[:, sl], ps[:, 0:512]),
                    waits=[tv, wxm] + ([x1_st] if (nb2 == 0 and x1_st) else []))
                P.emit("vector", lambda e, sl=sl: e.tensor_add(
                    xm2_t[:, sl], xm2_t[:, sl], f2b_bc[:, sl]), waits=[vval()])
                P.emit("vector", lambda e, sl=sl: e.tensor_add(
                    xm2_t[:, sl], xm2_t[:, sl], xin_t[:, sl]), waits=[wx, vval()],
                    inc=sm["vector"])
            vX1 = vval()
            x1_st = store(s_x1[t0:t0 + 128, :], xm2_t[:], waits=[vX1])

    phase_a3c()
    es_f.close()
    es_lion.close()
    fence()
    if upto == "pa3":
        return _emit(nc, P, es, din, douts)

    def do_ffn(src_d, dst_d, tag):
        es1 = ExitStack()
        es_gu = ExitStack()
        FF = lambda n, s, d=F32: sbR(es1, n + tag, s, d)
        GU = lambda n, s, d=F32: sbR(es_gu, n + tag, s, d)
        xsp = FF("xsp", (128, 16 * 1024))
        zsp = FF("zsp", (128, 2 * 1024), BF16)
        xg = [None] * 16
        ztg = [None, None]
        gstmp = FF("gs", (128, 2048))
        h2stg = [FF(f"h2stg{i}", (128, 2048), BF16) for i in range(2)]
        wgu_sb = GU("wgu", (128, 2 * 2 * 1024), BF16)
        zfm = GU("zfm", (128, 2 * 8 * 2048), BF16)
        h2_st = [None, None]
        h2_prev_v = None
        wgu_ld = [None, None]
        wgu_tg = [None, None]

        def ld_wgu(mc):
            sl2 = mc % 2
            o = 2048 * sl2
            w1 = load(wgu_sb[:, o:o + 1024].rearrange("p (kc m) -> p kc m", m=128),
                      s_wgT.rearrange("(kc p) m -> p kc m", p=128)[
                          :, :, 128 * (mc % 24):128 * (mc % 24) + 128],
                      q="scalar", waits=[wgu_tg[sl2]] if wgu_tg[sl2] else ())
            w2 = load(wgu_sb[:, o + 1024:o + 2048].rearrange("p (kc m) -> p kc m", m=128),
                      s_wuT.rearrange("(kc p) m -> p kc m", p=128)[
                          :, :, 128 * (mc % 24):128 * (mc % 24) + 128],
                      q="scalar")
            wgu_ld[sl2] = [w1, w2]

        ld_wgu(0)
        SPB = 2048
        NT = SPB // 128
        xl = [None] * NT
        vZF_s = [None, None]

        def ffn_ln_span(spn):
            t0 = spn * SPB
            zo = (spn % 2) * 8 * SPB
            for tt in range(NT):
                xl[tt] = load(xsp[:, 1024 * tt:1024 * (tt + 1)],
                              src_d[t0 + 128 * tt:t0 + 128 * (tt + 1), :],
                              waits=[xg[tt]] if xg[tt] else ())
            norm = layer_norm_span(
                lambda i: xsp[:, 1024 * i:1024 * (i + 1)],
                NT, 1024, lambda i: [xl[i]])
            for tt in range(NT):
                zs = tt % 2
                lnv = norm(tt, zsp[:, 1024 * zs:1024 * (zs + 1)],
                           waits=[ztg[zs]] if ztg[zs] else ())
                xg[tt] = lnv
                for kc in range(8):
                    bid = 6 + kc % 2
                    P.emit("tensor", lambda e, kc=kc, zs=zs: e.transpose(
                        psb[kc % 2][:, 0:128],
                        zsp[:, 1024 * zs + 128 * kc:1024 * zs + 128 * (kc + 1)],
                        ident_b[:]), waits=[lnv] + bfree(bid), inc=sm["tensor"])
                    tv = tval()
                    evac("vector", bid, lambda e, kc=kc, tt=tt, zo=zo: e.tensor_copy(
                        zfm[:, zo + SPB * kc + 128 * tt:zo + SPB * kc + 128 * (tt + 1)],
                        psb[kc % 2][:, 0:128]), waits=[tv])
                ztg[zs] = tval()
            vZF_s[spn % 2] = vval()

        ffn_ln_span(0)
        for spn in range(T // SPB):
            t0 = spn * SPB
            zo = (spn % 2) * 8 * SPB
            vZF = vZF_s[spn % 2]
            if spn + 1 < T // SPB:
                ffn_ln_span(spn + 1)
            for mc in range(FFI // 128):
                sl2 = mc % 2
                o2 = 2048 * sl2
                wld2 = wgu_ld[sl2]
                if spn * 24 + mc + 1 < 48:
                    ld_wgu(mc + 1)
                for kc in range(8):
                    for nt in range(4):
                        mm(psf[nt][:, 0:512], nt,
                           wgu_sb[:, o2 + 128 * kc:o2 + 128 * (kc + 1)],
                           zfm[:, zo + SPB * kc + 512 * nt:zo + SPB * kc + 512 * (nt + 1)],
                           kc == 0, kc == 7,
                           waits=([vZF] + wld2) if (kc == 0 and nt == 0) else ())
                for nt in range(4):
                    evac("scalar", nt, lambda e, nt=nt: e.activation(
                        gstmp[:, 512 * nt:512 * (nt + 1)], psf[nt][:, 0:512],
                        AF.Silu, scale=QS),
                        waits=[tval()] + ([h2_prev_v] if h2_prev_v else []))
                svG = sval()
                for kc in range(8):
                    for nt in range(4):
                        mm(psf[nt][:, 0:512], nt,
                           wgu_sb[:, o2 + 1024 + 128 * kc:o2 + 1024 + 128 * (kc + 1)],
                           zfm[:, zo + SPB * kc + 512 * nt:zo + SPB * kc + 512 * (nt + 1)],
                           kc == 0, kc == 7)
                tv = tval()
                wgu_tg[sl2] = tv
                sl = mc % 2
                for nt in range(4):
                    evac("vector", nt, lambda e, nt=nt, sl=sl: e.tensor_tensor(
                        h2stg[sl][:, 512 * nt:512 * (nt + 1)],
                        gstmp[:, 512 * nt:512 * (nt + 1)], psf[nt][:, 0:512],
                        op=ALU.mult),
                        waits=[tv, svG] + ([h2_st[sl]] if (nt == 0 and h2_st[sl])
                                           else []))
                vv = vval()
                h2_prev_v = vv
                h2_st[sl] = store(s_h2fm[128 * mc:128 * (mc + 1), t0:t0 + SPB],
                                  h2stg[sl][:, 0:SPB], waits=[vv])
        efence("tensor", "vector", "scalar", "sync")
        es_gu.close()
        wd_sb = FF("wd", (128, 24 * 1024), BF16)
        h2t = FF("h2t", (128, 24 * 128), BF16)
        xdn = FF("xdn", (128, 1024))
        wd_l = [load(wd_sb[:, 1024 * kc:1024 * (kc + 1)],
                     s_wdT[128 * kc:128 * (kc + 1), :], q="scalar")
                for kc in range(24)]
        P.wait("tensor", *wd_l)
        s2c = QS * QS
        xo_st = None
        for tt in range(T // 128):
            t0 = tt * 128
            efence("tensor", "vector", "scalar", "sync")
            wh = load(h2t[:], s_h2fm.rearrange("(kc p) t -> p kc t", p=128)[:, :, t0:t0 + 128])
            wx = load(xdn[:], src_d[t0:t0 + 128, :],
                      waits=[xo_st] if xo_st else ())
            for kc in range(24):
                for nb2 in range(2):
                    mm(psf[4 + nb2][:, 0:512], 4 + nb2, h2t[:, 128 * kc:128 * (kc + 1)],
                       wd_sb[:, 1024 * kc + 512 * nb2:1024 * kc + 512 * (nb2 + 1)],
                       kc == 0, kc == 23, waits=[wh] if (kc == 0 and nb2 == 0) else ())
            tv = tval()
            for nb2 in range(2):
                sl = slice(512 * nb2, 512 * (nb2 + 1))
                evac("vector", 4 + nb2, lambda e, ps=psf[4 + nb2], sl=sl:
                     e.scalar_tensor_tensor(
                         xdn[:, sl], ps[:, 0:512], s2c, xdn[:, sl],
                         op0=ALU.mult, op1=ALU.add),
                    waits=[tv, wx] + ([xo_st] if (nb2 == 0 and xo_st) else []))
            vXO = vval()
            xo_st = store(dst_d[t0:t0 + 128, :], xdn[:], waits=[vXO])
        es1.close()
        fence()

    P.scope("p_f1")
    do_ffn(s_x1, s_x2, "f1")
    if upto == "pf1":
        return _emit(nc, P, es, din, douts)

    P.scope("p_b1")
    es_b1 = ExitStack()
    BB = lambda n, s, d=F32: sbR(es_b1, n, s, d)
    SPB1 = 512
    win_full = BB("win_full", (128, 8 * 4096), BF16)
    zb_fm = BB("zb_fm", (128, 2 * 8 * SPB1), BF16)
    xb_sp = BB("xb_sp", (128, 4 * 1024))
    zb_sp = BB("zb_sp", (128, 2 * 1024), BF16)
    inner_sb = BB("inner_sb", (128, 16 * SPB1))
    carry2 = BB("carry2", (128, 16 * 2))
    hconv = BB("hconv", (128, 2 * SPB1))
    hconv_bf = BB("hconv_bf", (128, 2 * SPB1), BF16)
    sg_stg = BB("sg_stg", (128, SPB1))
    win_l = [load(win_full[:, 4096 * kc:4096 * (kc + 1)],
                  s_winbf[128 * kc:128 * (kc + 1), :], q="scalar")
             for kc in range(8)]
    P.wait("tensor", *win_l)
    P.emit("vector", lambda e: e.memset(carry2[:], 0.0), inc=sm["vector"])

    def phase_b1():
        SPB = SPB1
        sg_done = None
        xg_b = [None] * 4
        ztg_b = [None, None]
        zfm_g = [None, None]
        vZF_b = [None, None]
        hc_g = [[], []]
        conv_g = [None]

        def b1_ln_span(spn):
            t0 = spn * SPB
            zo = (spn % 2) * 8 * SPB
            xl = [None] * 4
            for tt in range(4):
                xl[tt] = load(xb_sp[:, 1024 * tt:1024 * (tt + 1)],
                              s_x2[t0 + 128 * tt:t0 + 128 * (tt + 1), :],
                              waits=[xg_b[tt]] if xg_b[tt] else ())
            norm = layer_norm_span(
                lambda i: xb_sp[:, 1024 * i:1024 * (i + 1)],
                4, 1024, lambda i: [xl[i]])
            for tt in range(4):
                zs = tt % 2
                lnv = norm(tt, zb_sp[:, 1024 * zs:1024 * (zs + 1)],
                           waits=[ztg_b[zs]] if ztg_b[zs] else ())
                xg_b[tt] = lnv
                for kc in range(8):
                    bid = 6 + kc % 2
                    P.emit("tensor", lambda e, kc=kc, zs=zs: e.transpose(
                        psb[kc % 2][:, 0:128],
                        zb_sp[:, 1024 * zs + 128 * kc:1024 * zs + 128 * (kc + 1)],
                        ident_b[:]), waits=[lnv] + bfree(bid), inc=sm["tensor"])
                    tv = tval()
                    evac("vector", bid, lambda e, kc=kc, tt=tt, zo=zo: e.tensor_copy(
                        zb_fm[:, zo + SPB * kc + 128 * tt:zo + SPB * kc + 128 * (tt + 1)],
                        psb[kc % 2][:, 0:128]),
                        waits=[tv] + ([zfm_g[spn % 2]] if (tt == 0 and kc == 0
                                                           and zfm_g[spn % 2]) else []))
                ztg_b[zs] = tval()
            vZF_b[spn % 2] = vval()

        b1_ln_span(0)
        for spn in range(T // SPB):
            t0 = spn * SPB
            zo = (spn % 2) * 8 * SPB
            vZF = vZF_b[spn % 2]
            for mc in range(32):
                bid = mc % 4
                ps = psf[bid]
                for kc in range(8):
                    mm(ps[:, 0:512], bid,
                       win_full[:, 4096 * kc + 128 * mc:4096 * kc + 128 * (mc + 1)],
                       zb_fm[:, zo + SPB * kc:zo + SPB * (kc + 1)],
                       kc == 0, kc == 7, waits=[vZF] if kc == 0 else ())
                tv = tval()
                if mc < 16:
                    evac("vector", bid, lambda e, mc=mc, ps=ps: e.tensor_copy(
                        inner_sb[:, SPB * mc:SPB * (mc + 1)], ps[:, 0:512]),
                        waits=[tv] + ([conv_g[0]] if (mc == 0 and conv_g[0]) else []))
                else:
                    cc = mc - 16
                    evac("scalar", bid, lambda e, ps=ps: e.activation(
                        sg_stg[:, 0:SPB], ps[:, 0:512], AF.Silu),
                        waits=[tv] + ([sg_done] if sg_done else []))
                    sv = sval()
                    sg_done = store(s_sg[128 * cc:128 * (cc + 1), t0:t0 + SPB],
                                    sg_stg[:, 0:SPB], waits=[sv])
            zfm_g[spn % 2] = tval()
            vIN = vval()
            if spn + 1 < T // SPB:
                b1_ln_span(spn + 1)
            for cc in range(16):
                isl = inner_sb[:, SPB * cc:SPB * (cc + 1)]
                sl = cc % 2
                ho = SPB * sl
                wv = ([vIN] if cc < 2 else []) + hc_g[sl]
                P.emit("vector", lambda e, cc=cc, ho=ho: e.tensor_scalar(
                    hconv[:, ho:ho + 1], carry2[:, 2 * cc:2 * cc + 1], cw0c[:, cc:cc + 1],
                    None, op0=ALU.mult), waits=wv)
                P.emit("vector", lambda e, cc=cc, ho=ho: e.scalar_tensor_tensor(
                    hconv[:, ho:ho + 1], carry2[:, 2 * cc + 1:2 * cc + 2], cw1c[:, cc:cc + 1],
                    hconv[:, ho:ho + 1], op0=ALU.mult, op1=ALU.add), waits=[vval()])
                P.emit("vector", lambda e, cc=cc, isl=isl, ho=ho: e.scalar_tensor_tensor(
                    hconv[:, ho:ho + 1], isl[:, 0:1], cw2c[:, cc:cc + 1], hconv[:, ho:ho + 1],
                    op0=ALU.mult, op1=ALU.add), waits=[vval()])
                P.emit("vector", lambda e, cc=cc, ho=ho: e.tensor_scalar(
                    hconv[:, ho + 1:ho + 2], carry2[:, 2 * cc + 1:2 * cc + 2], cw0c[:, cc:cc + 1],
                    None, op0=ALU.mult), waits=[vval()])
                P.emit("vector", lambda e, cc=cc, isl=isl, ho=ho: e.scalar_tensor_tensor(
                    hconv[:, ho + 1:ho + 2], isl[:, 0:1], cw1c[:, cc:cc + 1], hconv[:, ho + 1:ho + 2],
                    op0=ALU.mult, op1=ALU.add), waits=[vval()])
                P.emit("vector", lambda e, cc=cc, isl=isl, ho=ho: e.scalar_tensor_tensor(
                    hconv[:, ho + 1:ho + 2], isl[:, 1:2], cw2c[:, cc:cc + 1], hconv[:, ho + 1:ho + 2],
                    op0=ALU.mult, op1=ALU.add), waits=[vval()])
                Wc = SPB - 2
                P.emit("vector", lambda e, cc=cc, isl=isl, Wc=Wc, ho=ho: e.tensor_scalar(
                    hconv[:, ho + 2:ho + 2 + Wc], isl[:, 0:Wc], cw0c[:, cc:cc + 1], None,
                    op0=ALU.mult), waits=[vval()])
                P.emit("vector", lambda e, cc=cc, isl=isl, Wc=Wc, ho=ho: e.scalar_tensor_tensor(
                    hconv[:, ho + 2:ho + 2 + Wc], isl[:, 1:1 + Wc], cw1c[:, cc:cc + 1],
                    hconv[:, ho + 2:ho + 2 + Wc], op0=ALU.mult, op1=ALU.add), waits=[vval()])
                P.emit("vector", lambda e, cc=cc, isl=isl, Wc=Wc, ho=ho: e.scalar_tensor_tensor(
                    hconv[:, ho + 2:ho + 2 + Wc], isl[:, 2:2 + Wc], cw2c[:, cc:cc + 1],
                    hconv[:, ho + 2:ho + 2 + Wc], op0=ALU.mult, op1=ALU.add), waits=[vval()])
                P.emit("vector", lambda e, cc=cc, isl=isl: e.tensor_copy(
                    carry2[:, 2 * cc:2 * cc + 2], isl[:, SPB - 2:SPB]), inc=sm["vector"])
                vC = vval()
                P.emit("scalar", lambda e, cc=cc, ho=ho: e.activation(
                    hconv[:, ho:ho + SPB], hconv[:, ho:ho + SPB], AF.Silu,
                    bias=cbc[:, cc:cc + 1]), waits=[vC], inc=sm["scalar"])
                svH = sval()
                P.emit("vector", lambda e, ho=ho: e.tensor_copy(
                    hconv_bf[:, ho:ho + SPB], hconv[:, ho:ho + SPB]),
                    waits=[svH], inc=sm["vector"])
                vHB = vval()
                lo = t0 - 1
                if spn == 0:
                    h1s = store(s_hfm[128 * cc:128 * (cc + 1), 0:SPB - 1],
                                hconv[:, ho + 1:ho + SPB], waits=[svH])
                    h2s = store(s_hfm_bf[128 * cc:128 * (cc + 1), 0:SPB - 1],
                                hconv_bf[:, ho + 1:ho + SPB], waits=[vHB])
                else:
                    h1s = store(s_hfm[128 * cc:128 * (cc + 1), lo:lo + SPB],
                                hconv[:, ho:ho + SPB], waits=[svH])
                    h2s = store(s_hfm_bf[128 * cc:128 * (cc + 1), lo:lo + SPB],
                                hconv_bf[:, ho:ho + SPB], waits=[vHB])
                hc_g[sl] = [vHB, h1s, h2s]
            conv_g[0] = vval()
        fence(dma=True)
        hguard = [0]
        fc_hst = []
        for cc in range(16):
            P.emit("vector", lambda e, cc=cc: e.tensor_scalar(
                hconv[:, 0:1], carry2[:, 2 * cc:2 * cc + 1], cw0c[:, cc:cc + 1], None,
                op0=ALU.mult), waits=[sval(), (sm["vector"], hguard[0])] + fc_hst)
            P.emit("vector", lambda e, cc=cc: e.scalar_tensor_tensor(
                hconv[:, 0:1], carry2[:, 2 * cc + 1:2 * cc + 2], cw1c[:, cc:cc + 1],
                hconv[:, 0:1], op0=ALU.mult, op1=ALU.add), waits=[vval()],
                inc=sm["vector"])
            vC = vval()
            P.emit("scalar", lambda e, cc=cc: e.activation(
                hconv[:, 0:1], hconv[:, 0:1], AF.Silu, bias=cbc[:, cc:cc + 1]),
                waits=[vC], inc=sm["scalar"])
            svH = sval()
            P.emit("vector", lambda e: e.tensor_copy(hconv_bf[:, 0:1], hconv[:, 0:1]),
                   waits=[svH], inc=sm["vector"])
            vHB = vval()
            hguard[0] = vHB[1]
            f1s = store(s_hfm[128 * cc:128 * (cc + 1), T - 1:T], hconv[:, 0:1],
                        waits=[svH], slow=True)
            f2s = store(s_hfm_bf[128 * cc:128 * (cc + 1), T - 1:T], hconv_bf[:, 0:1],
                        waits=[vHB], slow=True)
            fc_hst = [f1s, f2s]

    phase_b1()
    fence()
    es_b1.close()
    if upto == "pb1":
        return _emit(nc, P, es, din, douts)

    P.scope("p_b23")
    es_b2 = ExitStack()
    B2 = lambda n, s, d=F32: sbR(es_b2, n, s, d)
    wt_tile = B2("wt_tile", (128, 2048), BF16)
    wst_sb = B2("wst_sb", (128, 16 * 32), BF16)
    wout_sb = B2("wout_sb", (128, 16 * 512), BF16)
    hbf_sp = B2("hbf_sp", (128, 16 * 512), BF16)
    dt_sb = B2("dt_sb", (128, 8 * 512))
    bbc_sb = B2("bbc_sb", (128, 16 * 512))
    sp_sb = B2("sp_sb", (32, 512))
    bbrows = B2("bbrows", (1, 2 * 512))
    hf_sp = B2("hf_sp", (128, 2 * 512))
    sg_sp = B2("sg_sp", (128, 2 * 512))
    g_sb = B2("g_sb", (128, 2 * 512))
    y_sb = B2("y_sb", (128, 2 * 512))
    dA_sb = B2("dA_sb", (128, 2 * 1026))
    gB_sb = B2("gB_sb", (128, 2 * 1026))
    mem_sb = B2("mem_sb", (128, 2 * 1026))
    x2t = B2("x2t", (128, 512))
    phtw = B2("phtw", (128, 4 * 512))
    scarry = B2("scarry", (128, 256))
    yg_bf = B2("yg_bf", (128, 16 * 512), BF16)
    wws = load(f32stage[:, 0:512].rearrange("p (kc s) -> p kc s", s=32),
               din["W_state"].rearrange("(kc p) s -> p kc s", p=128))
    P.emit("vector", lambda e: e.tensor_copy(wst_sb[:], f32stage[:, 0:512]), waits=[wws],
           inc=sm["vector"])
    P.emit("vector", lambda e: e.memset(scarry[:], 0.0), inc=sm["vector"])
    for off in (512, 1025, 1026 + 512, 1026 + 1025):
        P.emit("vector", lambda e, off=off: e.memset(dA_sb[:, off:off + 1], 0.0),
               inc=sm["vector"])
    for off in (1025, 1026 + 1025):
        P.emit("vector", lambda e, off=off: e.memset(gB_sb[:, off:off + 1], 0.0),
               inc=sm["vector"])
    P.wait("tensor", vval())

    def phase_b23():
        x3_st = None
        vx2guard = [None]
        last_scan = [None, None]
        last_scan2 = [None, None]
        last_acc = [None, None]
        last_carry = [None, None]
        last_gB = [None, None]
        last_yg = [None, None]
        vG = [None, None]
        yinit = [None, None]
        hfree = [None, None]
        sgfree = [None, None]
        carry_ready = {}
        bb_t = [0, 0]
        wsg_c = [None, None]
        wt_guard = [None]
        dt_ready = [None] * 16
        yg_guard = [None]
        dtadd_g = [None, None]
        dt_free = {}
        hbf_guard = [None]
        for spn in range(T // 512):
            t0 = spn * 512
            whb = load(hbf_sp[:],
                       s_hfm_bf.rearrange("(kc p) t -> p kc t", p=128)[:, :, t0:t0 + 512],
                       waits=[hbf_guard[0]] if hbf_guard[0] else ())
            for kc in range(16):
                mm(psf[4][0:32, 0:512], 4, wst_sb[:, 32 * kc:32 * (kc + 1)],
                   hbf_sp[:, 512 * kc:512 * (kc + 1)], kc == 0, kc == 15,
                   waits=[whb] if kc == 0 else ())
            tv = tval()
            evac("vector", 4, lambda e: e.tensor_copy(sp_sb[:], psf[4][0:32, 0:512]),
                 waits=[tv])
            vSP = vval()
            bb_tv = [None] * 16
            for s_i in range(16):
                bsl2 = bbrows[0:1, 512 * (s_i % 2):512 * (s_i % 2 + 1)]
                wBR = load(bsl2, sp_sb[16 + s_i:17 + s_i, :],
                           waits=[vSP] + ([(sm["tensor"], bb_t[s_i % 2])]
                                          if bb_t[s_i % 2] else []))
                bid5 = 4 + (s_i % 2)
                mm(psf[bid5][:, 0:512], bid5, ones_row[:], bsl2, True, True,
                   waits=[wBR])
                bb_t[s_i % 2] = sm["tensor"].n
                bb_tv[s_i] = tval()
                if s_i >= 1:
                    j = s_i - 1
                    evac("vector", 4 + (j % 2), lambda e, j=j: e.tensor_copy(
                        bbc_sb[:, 512 * j:512 * (j + 1)],
                        psf[4 + (j % 2)][:, 0:512]), waits=[bb_tv[j]])
            evac("vector", 5, lambda e: e.tensor_copy(
                bbc_sb[:, 512 * 15:512 * 16], psf[5][:, 0:512]), waits=[bb_tv[15]])
            vBB = vval()

            wt_sv = [None] * 16
            wt_tv = [None] * 16

            def emit_wtmm(cc):
                wt_ld = load(wt_tile[:], s_wtt[:, 2048 * cc:2048 * (cc + 1)], q="scalar",
                             waits=[wt_guard[0]] if wt_guard[0] else ())
                bid = cc % 4
                ps = psf[bid]
                for kc in range(16):
                    mm(ps[:, 0:512], bid,
                       wt_tile[:, 128 * kc:128 * (kc + 1)],
                       hbf_sp[:, 512 * kc:512 * (kc + 1)], kc == 0, kc == 15,
                       waits=[whb, wt_ld] if kc == 0 else ())
                wt_guard[0] = tval()
                if cc == 15:
                    hbf_guard[0] = tval()
                wt_tv[cc] = tval()

            def emit_wtact(cc):
                bid = cc % 4
                ps = psf[bid]
                po = 1024 * (cc % 2)
                P.emit("scalar", lambda e, cc=cc, ps=ps, po=po: e.activation(
                    phtw[:, po:po + 512], ps[:, 0:512], AF.Abs,
                    bias=btimec[:, cc:cc + 1]),
                    waits=[wt_tv[cc]] + ([dtadd_g[cc % 2]] if dtadd_g[cc % 2] else []),
                    inc=sm["scalar"])
                P.emit("scalar", lambda e, po=po: e.activation(
                    phtw[:, po:po + 512], phtw[:, po:po + 512], AF.Exp, scale=-1.0),
                    waits=[sval()])
                P.emit("scalar", lambda e, po=po: e.activation(
                    phtw[:, po:po + 512], phtw[:, po:po + 512], AF.Ln, bias=1.0),
                    waits=[sval()])
                evac("scalar", bid, lambda e, cc=cc, ps=ps, po=po: e.activation(
                    phtw[:, po + 512:po + 1024], ps[:, 0:512], AF.Relu,
                    bias=btimec[:, cc:cc + 1]))
                wt_sv[cc] = sval()

            def emit_wtadd(cc):
                po = 1024 * (cc % 2)
                sl8 = cc % 8
                P.emit("vector", lambda e, sl8=sl8, po=po: e.tensor_add(
                    dt_sb[:, 512 * sl8:512 * (sl8 + 1)], phtw[:, po:po + 512],
                    phtw[:, po + 512:po + 1024]),
                    waits=[wt_sv[cc]] + (dt_free.get((cc + 8) % 16) or []),
                    inc=sm["vector"])
                dt_ready[cc] = vval()
                dtadd_g[cc % 2] = dt_ready[cc]

            for cc4 in range(4):
                emit_wtmm(cc4)
            emit_wtact(0)
            emit_wtact(1)
            emit_wtadd(0)
            emit_wtadd(1)
            emit_wtact(2)
            emit_wtact(3)
            for k in range(8):
                if 2 * k + 4 < 16:
                    emit_wtmm(2 * k + 4)
                if 2 * k + 5 < 16:
                    emit_wtmm(2 * k + 5)
                for ch in (0, 1):
                    cc = 2 * k + ch
                    hsl = hf_sp[:, 512 * ch:512 * (ch + 1)]
                    sgsl = sg_sp[:, 512 * ch:512 * (ch + 1)]
                    gsl = g_sb[:, 512 * ch:512 * (ch + 1)]
                    ysl = y_sb[:, 512 * ch:512 * (ch + 1)]
                    whf = load(hsl, s_hfm[128 * cc:128 * (cc + 1), t0:t0 + 512],
                               waits=hfree[ch] or ())
                    wsg_c[ch] = load(sgsl, s_sg[128 * cc:128 * (cc + 1), t0:t0 + 512],
                                     waits=[sgfree[ch]] if sgfree[ch] else ())
                    P.emit("vector", lambda e, sl8=cc % 8, hsl=hsl, gsl=gsl:
                           e.tensor_tensor(
                        gsl, dt_sb[:, 512 * sl8:512 * (sl8 + 1)], hsl, op=ALU.mult),
                        waits=[whf, dt_ready[cc]] + ([last_gB[ch]] if last_gB[ch] else []),
                        inc=sm["vector"])
                    vG[ch] = vval()
                    P.emit("scalar", lambda e, cc=cc, hsl=hsl, ysl=ysl: e.activation(
                        ysl, hsl, AF.Copy, scale=skipc[:, cc:cc + 1]),
                        waits=[whf] + ([last_yg[ch]] if last_yg[ch] else []),
                        inc=sm["scalar"])
                    yinit[ch] = sval()
                if 2 * k + 2 < 16:
                    emit_wtadd(2 * k + 2)
                if 2 * k + 3 < 16:
                    emit_wtadd(2 * k + 3)
                for s_i in range(0, 16, 2):
                    CH = (0, 1)
                    fi_c = [2 * k * 16 + s_i, (2 * k + 1) * 16 + s_i]
                    dsl_c = [dA_sb[:, 1026 * ch:1026 * (ch + 1)] for ch in CH]
                    bsl_c = [gB_sb[:, 1026 * ch:1026 * (ch + 1)] for ch in CH]
                    msl_c = [mem_sb[:, 1026 * ch:1026 * (ch + 1)] for ch in CH]
                    ysl_c = [y_sb[:, 512 * ch:512 * (ch + 1)] for ch in CH]
                    svA_c = [None, None]
                    vgB_c = [None, None]
                    for ch in CH:
                        cc = 2 * k + ch
                        dsl = dsl_c[ch]
                        w_s = [dt_ready[cc]] + ([last_scan[ch]] if last_scan[ch] else [])
                        P.emit("scalar", lambda e, cc=cc, s_i=s_i, dsl=dsl,
                               sl8=cc % 8: e.activation(
                            dsl[:, 0:512], dt_sb[:, 512 * sl8:512 * (sl8 + 1)], AF.Exp,
                            scale=A_sb[:, 16 * cc + s_i:16 * cc + s_i + 1]),
                            waits=w_s, inc=sm["scalar"])
                        P.emit("scalar", lambda e, cc=cc, s_i=s_i, dsl=dsl,
                               sl8=cc % 8: e.activation(
                            dsl[:, 513:1025], dt_sb[:, 512 * sl8:512 * (sl8 + 1)], AF.Exp,
                            scale=A_sb[:, 16 * cc + s_i + 1:16 * cc + s_i + 2]),
                            inc=sm["scalar"])
                        svA_c[ch] = sval()
                    for ch in CH:
                        fi = fi_c[ch]
                        bsl = bsl_c[ch]
                        P.emit("vector", lambda e, fi=fi, bsl=bsl: e.tensor_copy(
                            bsl[:, 512:513], scarry[:, fi + 1:fi + 2]),
                            waits=([carry_ready[fi + 1]] if fi + 1 in carry_ready
                                   else []) +
                                  ([last_scan[ch]] if last_scan[ch] else []),
                            inc=sm["vector"])
                    for ch in CH:
                        gsl = g_sb[:, 512 * ch:512 * (ch + 1)]
                        bsl = bsl_c[ch]
                        grep2 = gsl.rearrange("p (one t) -> p one t", one=1
                                              ).broadcast_to((128, 2, 512))
                        w_g = [vG[ch], vBB] + ([last_scan[ch]] if last_scan[ch] else [])
                        P.emit("vector", lambda e, s_i=s_i, bsl=bsl, grep2=grep2:
                               e.tensor_tensor(
                                   bsl[:, 0:1026].rearrange("p (r t) -> p r t",
                                                            t=513)[:, :, 0:512],
                                   grep2,
                                   bbc_sb[:, 512 * s_i:512 * (s_i + 2)].rearrange(
                                       "p (r t) -> p r t", t=512),
                                   op=ALU.mult),
                               waits=w_g, inc=sm["vector"])
                        vgB_c[ch] = vval()
                    for ch in CH:
                        fi = fi_c[ch]
                        w_sc = [svA_c[ch], vgB_c[ch]]
                        if last_acc[ch]:
                            w_sc.append(last_acc[ch])
                        if last_carry[ch]:
                            w_sc.append(last_carry[ch])
                        if fi in carry_ready:
                            w_sc.append(carry_ready[fi])
                        P.emit("vector", lambda e, fi=fi, dsl=dsl_c[ch], bsl=bsl_c[ch],
                               msl=msl_c[ch]: e.tensor_tensor_scan(
                                   msl, dsl, bsl, scarry[:, fi:fi + 1],
                                   op0=ALU.mult, op1=ALU.add),
                               waits=w_sc, inc=sm["vector"])
                        last_scan2[ch] = last_scan[ch]
                        last_scan[ch] = vval()
                    ya1 = [None, None]
                    for ch in CH:
                        fi = fi_c[ch]
                        msl = msl_c[ch]
                        P.emit("vector", lambda e, fi=fi, msl=msl: e.tensor_copy(
                            scarry[:, fi:fi + 2].rearrange("p (r one) -> p r one", one=1),
                            msl[:, 0:1026].rearrange("p (r t) -> p r t",
                                                     t=513)[:, :, 511:512]),
                            waits=[last_scan[ch]], inc=sm["vector"])
                        last_carry[ch] = vval()
                        carry_ready[fi] = last_carry[ch]
                        carry_ready[fi + 1] = last_carry[ch]
                        P.emit("vector", lambda e, msl=msl, ysl=ysl_c[ch]: e.tensor_tensor(
                            ysl, ysl, msl[:, 0:512], op=ALU.add),
                            waits=[last_scan[ch]], inc=sm["vector"])
                        ya1[ch] = vval()
                    for ch in CH:
                        P.emit("vector", lambda e, msl=msl_c[ch], ysl=ysl_c[ch]:
                               e.tensor_tensor(ysl, ysl, msl[:, 513:1025], op=ALU.add),
                               waits=[ya1[ch]], inc=sm["vector"])
                        last_acc[ch] = vval()
                for ch in (0, 1):
                    cc = 2 * k + ch
                    dt_free[cc] = [sval(), vG[ch]]
                    sgsl = sg_sp[:, 512 * ch:512 * (ch + 1)]
                    ysl = y_sb[:, 512 * ch:512 * (ch + 1)]
                    P.emit("vector", lambda e, cc=cc, sgsl=sgsl, ysl=ysl: e.tensor_tensor(
                        yg_bf[:, 512 * cc:512 * (cc + 1)], ysl, sgsl, op=ALU.mult),
                        waits=[last_acc[ch], wsg_c[ch], yg_guard[0]] if yg_guard[0] else
                              [last_acc[ch], wsg_c[ch]], inc=sm["vector"])
                    last_yg[ch] = vval()
                    hfree[ch] = [vG[ch], yinit[ch]]
                    sgfree[ch] = last_yg[ch]
                if 2 * k + 4 < 16:
                    emit_wtact(2 * k + 4)
                if 2 * k + 5 < 16:
                    emit_wtact(2 * k + 5)
            vYG = vval()
            for nb2 in range(2):
                sl = slice(512 * nb2, 512 * (nb2 + 1))
                wo_ld = [load(wout_sb[:, 512 * kc:512 * (kc + 1)],
                              s_woutbf[128 * kc:128 * (kc + 1),
                                       512 * nb2:512 * (nb2 + 1)],
                              q="scalar", waits=[tval()] if kc == 0 else ())
                         for kc in range(16)]
                for tt in range(4):
                    wx2 = load(x2t[:], s_x2[t0 + 128 * tt:t0 + 128 * (tt + 1), sl],
                               waits=[vx2guard[0]] if vx2guard[0] else ())
                    bid = tt % 2
                    ps = psf[bid]
                    for kc in range(16):
                        mm(ps[:, 0:512], bid,
                           yg_bf[:, 512 * kc + 128 * tt:512 * kc + 128 * (tt + 1)],
                           wout_sb[:, 512 * kc:512 * (kc + 1)],
                           kc == 0, kc == 15,
                           waits=[vYG] + wo_ld if (kc == 0 and tt == 0) else ())
                    tv = tval()
                    evac("vector", bid, lambda e, ps=ps: e.tensor_add(
                        x2t[:], ps[:, 0:512], x2t[:]),
                        waits=[tv, wx2] + ([x3_st] if x3_st else []))
                    vX3 = vval()
                    x3_st = store(s_x3[t0 + 128 * tt:t0 + 128 * (tt + 1), sl],
                                  x2t[:], waits=[vX3])
                    vx2guard[0] = x3_st
            yg_guard[0] = tval()

    phase_b23()
    es_b2.close()
    fence()
    if upto == "pb23":
        return _emit(nc, P, es, din, douts)

    P.scope("p_f2")
    do_ffn(s_x3, out_d, "f2")

    return _emit(nc, P, es, din, douts)


def _emit(nc, P, es, din, douts):
    P.wait("sync", *[(s, s.n) for nm, s in P.sems.items()
                     if nm.startswith("st_") and s.n > 0])
    block_cm = nc.Block()
    block = block_cm.__enter__()
    for eng in ENGINES:
        deco = getattr(block, eng)

        def mk(eng):
            def body(e):
                cur = [None]
                for fn in P.ops[eng]:
                    if isinstance(fn, tuple):
                        if cur[0] is not None:
                            cur[0].__exit__(None, None, None)
                        cur[0] = nc.named_scope(fn[1])
                        cur[0].__enter__()
                    else:
                        fn(e)
                if cur[0] is not None:
                    cur[0].__exit__(None, None, None)
            return body

        deco(mk(eng))
    block_cm.__exit__(None, None, None)
    es.close()
    return nc, list(din.keys()), list(douts.keys())


_CACHE = {}


def get_program(upto="all", dbg=()):
    key = (upto, tuple(sorted(dbg)))
    if key not in _CACHE:
        _CACHE[key] = build_core_program(upto, dbg)
    return _CACHE[key]


_REN = {
    "q_w": "L0_attn_a_q_w", "q_b": "L0_attn_a_q_b", "k_w": "L0_attn_a_k_w",
    "k_b": "L0_attn_a_k_b", "v_w": "L0_attn_a_v_w", "v_b": "L0_attn_a_v_b",
    "o_w": "L0_attn_a_o_w", "o_b": "L0_attn_a_o_b", "ln1_w": "L0_attn_a_ln1_w",
    "ln1_b": "L0_attn_a_ln1_b", "ln2_w": "L0_attn_a_ln2_w", "ln2_b": "L0_attn_a_ln2_b",
    "dec_w": "L0_attn_a_dec_w", "dec_b": "L0_attn_a_dec_b", "f1_w": "L0_attn_a_f1_w",
    "f1_b": "L0_attn_a_f1_b", "f2_w": "L0_attn_a_f2_w", "f2_b": "L0_attn_a_f2_b",
    "W_in": "L0_attn_b_W_in", "conv_w": "L0_attn_b_conv_w", "conv_b": "L0_attn_b_conv_b",
    "W_state": "L0_attn_b_W_state", "W_time": "L0_attn_b_W_time",
    "b_time": "L0_attn_b_b_time", "A_log": "L0_attn_b_A_log", "skip": "L0_attn_b_skip",
    "W_out": "L0_attn_b_W_out", "w_gate": "L0_ffn_w_gate", "w_up": "L0_ffn_w_up",
    "w_down": "L0_ffn_w_down",
}


def make_in_maps(inputs):
    consts = host_constants()
    x = np.ascontiguousarray(np.asarray(inputs["L0_attn_a_x"], np.float32))
    w_map = {}
    for k, src in _REN.items():
        w_map[k] = np.ascontiguousarray(np.asarray(inputs[src]))
    for k, v in consts.items():
        w_map[k] = v
    in_maps = []
    for b in range(8):
        m = dict(w_map)
        m["x"] = x[b]
        in_maps.append(m)
    return in_maps


def kernel(**inputs):
    nc, in_names, out_names = get_program()
    in_maps = make_in_maps(inputs)
    res = run_bass_kernel_spmd(nc, in_maps, core_ids=list(range(8)))
    return np.stack([np.asarray(r["out"], np.float32) for r in res.results], 0)
